# version 03
# baseline (speedup 1.0000x reference)
import numpy as np
from contextlib import ExitStack
import concourse.bass as bass
import concourse.mybir as mybir
from concourse.bass_utils import run_bass_kernel_spmd

F32 = mybir.dt.float32
BF16 = mybir.dt.bfloat16
AX = mybir.AxisListType.X
AF = mybir.ActivationFunctionType
ALU = mybir.AluOpType

D = 2048
L = 2048
HK, HV = 16, 32
DK = DV = 128
C = 128
NCH = L // C
IA = 5120
IF = 8192
NEGC = -1e30
MAGIC = float(3 * 2 ** 22)
LNQ = float(np.log(DK ** -0.5))

DEV = False


class Sem:
    def __init__(self, stk, nc, name):
        self.h = stk.enter_context(nc.semaphore(name))
        self.n = 0


class Em:
    ENGS = ("y", "t", "v", "s", "g")

    def __init__(self):
        self.q = {e: [] for e in self.ENGS}
        self.lastw = {e: {} for e in self.ENGS}

    def on(self, eng, fn):
        self.q[eng].append(fn)

    def wait(self, eng, sem, val=None):
        v = sem.n if val is None else val
        if v <= 0:
            return
        if self.lastw[eng].get(id(sem), -1) >= v:
            return
        self.lastw[eng][id(sem)] = v
        self.q[eng].append(lambda e, v=v, s=sem: e.wait_ge(s.h, v))

    def emit(self, nc):
        with nc.Block() as blk:
            for eng, deco in (("y", blk.sync), ("t", blk.tensor), ("v", blk.vector),
                              ("s", blk.scalar), ("g", blk.gpsimd)):
                fns = self.q[eng]

                def mk(fns):
                    def _(e):
                        for f in fns:
                            f(e)
                    return _
                deco(mk(fns))


def build_program(dev=False, stop_after=None):
    nc = bass.Bass("TRN2", target_bir_lowering=False, detect_race_conditions=False)

    def inp(name, shape, dt=F32):
        return nc.dram_tensor(name, shape, dt, kind="ExternalInput").ap()

    x_h = inp("x", [L, D])
    Wq_h = inp("Wq", [D, HK * DK])
    Wk_h = inp("Wk", [D, HK * DK])
    Wv_h = inp("Wv", [D, HV * DV])
    Wz_h = inp("Wz", [D, HV * DV])
    Wba_h = inp("Wba", [D, 64])
    convw_h = inp("convw", [IF, 4])
    Wout_h = inp("Wout", [HV * DV, D])
    Wg_h = inp("Wg", [D, IA])
    Wu_h = inp("Wu", [D, IA])
    Wd_h = inp("Wd", [IA, D])
    Fg_h = inp("Fg", [D, IF])
    Fu_h = inp("Fu", [D, IF])
    Fd_h = inp("Fd", [IF, D])
    ln1c_h = inp("ln1c", [128, 16])
    ln2c_h = inp("ln2c", [128, 16])
    dtrep_h = inp("dtrep", [128, HV])
    alogrep_h = inp("alogrep", [128, HV])
    gnwz_h = inp("gnwz", [128, 512])
    flnrep_h = inp("flnrep", [128, IF])
    identf_h = inp("identf", [128, 128])
    identb_h = inp("identb", [128, 128], BF16)
    m0T_h = inp("m0T", [128, 128])
    m1T_h = inp("m1T", [128, 128])
    m2T_h = inp("m2T", [128, 128])
    negup_h = inp("negup", [128, 128])
    epsc_h = inp("epsc", [128, 8])

    out_h = nc.dram_tensor("out", [L, D], F32, kind="ExternalOutput").ap()

    def scr(name, shape, dt):
        return nc.dram_tensor(name, shape, dt, kind="Internal").ap()

    xn_s = scr("xn_s", [L, D], F32)
    qh_s = scr("qh_s", [HK * DK, L], BF16)
    kh_s = scr("kh_s", [HK * DK, L], BF16)
    v_s = scr("v_s", [L, HV * DV], BF16)
    sz_s = scr("sz_s", [L, HV * DV], BF16)
    core_s = scr("core_s", [L, HV * DV], BF16)
    h_s = scr("h_s", [L, D], F32)
    ao_s = scr("ao_s", [L, D], F32)
    hl_s = scr("hl_s", [L, IF], F32)
    wdq_s = scr("wdq_s", [IF, D], BF16)

    dump_specs = []
    hat_dbg = None
    if dev:
        hat_dbg = nc.dram_tensor("d_hAT", [128, 16 * L], BF16, kind="ExternalOutput").ap()
        mix_dbg = nc.dram_tensor("d_mix0", [128, 3 + L], BF16, kind="ExternalOutput").ap()
        cvs_dbg = nc.dram_tensor("d_cvs0", [128, L], BF16, kind="ExternalOutput").ap()
        st_dbg = nc.dram_tensor("d_stats", [128, 64], F32, kind="ExternalOutput").ap()
        h2t_dbg = nc.dram_tensor("d_h2T", [128, 16 * 1024], BF16, kind="ExternalOutput").ap()
        mid_dbg = nc.dram_tensor("d_mid", [128, 40 * 1024], BF16, kind="ExternalOutput").ap()
    if dev:
        for nm, ap_, shape, dt in [
            ("d_xn", xn_s, [L, D], F32), ("d_qh", qh_s, [HK * DK, L], BF16),
            ("d_kh", kh_s, [HK * DK, L], BF16), ("d_v", v_s, [L, HV * DV], BF16),
            ("d_sz", sz_s, [L, HV * DV], BF16), ("d_core", core_s, [L, HV * DV], BF16),
            ("d_h", h_s, [L, D], F32), ("d_ao", ao_s, [L, D], F32),
            ("d_hl", hl_s, [L, IF], F32), ("d_wdq", wdq_s, [IF, D], BF16),
        ]:
            dump_specs.append((nc.dram_tensor(nm, shape, dt, kind="ExternalOutput").ap(), ap_))

    gstk = ExitStack()
    E = gstk.enter_context

    def emit_dumps(em_prev_bar):
        emd = Em()
        for eng in Em.ENGS:
            emd.wait(eng, bar, em_prev_bar)
        if dev:
            for dap, sap in dump_specs:
                sG.n += 16
                emd.on("g", lambda e, o=dap, i=sap, s=sG: e.dma_start(out=o, in_=i).then_inc(s.h, 16))
        emd.on("g", lambda e, v=sG.n, s=sG: e.wait_ge(s.h, v))
        emd.emit(nc)

    identf = E(nc.sbuf_tensor("sb_identf", [128, 128], F32))
    identb = E(nc.sbuf_tensor("sb_identb", [128, 128], BF16))
    m0T = E(nc.sbuf_tensor("sb_m0T", [128, 128], F32))
    m1T = E(nc.sbuf_tensor("sb_m1T", [128, 128], F32))
    m2T = E(nc.sbuf_tensor("sb_m2T", [128, 128], F32))
    negup = E(nc.sbuf_tensor("sb_negup", [128, 128], F32))
    epsc = E(nc.sbuf_tensor("sb_epsc", [128, 8], F32))
    ln1c = E(nc.sbuf_tensor("sb_ln1c", [128, 16], F32))
    ln2c = E(nc.sbuf_tensor("sb_ln2c", [128, 16], F32))
    dtrep = E(nc.sbuf_tensor("sb_dtrep", [128, HV], F32))
    alogrep = E(nc.sbuf_tensor("sb_alogrep", [128, HV], F32))
    gnwz = E(nc.sbuf_tensor("sb_gnwz", [128, 512], BF16))
    ones_col = E(nc.sbuf_tensor("ones_col", [128, 1], F32))
    ones_colb = E(nc.sbuf_tensor("ones_colb", [128, 1], BF16))
    ones_row = E(nc.sbuf_tensor("ones_row", [1, 128], F32))
    nones_row = E(nc.sbuf_tensor("nones_row", [1, 128], F32))
    expAl = E(nc.sbuf_tensor("expAl", [128, HV], F32))
    mstat = E(nc.sbuf_tensor("mstat", [128, 16], F32))
    qstat = E(nc.sbuf_tensor("qstat", [128, 16], F32))
    rstdS = E(nc.sbuf_tensor("rstdS", [128, 16], F32))
    nmrS = E(nc.sbuf_tensor("nmrS", [128, 16], F32))
    rrS = E(nc.sbuf_tensor("rrS", [128, 16], F32))
    partW = E(nc.sbuf_tensor("partW", [128, 192], F32, side="right"))
    pcol = E(nc.sbuf_tensor("pcol", [128, 4], F32, side="right"))
    esP = ExitStack()
    EP = esP.enter_context
    ba_all = EP(nc.sbuf_tensor("ba_all", [128, NCH * 64], F32, side="right"))
    beta_all = EP(nc.sbuf_tensor("beta_all", [128, NCH * HV], F32, side="right"))
    gam_all = EP(nc.sbuf_tensor("gam_all", [128, NCH * HV], F32, side="right"))
    nbg_all = EP(nc.sbuf_tensor("nbg_all", [128, NCH * HV], F32, side="right"))
    rows_all = EP(nc.sbuf_tensor("rows_all", [64, NCH * 128], F32, side="right"))
    glast = EP(nc.sbuf_tensor("glast", [1, NCH * HV], F32, side="right"))
    swg_c = E(nc.sbuf_tensor("swg_c", [128, 1], F32))
    swu_c = E(nc.sbuf_tensor("swu_c", [128, 1], F32))
    swd_c = E(nc.sbuf_tensor("swd_c", [128, 1], F32))
    rswg_c = E(nc.sbuf_tensor("rswg_c", [128, 1], F32))
    rswu_c = E(nc.sbuf_tensor("rswu_c", [128, 1], F32))
    rswd_c = E(nc.sbuf_tensor("rswd_c", [128, 1], F32))
    rsx = E(nc.sbuf_tensor("rsx", [128, 16], F32))
    sx_all = E(nc.sbuf_tensor("sx_all", [128, 16], F32))
    cs2a = E(nc.sbuf_tensor("cs2a", [128, 16], F32))
    cdna = E(nc.sbuf_tensor("cdna", [128, 16], F32))
    cc1a = E(nc.sbuf_tensor("cc1a", [128, 16], F32))
    cc2a = E(nc.sbuf_tensor("cc2a", [128, 16], F32))
    tA = E(nc.sbuf_tensor("tA", [128, 16], F32))
    tB = E(nc.sbuf_tensor("tB", [128, 16], F32))
    t1x1 = E(nc.sbuf_tensor("t1x1", [1, 4], F32))

    sY = Sem(gstk, nc, "sY")
    sS = Sem(gstk, nc, "sS")
    sG = Sem(gstk, nc, "sG")
    cT = Sem(gstk, nc, "cT")
    cV = Sem(gstk, nc, "cV")
    cS = Sem(gstk, nc, "cS")
    bar = Sem(gstk, nc, "bar")
    dx = [Sem(gstk, nc, f"dx{i}") for i in range(2)]
    gxs = [Sem(gstk, nc, f"gxs{i}") for i in range(2)]
    wst = [Sem(gstk, nc, f"wst{i}") for i in range(2)]
    qst = [Sem(gstk, nc, f"qst{i}") for i in range(2)]
    vst = [Sem(gstk, nc, f"vst{i}") for i in range(2)]
    zst = Sem(gstk, nc, "zst")
    szst = [Sem(gstk, nc, f"szst{i}") for i in range(2)]
    fl_st = [Sem(gstk, nc, f"fl{i}") for i in range(2)]
    fqs = [Sem(gstk, nc, f"fqs{i}") for i in range(2)]
    p2l = [Sem(gstk, nc, f"p2l{i}") for i in range(2)]
    cst = [Sem(gstk, nc, f"cst{i}") for i in range(2)]
    p3l = Sem(gstk, nc, "p3l")
    x3st = [Sem(gstk, nc, f"x3st{i}") for i in range(2)]
    h3st = [Sem(gstk, nc, f"h3st{i}") for i in range(2)]
    h4l = [Sem(gstk, nc, f"h4l{i}") for i in range(2)]
    w4l = Sem(gstk, nc, "w4l")
    hrl = [Sem(gstk, nc, f"hrl{i}") for i in range(2)]
    aos = [Sem(gstk, nc, f"aos{i}") for i in range(2)]
    x5l = [Sem(gstk, nc, f"x5l{i}") for i in range(2)]
    g5l = Sem(gstk, nc, "g5l")
    hls = Sem(gstk, nc, "hls")
    h6l = Sem(gstk, nc, "h6l")
    w6l = [Sem(gstk, nc, f"w6l{i}") for i in range(2)]
    a6l = Sem(gstk, nc, "a6l")
    o6s = [Sem(gstk, nc, f"o6s{i}") for i in range(2)]

    def dma(em, eng, sem, out, in_):
        sem.n += 16
        em.on(eng, lambda e, o=out, i=in_, s=sem: e.dma_start(out=o, in_=i).then_inc(s.h, 16))
        return sem.n

    def op(em, eng, sem, fn):
        sem.n += 1
        em.on(eng, lambda e, f=fn, s=sem: f(e).then_inc(s.h, 1))
        return sem.n

    def vop(em, fn):
        n = op(em, "v", cV, fn)
        em.wait("v", cV, n)
        return n

    def vrecip(em, out_ap, in_ap):
        em.wait("v", cV, cV.n)
        n = op(em, "v", cV, lambda e, o=out_ap, i=in_ap: e.reciprocal(o, i))
        em.wait("v", cV, n)
        return n

    def barrier(em):
        bar.n += 5
        for eng in Em.ENGS:
            em.on(eng, lambda e: e.sem_inc(bar.h, 1))
        return bar.n

    def phase_start(em, tgt):
        for eng in Em.ENGS:
            em.wait(eng, bar, tgt)

    def slab_load(em, eng, sem, slab, Wh, j0, w, nk=16, a0=0, ncols=None):
        ncols = nk * w if ncols is None else ncols
        tot = Wh.shape[0] // 128
        srcv = Wh.rearrange("(a p) n -> p a n", p=128)
        last = None
        nhalf = 2 if nk > 4 else 1
        for half in range(nhalf):
            b0, b1 = half * nk // nhalf, (half + 1) * nk // nhalf
            src = srcv[:, a0 + b0:a0 + b1, j0:j0 + w]
            dst = slab[:, :].rearrange("p (a n) -> p a n", n=w)[:, b0:b1, :]
            last = dma(em, eng, sem, dst, src)
        return last

    esI = ExitStack()
    gnwz32 = esI.enter_context(nc.sbuf_tensor("gnwz32", [128, 512], F32))
    em = Em()
    for dst, src in [(identf, identf_h), (identb, identb_h), (m0T, m0T_h), (m1T, m1T_h),
                     (m2T, m2T_h), (negup, negup_h), (epsc, epsc_h), (ln1c, ln1c_h), (ln2c, ln2c_h),
                     (dtrep, dtrep_h), (alogrep, alogrep_h), (gnwz32, gnwz_h)]:
        dma(em, "y", sY, dst[:], src)
    em.wait("v", sY)
    op(em, "v", cV, lambda e: e.memset(ones_col[:], 1.0))
    op(em, "v", cV, lambda e: e.memset(ones_colb[:], 1.0))
    op(em, "v", cV, lambda e: e.memset(ones_row[:], 1.0))
    op(em, "v", cV, lambda e: e.memset(nones_row[:], -1.0))
    op(em, "v", cV, lambda e: e.tensor_copy(gnwz[:], gnwz32[:]))
    em.wait("s", sY)
    op(em, "s", cS, lambda e: e.activation(expAl[:], alogrep[:], AF.Exp))
    tbar = barrier(em)
    em.emit(nc)
    esI.close()
    if stop_after == "init":
        emit_dumps(tbar)
        return nc

    esA = ExitStack()
    hAT = esA.enter_context(nc.sbuf_tensor("hAT", [128, 16 * L], BF16))
    es0 = ExitStack()
    E0 = es0.enter_context
    xt = [E0(nc.sbuf_tensor(f"xt{i}", [128, D], F32)) for i in range(2)]
    sqdump = E0(nc.sbuf_tensor("sqdump", [128, D], F32))
    xn_sb = [E0(nc.sbuf_tensor(f"xn_sb{i}", [128, D], F32)) for i in range(2)]
    hat_tok = [E0(nc.sbuf_tensor(f"hat_tok{i}", [128, D], BF16)) for i in range(2)]
    ps_tr = [E0(nc.psum_tensor(f"ps_tr{i}", [128, 128], F32)) for i in range(4)]

    em = Em()
    phase_start(em, tbar)
    for i in range(16):
        xd = dma(em, "y", dx[i % 2], xt[i % 2][:], x_h[i * 128:(i + 1) * 128, :])
        em.wait("v", dx[i % 2], xd)
        vred = op(em, "v", cV, lambda e, i=i: e.reduce_sum(mstat[:, i:i + 1], xt[i % 2][:], AX))
        em.wait("s", dx[i % 2], xd)
        sred = op(em, "s", cS, lambda e, i=i: e.activation(
            sqdump[:], xt[i % 2][:], AF.Square, accum_out=qstat[:, i:i + 1]))
        if i < 15:
            em.wait("y", cV, vred)
            em.wait("y", cS, sred)
    em.wait("v", cS, cS.n)
    em.wait("v", cV, cV.n)
    vop(em, lambda e: e.tensor_scalar(mstat[:], mstat[:], 1.0 / D, None, ALU.mult))
    vop(em, lambda e: e.tensor_scalar(qstat[:], qstat[:], 1.0 / D, None, ALU.mult))
    vop(em, lambda e: e.tensor_mul(tA[:], mstat[:], mstat[:]))
    vv = vop(em, lambda e: e.tensor_sub(qstat[:], qstat[:], tA[:]))
    em.wait("s", cV, vv)
    op(em, "s", cS, lambda e: e.activation(tB[:], qstat[:], AF.Ln, bias=epsc[:, 0:1]))
    ss1 = op(em, "s", cS, lambda e: e.activation(rstdS[:], tB[:], AF.Exp, scale=-0.5))
    em.wait("v", cS, ss1)
    vop(em, lambda e: e.tensor_scalar(tA[:], qstat[:], 1e-5, None, ALU.add))
    vrecip(em, tB[:], tA[:])
    vv2 = vop(em, lambda e: e.tensor_mul(tA[:], qstat[:], tB[:]))
    em.wait("s", cV, vv2)
    op(em, "s", cS, lambda e: e.activation(tB[:], tA[:], AF.Ln, bias=epsc[:, 1:2]))
    ss2 = op(em, "s", cS, lambda e: e.activation(rrS[:], tB[:], AF.Exp, scale=-0.5))
    tr_evac = []
    for i in range(16):
        xd = dma(em, "y", dx[i % 2], xt[i % 2][:], x_h[i * 128:(i + 1) * 128, :])
        em.wait("v", dx[i % 2], xd)
        em.wait("v", cS, ss2)
        em.wait("v", gxs[i % 2])
        sxn = op(em, "v", cV, lambda e, i=i: e.tensor_scalar(
            xn_sb[i % 2][:], xt[i % 2][:], mstat[:, i:i + 1], rstdS[:, i:i + 1],
            ALU.subtract, ALU.mult))
        em.wait("g", cV, sxn)
        gx = dma(em, "g", gxs[i % 2], xn_s[i * 128:(i + 1) * 128, :], xn_sb[i % 2][:])
        vht = op(em, "v", cV, lambda e, i=i: e.tensor_scalar(
            hat_tok[i % 2][:], xn_sb[i % 2][:], rrS[:, i:i + 1], None, ALU.mult))
        em.wait("t", cV, vht)
        for c in range(16):
            n = i * 16 + c
            if n >= 4:
                em.wait("t", cV, tr_evac[n - 4])
            op(em, "t", cT, lambda e, c=c, i=i, n=n: e.matmul(
                ps_tr[n % 4][:], hat_tok[i % 2][:, c * 128:(c + 1) * 128], identb[:],
                start=True, stop=True))
            em.wait("v", cT, cT.n)
            ev = op(em, "v", cV, lambda e, c=c, i=i, n=n: e.tensor_scalar(
                hAT[:, c * L + i * 128: c * L + i * 128 + 128], ps_tr[n % 4][:],
                ln1c[:, c:c + 1], None, ALU.mult))
            tr_evac.append(ev)
        if i < 15:
            em.wait("y", cV, cV.n)
    em.wait("g", gxs[0])
    em.wait("g", gxs[1])
    if dev:
        em.wait("g", cV, cV.n)
        em.wait("g", cS, cS.n)
        dma(em, "g", sG, hat_dbg, hAT[:])
        dma(em, "g", sG, st_dbg[:, 0:16], mstat[:])
        dma(em, "g", sG, st_dbg[:, 16:32], qstat[:])
        dma(em, "g", sG, st_dbg[:, 32:48], rstdS[:])
        dma(em, "g", sG, st_dbg[:, 48:64], rrS[:])
        em.wait("g", sG, sG.n)
    tbar = barrier(em)
    em.emit(nc)
    es0.close()
    if stop_after == "p0":
        emit_dumps(tbar)
        return nc

    es1 = ExitStack()
    E0 = es1.enter_context
    wsl32 = [E0(nc.sbuf_tensor(f"wsl32{i}", [128, 2048], F32)) for i in range(2)]
    wslb = [E0(nc.sbuf_tensor(f"wslb{i}", [128, 2048], BF16)) for i in range(2)]
    mix = E0(nc.sbuf_tensor("mix", [128, 3 + L], BF16))
    cacc = [E0(nc.sbuf_tensor(f"cacc{i}", [128, L], BF16)) for i in range(2)]
    convs = E0(nc.sbuf_tensor("convs", [128, L], BF16))
    sqb = E0(nc.sbuf_tensor("sqb", [128, L], BF16))
    cw4 = [E0(nc.sbuf_tensor(f"cw4{i}", [128, 4], F32)) for i in range(2)]
    lnrow = E0(nc.sbuf_tensor("lnrow", [1, 512], F32))
    rstrow = E0(nc.sbuf_tensor("rstrow", [1, L], F32))
    bcb = E0(nc.sbuf_tensor("bcb", [128, L], BF16))
    qk_sb = [E0(nc.sbuf_tensor(f"qk_sb{i}", [128, L], BF16)) for i in range(2)]
    vtr = [E0(nc.sbuf_tensor(f"vtr{i}", [128, 512], BF16)) for i in range(2)]
    ft32 = [E0(nc.sbuf_tensor(f"ft32_{i}", [128, 2048], F32)) for i in range(2)]
    fdump = E0(nc.sbuf_tensor("fdump", [128, 2048], F32))

    ps_mm = [E0(nc.psum_tensor(f"ps_mm{i}", [128, 512], F32)) for i in range(2)]
    ps_ss = [E0(nc.psum_tensor(f"ps_ss{i}", [1, 512], F32)) for i in range(2)]
    ps_bc = [E0(nc.psum_tensor(f"ps_bc{i}", [128, 512], F32)) for i in range(2)]

    em = Em()
    phase_start(em, tbar)
    cast_of = {}
    conv_of = {}
    mm_evac = {0: 0, 1: 0}
    bc_evac = {0: 0, 1: 0}
    vtr_dma = {0: 0, 1: 0}
    qk_dma = {0: 0, 1: 0}
    sred_of = {}

    def emit_wred(idx):
        if idx >= 192:
            return
        if idx < 64:
            Wh, a, sgi = Fg_h, idx // 4, idx % 4
        elif idx < 128:
            Wh, a, sgi = Fu_h, (idx - 64) // 4, idx % 4
        else:
            Wh, a, sgi = Fd_h, idx - 128, 0
        pp = idx % 2
        if idx >= 2:
            em.wait("y", cS, sred_of[idx - 2])
        if idx >= 128:
            fd_ = dma(em, "y", fl_st[pp], ft32[pp][:], Wh[a * 128:(a + 1) * 128, :])
        else:
            fd_ = dma(em, "y", fl_st[pp], ft32[pp][:],
                      Wh[a * 128:(a + 1) * 128, sgi * 2048:(sgi + 1) * 2048])
        em.wait("s", fl_st[pp], fd_)
        sred_of[idx] = op(em, "s", cS, lambda e, pp=pp, idx=idx: e.activation(
            fdump[:], ft32[pp][:], AF.Abs,
            accum_out=partW[:, idx:idx + 1]))

    WRED_ON = True
    for j in range(64):
        par = j % 2
        if WRED_ON:
            for m3 in range(3):
                emit_wred(j * 3 + m3)
        if j < 16:
            Wh, jj = Wq_h, j
        elif j < 32:
            Wh, jj = Wk_h, j - 16
        else:
            Wh, jj = Wv_h, j - 32
        if j >= 2:
            em.wait("y", cV, cast_of[j - 2])
            em.wait("y", cV, conv_of[j - 2])
        sd = slab_load(em, "y", wst[par], wsl32[par], Wh, jj * 128, 128)
        cwd = dma(em, "y", wst[par], cw4[par][:], convw_h[j * 128:(j + 1) * 128, :])
        em.wait("v", wst[par], cwd)
        cast = op(em, "v", cV, lambda e, par=par: e.tensor_copy(wslb[par][:], wsl32[par][:]))
        cast_of[j] = cast
        em.wait("t", cV, cast)
        for n in range(4):
            bk = n % 2
            em.wait("t", cV, mm_evac[bk])
            for a in range(16):
                t_ = op(em, "t", cT, lambda e, par=par, a=a, n=n, bk=bk: e.matmul(
                    ps_mm[bk][:], wslb[par][:, a * 128:(a + 1) * 128],
                    hAT[:, a * L + n * 512: a * L + n * 512 + 512],
                    start=(a == 0), stop=(a == 15)))
            em.wait("v", cT, t_)
            ev = op(em, "v", cV, lambda e, n=n, bk=bk: e.tensor_copy(
                mix[:, 3 + n * 512: 3 + n * 512 + 512], ps_mm[bk][:]))
            mm_evac[bk] = ev
        op(em, "v", cV, lambda e: e.memset(mix[:, 0:3], 0.0))
        em.wait("v", wst[par], cwd)
        em.wait("v", cS, cS.n)
        op(em, "v", cV, lambda e, par=par: e.tensor_scalar(
            cacc[0][:], mix[:, 0:L], cw4[par][:, 0:1], None, ALU.mult))
        for i_ in range(1, 4):
            vcv = op(em, "v", cV, lambda e, i_=i_, par=par: e.scalar_tensor_tensor(
                cacc[i_ % 2][:], mix[:, i_:i_ + L], cw4[par][:, i_:i_ + 1],
                cacc[(i_ + 1) % 2][:], ALU.mult, ALU.add))
        conv_of[j] = vcv
        em.wait("s", cV, vcv)
        ssil = op(em, "s", cS, lambda e: e.activation(convs[:], cacc[1][:], AF.Silu))
        if dev and j == 0:
            em.wait("g", cV, vcv)
            em.wait("g", cS, ssil)
            dma(em, "g", sG, mix_dbg, mix[:])
            dma(em, "g", sG, cvs_dbg, convs[:])
            em.wait("g", sG, sG.n)
        if j < 32:
            em.wait("v", cS, ssil)
            em.wait("v", qst[par], qk_dma[par])
            vsq = op(em, "v", cV, lambda e: e.tensor_mul(sqb[:], convs[:], convs[:]))
            em.wait("t", cV, vsq)
            for n in range(4):
                bk = n % 2
                tss = op(em, "t", cT, lambda e, n=n, bk=bk: e.matmul(
                    ps_ss[bk][:], ones_colb[:], sqb[:, n * 512:(n + 1) * 512],
                    start=True, stop=True))
                em.wait("s", cT, tss)
                op(em, "s", cS, lambda e, bk=bk: e.activation(lnrow[:], ps_ss[bk][:], AF.Ln, bias=epsc[0:1, 1:2]))
                srs = op(em, "s", cS, lambda e, n=n, j=j: e.activation(
                    rstrow[0:1, n * 512:(n + 1) * 512], lnrow[:], AF.Exp, scale=-0.5,
                    bias=(epsc[0:1, 5:6] if j < 16 else 0.0)))
                em.wait("t", cS, srs)
                em.wait("t", cV, bc_evac[bk])
                tbc = op(em, "t", cT, lambda e, n=n, bk=bk: e.matmul(
                    ps_bc[bk][:], ones_row[:], rstrow[0:1, n * 512:(n + 1) * 512],
                    start=True, stop=True))
                em.wait("v", cT, tbc)
                bc_evac[bk] = op(em, "v", cV, lambda e, n=n, bk=bk: e.tensor_copy(
                    bcb[:, n * 512:(n + 1) * 512], ps_bc[bk][:]))
            vsc = op(em, "v", cV, lambda e, par=par: e.tensor_mul(qk_sb[par][:], convs[:], bcb[:]))
            em.wait("g", cV, vsc)
            if j < 16:
                qk_dma[par] = dma(em, "g", qst[par], qh_s[j * 128:(j + 1) * 128, :], qk_sb[par][:])
            else:
                qk_dma[par] = dma(em, "g", qst[par], kh_s[(j - 16) * 128:(j - 15) * 128, :], qk_sb[par][:])
        else:
            jj32 = j - 32
            em.wait("t", cS, ssil)
            for ii0 in range(0, 16, 4):
                pp = (ii0 // 4) % 2
                em.wait("t", cV, bc_evac[pp])
                em.wait("v", vst[pp], vtr_dma[pp])
                for ii in range(ii0, ii0 + 4):
                    ttr = op(em, "t", cT, lambda e, ii=ii, pp=pp: e.matmul(
                        ps_bc[pp][:, (ii % 4) * 128:(ii % 4) * 128 + 128],
                        convs[:, ii * 128:(ii + 1) * 128], identb[:], start=True, stop=True))
                em.wait("v", cT, ttr)
                bc_evac[pp] = op(em, "v", cV, lambda e, pp=pp: e.tensor_copy(vtr[pp][:], ps_bc[pp][:]))
                em.wait("g", cV, bc_evac[pp])
                vtr_dma[pp] = dma(
                    em, "g", vst[pp],
                    v_s.rearrange("(i p) f -> p i f", p=128)[:, ii0:ii0 + 4, jj32 * 128:(jj32 + 1) * 128],
                    vtr[pp][:, :].rearrange("p (i f) -> p i f", f=128))
    tbar = barrier(em)
    em.emit(nc)
    es1.close()
    if stop_after == "p1q":
        emit_dumps(tbar)
        return nc

    esz = ExitStack()
    E0 = esz.enter_context
    zsl32 = E0(nc.sbuf_tensor("zsl32", [128, 16 * 512], F32))
    zslb = E0(nc.sbuf_tensor("zslb", [128, 16 * 512], BF16))
    szg_sb = [E0(nc.sbuf_tensor(f"szg_sb{i}", [128, 512], BF16)) for i in range(2)]
    ps_mm = [E0(nc.psum_tensor(f"ps_zmm{i}", [128, 512], F32)) for i in range(2)]

    em = Em()
    phase_start(em, tbar)
    zcast = 0
    mm_evac = {0: 0, 1: 0}
    sz_dma = {0: 0, 1: 0}
    sz_mul = {0: 0, 1: 0}
    z_ev = {0: 0, 1: 0}
    for zc in range(8):
        em.wait("y", cV, zcast)
        zd = slab_load(em, "y", zst, zsl32, Wz_h, zc * 512, 512)
        em.wait("v", zst, zd)
        zcast = op(em, "v", cV, lambda e: e.tensor_copy(zslb[:], zsl32[:]))
        em.wait("t", cV, zcast)
        for i in range(16):
            bk = i % 2
            em.wait("t", cV, mm_evac[bk])
            em.wait("t", cS, z_ev[bk])
            for a in range(16):
                tz = op(em, "t", cT, lambda e, a=a, i=i, bk=bk: e.matmul(
                    ps_mm[bk][:], hAT[:, a * L + i * 128: a * L + i * 128 + 128],
                    zslb[:, a * 512:(a + 1) * 512], start=(a == 0), stop=(a == 15)))
            em.wait("s", cT, tz)
            em.wait("s", cV, sz_mul[bk])
            em.wait("s", szst[bk], sz_dma[bk])
            z_ev[bk] = op(em, "s", cS, lambda e, bk=bk: e.activation(
                szg_sb[bk][:], ps_mm[bk][:], AF.Silu))
            em.wait("v", cS, z_ev[bk])
            sz_mul[bk] = op(em, "v", cV, lambda e, bk=bk: e.tensor_mul(
                szg_sb[bk][:], szg_sb[bk][:], gnwz[:]))
            em.wait("g", cV, sz_mul[bk])
            sz_dma[bk] = dma(em, "g", szst[bk], sz_s[i * 128:(i + 1) * 128, zc * 512:(zc + 1) * 512], szg_sb[bk][:])
    tbar = barrier(em)
    em.emit(nc)
    esz.close()
    if stop_after == "p1z":
        emit_dumps(tbar)
        return nc

    esb = ExitStack()
    E0 = esb.enter_context
    wba32 = E0(nc.sbuf_tensor("wba32", [128, 1024], F32))
    wbab = E0(nc.sbuf_tensor("wbab", [128, 1024], BF16))
    spt = [E0(nc.sbuf_tensor(f"spt{i}", [128, HV], F32)) for i in range(4)]
    tin = E0(nc.sbuf_tensor("tin", [128, 64], F32))
    ps_mm = [E0(nc.psum_tensor(f"ps_bmm{i}", [128, 512], F32)) for i in range(2)]
    ps_G = E0(nc.psum_tensor("ps_G", [128, 64], F32))
    ps_R = E0(nc.psum_tensor("ps_R", [64, 128], F32))

    em = Em()
    phase_start(em, tbar)
    bad = slab_load(em, "y", sY, wba32, Wba_h, 0, 64)
    em.wait("v", sY, bad)
    bcast = op(em, "v", cV, lambda e: e.tensor_copy(wbab[:], wba32[:]))
    em.wait("t", cV, bcast)
    for i in range(16):
        bk = i % 2
        for a in range(16):
            tb_ = op(em, "t", cT, lambda e, a=a, i=i, bk=bk: e.matmul(
                ps_mm[bk][:, 0:64], hAT[:, a * L + i * 128: a * L + i * 128 + 128],
                wbab[:, a * 64:(a + 1) * 64], start=(a == 0), stop=(a == 15)))
        em.wait("v", cT, tb_)
        op(em, "v", cV, lambda e, i=i, bk=bk: e.tensor_copy(
            ba_all[:, i * 64:(i + 1) * 64], ps_mm[bk][:, 0:64]))
        em.wait("t", cV, cV.n)
    em.wait("s", cV, cV.n)
    for c_ in range(NCH):
        sb1 = op(em, "s", cS, lambda e, c_=c_: e.activation(
            beta_all[:, c_ * HV:(c_ + 1) * HV], ba_all[:, c_ * 64: c_ * 64 + 32], AF.Sigmoid))
        em.wait("v", cS, sb1)
        v1 = vop(em, lambda e, c_=c_: e.tensor_add(
            spt[0][:], ba_all[:, c_ * 64 + 32: c_ * 64 + 64], dtrep[:]))
        em.wait("s", cV, v1)
        op(em, "s", cS, lambda e: e.activation(spt[1][:], spt[0][:], AF.Abs))
        op(em, "s", cS, lambda e: e.activation(spt[1][:], spt[1][:], AF.Exp, scale=-1.0))
        s2_ = op(em, "s", cS, lambda e: e.activation(spt[1][:], spt[1][:], AF.Ln, bias=1.0))
        em.wait("v", cS, s2_)
        vop(em, lambda e: e.tensor_scalar(spt[2][:], spt[0][:], 0.0, None, ALU.max))
        vop(em, lambda e: e.tensor_add(spt[2][:], spt[2][:], spt[1][:]))
        v2 = vop(em, lambda e: e.tensor_mul(spt[2][:], spt[2][:], expAl[:]))
        em.wait("t", cV, v2)
        tg = op(em, "t", cT, lambda e: e.matmul(ps_G[:, 0:HV], negup[:], spt[2][:], start=True, stop=True))
        em.wait("v", cT, tg)
        vg = vop(em, lambda e: e.tensor_copy(tin[:, 32:64], ps_G[:, 0:HV]))
        em.wait("s", cV, vg)
        sg_ = op(em, "s", cS, lambda e, c_=c_: e.activation(
            gam_all[:, c_ * HV:(c_ + 1) * HV], tin[:, 32:64], AF.Exp))
        em.wait("v", cS, sg_)
        vop(em, lambda e, c_=c_: e.tensor_mul(
            spt[3][:], gam_all[:, c_ * HV:(c_ + 1) * HV], beta_all[:, c_ * HV:(c_ + 1) * HV]))
        vop(em, lambda e, c_=c_: e.tensor_scalar(
            nbg_all[:, c_ * HV:(c_ + 1) * HV], spt[3][:], -1.0, None, ALU.mult))
        op(em, "s", cS, lambda e, c_=c_: e.activation(
            spt[1][:], ba_all[:, c_ * 64: c_ * 64 + 32], AF.Abs))
        op(em, "s", cS, lambda e: e.activation(spt[1][:], spt[1][:], AF.Exp, scale=-1.0))
        s3_ = op(em, "s", cS, lambda e: e.activation(spt[1][:], spt[1][:], AF.Ln, bias=1.0))
        em.wait("v", cS, s3_)
        vop(em, lambda e, c_=c_: e.tensor_scalar(
            spt[0][:], ba_all[:, c_ * 64: c_ * 64 + 32], -1.0, 0.0, ALU.mult, ALU.max))
        vop(em, lambda e: e.tensor_add(spt[0][:], spt[0][:], spt[1][:]))
        v3 = vop(em, lambda e: e.tensor_sub(tin[:, 0:32], tin[:, 32:64], spt[0][:]))
        em.wait("t", cV, v3)
        tr_ = op(em, "t", cT, lambda e: e.matmul(ps_R[:], tin[:], identf[:], start=True, stop=True))
        em.wait("v", cT, tr_)
        op(em, "v", cV, lambda e, c_=c_: e.tensor_copy(
            rows_all[0:32, c_ * 128:(c_ + 1) * 128], ps_R[0:32, :]))
        v4 = op(em, "v", cV, lambda e, c_=c_: e.tensor_scalar(
            rows_all[32:64, c_ * 128:(c_ + 1) * 128], ps_R[32:64, :], -1.0, None, ALU.mult))
        em.wait("t", cV, v4)
    em.wait("g", cV, cV.n)
    em.wait("g", cS, cS.n)
    dma(em, "g", sG, glast[:], gam_all[127:128, :])
    em.wait("g", sG, sG.n)
    tbar = barrier(em)
    em.emit(nc)
    esb.close()
    esA.close()
    if stop_after == "p1b":
        emit_dumps(tbar)
        return nc

    es15 = ExitStack()
    E15 = es15.enter_context
    ftp = [E15(nc.sbuf_tensor(f"ftp{i}", [128, 2048], F32)) for i in range(2)]
    fq1 = E15(nc.sbuf_tensor("fq1", [128, 2048], F32))
    fqb = [E15(nc.sbuf_tensor(f"fqb{i}", [128, 2048], BF16)) for i in range(2)]
    ps_w = E15(nc.psum_tensor("ps_w", [1, 4], F32))
    ps_wb = E15(nc.psum_tensor("ps_wb", [128, 4], F32))
    wred15 = {}

    def emit_wred_p15(em, idx):
        if idx < 64:
            Wh, a, sgi = Fg_h, idx // 4, idx % 4
        elif idx < 128:
            Wh, a, sgi = Fu_h, (idx - 64) // 4, idx % 4
        else:
            Wh, a, sgi = Fd_h, idx - 128, 0
        pp = idx % 2
        if idx >= 2:
            em.wait("y", cS, wred15[idx - 2])
        if idx >= 128:
            fd_ = dma(em, "y", fl_st[pp], ftp[pp][:], Wh[a * 128:(a + 1) * 128, :])
        else:
            fd_ = dma(em, "y", fl_st[pp], ftp[pp][:],
                      Wh[a * 128:(a + 1) * 128, sgi * 2048:(sgi + 1) * 2048])
        em.wait("s", fl_st[pp], fd_)
        wred15[idx] = op(em, "s", cS, lambda e, pp=pp, idx=idx: e.activation(
            fq1[:], ftp[pp][:], AF.Abs, accum_out=partW[:, idx:idx + 1]))

    em = Em()
    phase_start(em, tbar)
    if not WRED_ON:
        for idx in range(192):
            emit_wred_p15(em, idx)
    for wi, Wh in enumerate((Fg_h, Fu_h, Fd_h)):
        first = wi * 64
        vp = op(em, "v", cV, lambda e, first=first, wi=wi: e.tensor_reduce(
            pcol[:, wi:wi + 1], partW[:, first:first + 64], AX, ALU.add))
        em.wait("v", cV, vp)
        em.wait("t", cV, vp)
        tw = op(em, "t", cT, lambda e, wi=wi: e.matmul(
            ps_w[0:1, wi:wi + 1], ones_col[:], pcol[:, wi:wi + 1], start=True, stop=True))
        em.wait("v", cT, tw)
        nel = float(Wh.shape[0] * Wh.shape[1])
        vop(em, lambda e, wi=wi, nel=nel: e.tensor_scalar(
            t1x1[0:1, wi:wi + 1], ps_w[0:1, wi:wi + 1], 1.0 / nel, 1e-5, ALU.mult, ALU.max))
        em.wait("v", cV, cV.n)
        vq = op(em, "v", cV, lambda e, wi=wi: e.reciprocal(t1x1[0:1, 3:4], t1x1[0:1, wi:wi + 1]))
        em.wait("v", cV, vq)
        em.wait("t", cV, vq)
        tb1 = op(em, "t", cT, lambda e, wi=wi: e.matmul(
            ps_wb[:, 0:1], ones_row[:], t1x1[0:1, 3:4], start=True, stop=True))
        tb2 = op(em, "t", cT, lambda e, wi=wi: e.matmul(
            ps_wb[:, 1:2], ones_row[:], t1x1[0:1, wi:wi + 1], start=True, stop=True))
        em.wait("v", cT, tb2)
        sw_dst = (swg_c, swu_c, swd_c)[wi]
        rsw_dst = (rswg_c, rswu_c, rswd_c)[wi]
        op(em, "v", cV, lambda e, sw_dst=sw_dst: e.tensor_copy(sw_dst[:], ps_wb[:, 0:1]))
        vb2 = op(em, "v", cV, lambda e, rsw_dst=rsw_dst: e.tensor_copy(rsw_dst[:], ps_wb[:, 1:2]))
        em.wait("v", cV, vb2)
        em.wait("t", cV, vb2)
    fq_dma = {0: 0, 1: 0}
    fq_hist = {}
    for a in range(IF // 128):
        pp = a % 2
        em.wait("y", cV, fq_hist[a - 2] if a >= 2 else 0)
        fd_ = dma(em, "y", fl_st[pp], ftp[pp][:], Fd_h[a * 128:(a + 1) * 128, :])
        em.wait("v", fl_st[pp], fd_)
        em.wait("v", fqs[pp], fq_dma[pp])
        op(em, "v", cV, lambda e, pp=pp: e.tensor_scalar(
            fq1[:], ftp[pp][:], swd_c[:], MAGIC, ALU.mult, ALU.add))
        vq1 = op(em, "v", cV, lambda e, pp=pp: e.tensor_scalar(
            fq1[:], fq1[:], MAGIC, None, ALU.subtract))
        fq_hist[a] = vq1
        vq2 = op(em, "v", cV, lambda e, pp=pp: e.tensor_scalar(
            fqb[pp][:], fq1[:], -1.0, 1.0, ALU.max, ALU.min))
        em.wait("g", cV, vq2)
        fq_dma[pp] = dma(em, "g", fqs[pp], wdq_s[a * 128:(a + 1) * 128, :], fqb[pp][:])
    em.wait("g", fqs[0])
    em.wait("g", fqs[1])
    tbar = barrier(em)
    em.emit(nc)
    es15.close()
    if stop_after == "p15":
        emit_dumps(tbar)
        return nc

    es2 = ExitStack()
    E2 = es2.enter_context
    k_fm = [E2(nc.sbuf_tensor(f"k_fm{i}", [128, 2048], BF16)) for i in range(2)]
    q_fm = [E2(nc.sbuf_tensor(f"q_fm{i}", [128, 2048], BF16)) for i in range(2)]
    v_tok = [E2(nc.sbuf_tensor(f"v_tok{i}", [128, 4096], BF16)) for i in range(2)]
    rows_cur = [E2(nc.sbuf_tensor(f"rows_cur{i}", [1, 8192], F32)) for i in range(2)]
    k_tok = E2(nc.sbuf_tensor("k_tok", [128, 2048], BF16))
    vb_sb = E2(nc.sbuf_tensor("vb_sb", [128, 4096], BF16))
    S32 = [E2(nc.sbuf_tensor(f"S32_{i}", [128, 4096], F32)) for i in range(2)]
    S0b = E2(nc.sbuf_tensor("S0b", [128, 4096], BF16))
    gC_sb = E2(nc.sbuf_tensor("gC_sb", [128, 32], F32))
    wcol32 = E2(nc.sbuf_tensor("wcol32", [128, 8], F32))
    d1b = E2(nc.sbuf_tensor("d1b", [128, 512], BF16))
    d3b = E2(nc.sbuf_tensor("d3b", [128, 512], BF16))
    mtb = [E2(nc.sbuf_tensor(f"mtb{i}", [128, 512], BF16)) for i in range(2)]
    mttb = [E2(nc.sbuf_tensor(f"mttb{i}", [128, 512], BF16)) for i in range(2)]
    attnT = [E2(nc.sbuf_tensor(f"attnT{i}", [128, 512], BF16)) for i in range(2)]
    bt_bf = [E2(nc.sbuf_tensor(f"bt_bf{i}", [128, 512], BF16)) for i in range(2)]
    core_t1 = [E2(nc.sbuf_tensor(f"core_t1_{i}", [128, 512], F32)) for i in range(2)]
    pow_u = E2(nc.sbuf_tensor("pow_u", [128, 5 * 512], BF16))
    pow_t = E2(nc.sbuf_tensor("pow_t", [128, 6 * 512], BF16))
    tch = [E2(nc.sbuf_tensor(f"tch{i}", [128, 512], BF16)) for i in range(2)]
    U_bf = E2(nc.sbuf_tensor("U_bf", [128, 512], BF16))
    uw_bf = E2(nc.sbuf_tensor("uw_bf", [128, 512], BF16))
    core_bf = [E2(nc.sbuf_tensor(f"core_bf{i}", [128, 512], BF16)) for i in range(2)]

    P_kkq = E2(nc.psum_tensor("P_kkq", [128, 512], F32))
    P_d1 = E2(nc.psum_tensor("P_d1", [128, 512], F32))
    P_d3 = E2(nc.psum_tensor("P_d3", [128, 512], F32))
    P_bt = E2(nc.psum_tensor("P_bt", [128, 512], F32))
    P_o2 = E2(nc.psum_tensor("P_o2", [128, 512], F32))
    P_sqA = E2(nc.psum_tensor("P_sqA", [128, 512], F32))
    P_sqB = E2(nc.psum_tensor("P_sqB", [128, 512], F32))
    P_ap = E2(nc.psum_tensor("P_ap", [128, 512], F32))

    em = Em()
    phase_start(em, tbar)
    op(em, "v", cV, lambda e: e.memset(S32[1][:], 0.0))
    szero = op(em, "v", cV, lambda e: e.memset(S0b[:], 0.0))
    em.wait("t", cV, szero)

    def chunk_pre_dma(c_, par):
        em.wait("s", cT, cT.n)
        em.wait("s", cV, cV.n)
        for h in range(16):
            dma(em, "s", p2l[par], k_fm[par][:, h * 128:(h + 1) * 128],
                kh_s[h * 128:(h + 1) * 128, c_ * 128:(c_ + 1) * 128])
        for h in range(16):
            dma(em, "s", p2l[par], q_fm[par][:, h * 128:(h + 1) * 128],
                qh_s[h * 128:(h + 1) * 128, c_ * 128:(c_ + 1) * 128])
        dma(em, "s", p2l[par], v_tok[par][:], v_s[c_ * 128:(c_ + 1) * 128, :])
        dma(em, "s", p2l[par], rows_cur[par][0:1, :].rearrange("o (r t) -> o r t", t=128),
            rows_all[:, c_ * 128:(c_ + 1) * 128])
        return p2l[par].n

    R2 = {"core_add": 0, "s_stt": 0, "pow6_ev": 0, "so2": 0, "uwv": 0}
    FE = {}

    def emit_fe_T_p123(gi, cpar, prev_tt_v, prev_exps):
        kh0, hv0 = gi * 2, gi * 4
        em.wait("t", cV, prev_tt_v)
        for kk in range(2):
            op(em, "t", cT, lambda e, kk=kk, kh0=kh0, cpar=cpar: e.matmul(
                P_kkq[:, kk * 128:(kk + 1) * 128],
                k_fm[cpar][:, (kh0 + kk) * 128:(kh0 + kk + 1) * 128],
                k_fm[cpar][:, (kh0 + kk) * 128:(kh0 + kk + 1) * 128], start=True, stop=True))
        for kk in range(2):
            op(em, "t", cT, lambda e, kk=kk, kh0=kh0, cpar=cpar: e.matmul(
                P_kkq[:, 256 + kk * 128: 256 + (kk + 1) * 128],
                k_fm[cpar][:, (kh0 + kk) * 128:(kh0 + kk + 1) * 128],
                q_fm[cpar][:, (kh0 + kk) * 128:(kh0 + kk + 1) * 128], start=True, stop=True))
        FE[gi] = {"kkq": cT.n}
        em.wait("t", cS, prev_exps)
        for vh in range(4):
            hv = hv0 + vh
            R1 = rows_cur[cpar][0:1, hv * 128:(hv + 1) * 128]
            NG = rows_cur[cpar][0:1, (32 + hv) * 128:(33 + hv) * 128]
            op(em, "t", cT, lambda e, vh=vh, R1=R1: e.matmul(
                P_d1[:, vh * 128:(vh + 1) * 128], R1, ones_row[:], start=True, stop=False))
            op(em, "t", cT, lambda e, vh=vh, NG=NG: e.matmul(
                P_d1[:, vh * 128:(vh + 1) * 128], ones_row[:], NG, start=False, stop=False))
            op(em, "t", cT, lambda e, vh=vh: e.matmul(
                P_d1[:, vh * 128:(vh + 1) * 128], m0T[:], identf[:], start=False, stop=True))
        for vh in range(4):
            hv = hv0 + vh
            NG = rows_cur[cpar][0:1, (32 + hv) * 128:(33 + hv) * 128]
            op(em, "t", cT, lambda e, vh=vh, NG=NG: e.matmul(
                P_d3[:, vh * 128:(vh + 1) * 128], NG, ones_row[:], start=True, stop=False))
            op(em, "t", cT, lambda e, vh=vh, NG=NG: e.matmul(
                P_d3[:, vh * 128:(vh + 1) * 128], nones_row[:], NG, start=False, stop=False))
            op(em, "t", cT, lambda e, vh=vh: e.matmul(
                P_d3[:, vh * 128:(vh + 1) * 128], m2T[:], identf[:], start=False, stop=True))
        FE[gi]["dmm"] = cT.n

    def emit_fe_S(gi, par, prev_tt_v):
        em.wait("s", cT, FE[gi]["dmm"])
        em.wait("s", cV, prev_tt_v)
        em.wait("s", cV, R2["uwv"])
        op(em, "s", cS, lambda e: e.activation(d1b[:], P_d1[:], AF.Exp))
        op(em, "s", cS, lambda e: e.activation(d3b[:], P_d3[:], AF.Exp))
        wbase = par * 4
        FE[gi]["exps"] = op(em, "s", cS, lambda e, wbase=wbase: e.activation(
            wcol32[:, wbase:wbase + 4],
            P_d3[:, :].rearrange("p (v t) -> p v t", t=128)[:, :, 127:128], AF.Exp))

    def emit_fe_V_A(gi, par):
        em.wait("v", cT, FE[gi]["kkq"])
        em.wait("v", cS, FE[gi]["exps"])
        for vh in range(4):
            kk = vh // 2
            op(em, "v", cV, lambda e, vh=vh, kk=kk, par=par: e.tensor_mul(
                mtb[par][:, vh * 128:(vh + 1) * 128], P_kkq[:, kk * 128:(kk + 1) * 128],
                d1b[:, vh * 128:(vh + 1) * 128]))
        FE[gi]["mtb_tt"] = cV.n
        for vh in range(4):
            kk = vh // 2
            op(em, "v", cV, lambda e, vh=vh, kk=kk, par=par: e.tensor_mul(
                attnT[par][:, vh * 128:(vh + 1) * 128],
                P_kkq[:, 256 + kk * 128: 256 + (kk + 1) * 128],
                d3b[:, vh * 128:(vh + 1) * 128]))
        FE[gi]["tt_v"] = cV.n

    def emit_fe_T_p4a(gi, par, prev_bt_stt):
        em.wait("t", cV, FE[gi]["mtb_tt"])
        em.wait("t", cV, prev_bt_stt)
        for vh in range(4):
            op(em, "t", cT, lambda e, vh=vh, par=par: e.matmul(
                P_bt[:, vh * 128:(vh + 1) * 128], mtb[par][:, vh * 128:(vh + 1) * 128],
                identb[:], start=True, stop=True))
        FE[gi]["mtT"] = cT.n

    def emit_fe_V_B(gi, par):
        em.wait("v", cT, FE[gi]["mtT"])
        op(em, "v", cV, lambda e, par=par: e.tensor_copy(mttb[par][:], P_bt[:]))
        FE[gi]["mttb_ev"] = cV.n

    def emit_fe_T_p4b(gi, par, cpar, prev_so2, extra_o2_cv):
        hv0, kh0 = gi * 4, gi * 2
        em.wait("t", cV, FE[gi]["mttb_ev"])
        for vh in range(4):
            hv = hv0 + vh
            kk = vh // 2
            op(em, "t", cT, lambda e, vh=vh, hv=hv, kk=kk, kh0=kh0, cpar=cpar: e.matmul(
                P_bt[:, vh * 128:(vh + 1) * 128],
                k_fm[cpar][:, (kh0 + kk) * 128:(kh0 + kk + 1) * 128],
                S0b[:, hv * 128:(hv + 1) * 128], start=True, stop=True))
        FE[gi]["btm"] = cT.n
        em.wait("t", cS, prev_so2)
        em.wait("t", cV, extra_o2_cv)
        for vh in range(4):
            hv = hv0 + vh
            kk = vh // 2
            op(em, "t", cT, lambda e, vh=vh, hv=hv, kk=kk, kh0=kh0, cpar=cpar: e.matmul(
                P_o2[:, vh * 128:(vh + 1) * 128],
                q_fm[cpar][:, (kh0 + kk) * 128:(kh0 + kk + 1) * 128],
                S0b[:, hv * 128:(hv + 1) * 128], start=True, stop=True))
        FE[gi]["o2m"] = cT.n

    def emit_fe_S_so2(gi, par, c_, prev_core_add_cv):
        hv0 = gi * 4
        em.wait("s", cT, FE[gi]["o2m"])
        em.wait("s", cV, prev_core_add_cv)
        so2 = 0
        for vh in range(4):
            hv = hv0 + vh
            so2 = op(em, "s", cS, lambda e, vh=vh, hv=hv, par=par, c_=c_: e.activation(
                core_t1[par][:, vh * 128:(vh + 1) * 128], P_o2[:, vh * 128:(vh + 1) * 128],
                AF.Copy, scale=gam_all[:, c_ * 32 + hv: c_ * 32 + hv + 1]))
        FE[gi]["so2"] = so2

    def emit_fe_V_C(gi, par, c_):
        hv0 = gi * 4
        em.wait("v", cT, FE[gi]["btm"])
        for vh in range(4):
            hv = hv0 + vh
            op(em, "v", cV, lambda e, vh=vh, hv=hv, c_=c_, par=par: e.scalar_tensor_tensor(
                bt_bf[par][:, vh * 128:(vh + 1) * 128], P_bt[:, vh * 128:(vh + 1) * 128],
                nbg_all[:, c_ * 32 + hv: c_ * 32 + hv + 1], vb_sb[:, hv * 128:(hv + 1) * 128],
                ALU.mult, ALU.add))
        FE[gi]["bt_stt"] = cV.n

    core_dma = {0: 0, 1: 0}
    pre_done = {0: chunk_pre_dma(0, 0)}
    for c_ in range(NCH):
        cpar = c_ % 2
        ping, pong = S32[(c_ + 1) % 2], S32[c_ % 2]
        if c_ + 1 < NCH:
            pre_done[c_ + 1] = chunk_pre_dma(c_ + 1, 1 - cpar)
        em.wait("t", p2l[cpar], pre_done[c_])
        em.wait("v", p2l[cpar], pre_done[c_])
        em.wait("t", cS, R2["so2"])
        tgc = op(em, "t", cT, lambda e, c_=c_: e.matmul(
            P_o2[:, 0:32], ones_row[:], glast[0:1, c_ * 32:(c_ + 1) * 32], start=True, stop=True))
        em.wait("v", cT, tgc)
        vgc = op(em, "v", cV, lambda e: e.tensor_copy(gC_sb[:], P_o2[:, 0:32]))
        em.wait("t", cV, R2["s_stt"])
        for g4 in range(4):
            for hh in range(4):
                h = g4 * 4 + hh
                tkt = op(em, "t", cT, lambda e, h=h, hh=hh, cpar=cpar: e.matmul(
                    P_sqA[:, hh * 128:(hh + 1) * 128], k_fm[cpar][:, h * 128:(h + 1) * 128],
                    identb[:], start=True, stop=True))
            em.wait("v", cT, tkt)
            vkt = op(em, "v", cV, lambda e, g4=g4: e.tensor_copy(
                k_tok[:, g4 * 512:(g4 + 1) * 512], P_sqA[:]))
            em.wait("t", cV, vkt)
        em.wait("v", cT, cT.n)
        for hv in range(32):
            op(em, "v", cV, lambda e, hv=hv, c_=c_, cpar=cpar: e.tensor_scalar(
                vb_sb[:, hv * 128:(hv + 1) * 128], v_tok[cpar][:, hv * 128:(hv + 1) * 128],
                beta_all[:, c_ * 32 + hv: c_ * 32 + hv + 1], None, ALU.mult))
        if c_ > 0:
            for seg in range(8):
                op(em, "v", cV, lambda e, seg=seg, ping=ping: e.tensor_copy(
                    S0b[:, seg * 512:(seg + 1) * 512], ping[:, seg * 512:(seg + 1) * 512]))
        s0_ready = cV.n
        em.wait("t", cV, s0_ready)

        prev = FEprev if c_ > 0 else {"tt_v": 0, "exps": 0, "bt_stt": 0, "so2": 0}
        emit_fe_T_p123(0, cpar, prev["tt_v"], prev["exps"])
        emit_fe_S(0, 0, prev["tt_v"])
        emit_fe_V_A(0, 0)
        emit_fe_T_p4a(0, 0, prev["bt_stt"])
        emit_fe_V_B(0, 0)
        emit_fe_T_p4b(0, 0, cpar, prev["so2"], vgc)
        emit_fe_S_so2(0, 0, c_, R2["core_add"])
        emit_fe_V_C(0, 0, c_)

        for g in range(8):
            par = g % 2
            npar = 1 - par
            fe = FE.pop(g)
            em.wait("t", cV, fe["mttb_ev"])
            em.wait("t", cV, fe["bt_stt"])
            em.wait("t", cV, R2["core_add"])
            for vh in range(4):
                op(em, "t", cT, lambda e, vh=vh, par=par: e.matmul(
                    P_ap[:, vh * 128:(vh + 1) * 128], mttb[par][:, vh * 128:(vh + 1) * 128],
                    bt_bf[par][:, vh * 128:(vh + 1) * 128], start=True, stop=True))
            r0m = cT.n
            if g < 7:
                emit_fe_T_p123(g + 1, cpar, fe["tt_v"], fe["exps"])
            em.wait("v", cT, r0m)
            tev = op(em, "v", cV, lambda e, par=par: e.tensor_sub(tch[0][:], bt_bf[par][:], P_ap[:]))
            powev = {}
            for k in range(1, 7):
                if k == 1:
                    em.wait("t", cV, R2["s_stt"])
                    em.wait("t", cS, R2["pow6_ev"])
                    pu, pt = mtb[par], mttb[par]
                else:
                    em.wait("t", cS, powev[k - 1])
                    pu = pow_u[:, (k - 2) * 512:(k - 1) * 512]
                    pt = pow_t[:, (k - 2) * 512:(k - 1) * 512]
                for vh in range(4):
                    if k < 6:
                        op(em, "t", cT, lambda e, vh=vh, pu=pu, pt=pt: e.matmul(
                            P_sqA[:, vh * 128:(vh + 1) * 128], pt[:, vh * 128:(vh + 1) * 128],
                            pu[:, vh * 128:(vh + 1) * 128], start=True, stop=True))
                    op(em, "t", cT, lambda e, vh=vh, pu=pu, pt=pt: e.matmul(
                        P_sqB[:, vh * 128:(vh + 1) * 128], pu[:, vh * 128:(vh + 1) * 128],
                        pt[:, vh * 128:(vh + 1) * 128], start=True, stop=True))
                sqm = cT.n
                if k >= 2:
                    em.wait("t", cV, tev)
                    ptap = pow_t[:, (k - 2) * 512:(k - 1) * 512]
                    tin_, tout_ = tch[(k - 2) % 2], tch[(k - 1) % 2]
                    for vh in range(4):
                        op(em, "t", cT, lambda e, vh=vh, ptap=ptap, tin_=tin_: e.matmul(
                            P_ap[:, vh * 128:(vh + 1) * 128], ptap[:, vh * 128:(vh + 1) * 128],
                            tin_[:, vh * 128:(vh + 1) * 128], start=True, stop=True))
                    apm = cT.n
                    em.wait("v", cT, apm)
                    tev = op(em, "v", cV, lambda e, tin_=tin_, tout_=tout_: e.tensor_add(
                        tout_[:], tin_[:], P_ap[:]))
                em.wait("s", cT, sqm)
                if k < 6:
                    op(em, "s", cS, lambda e, k=k: e.activation(
                        pow_u[:, (k - 1) * 512: k * 512], P_sqA[:], AF.Copy))
                powev[k] = op(em, "s", cS, lambda e, k=k: e.activation(
                    pow_t[:, (k - 1) * 512: k * 512], P_sqB[:], AF.Copy))
                if g < 7:
                    if k == 1:
                        emit_fe_S(g + 1, npar, fe["tt_v"])
                        emit_fe_V_A(g + 1, npar)
                    elif k == 3:
                        emit_fe_T_p4a(g + 1, npar, fe["bt_stt"])
                        emit_fe_V_B(g + 1, npar)
                        emit_fe_T_p4b(g + 1, npar, cpar, fe["so2"], 0)
                        emit_fe_S_so2(g + 1, npar, c_, R2["core_add"])
                        emit_fe_V_C(g + 1, npar, c_)
            R2["pow6_ev"] = powev[6]
            em.wait("t", cS, powev[6])
            em.wait("t", cV, tev)
            ptap = pow_t[:, 5 * 512: 6 * 512]
            tin_ = tch[1]
            for vh in range(4):
                op(em, "t", cT, lambda e, vh=vh, ptap=ptap, tin_=tin_: e.matmul(
                    P_ap[:, vh * 128:(vh + 1) * 128], ptap[:, vh * 128:(vh + 1) * 128],
                    tin_[:, vh * 128:(vh + 1) * 128], start=True, stop=True))
            ap6 = cT.n
            em.wait("v", cT, ap6)
            uev = op(em, "v", cV, lambda e, tin_=tin_: e.tensor_add(U_bf[:], tin_[:], P_ap[:]))
            em.wait("v", cS, fe["exps"])
            for vh in range(4):
                op(em, "v", cV, lambda e, vh=vh, par=par: e.tensor_scalar(
                    uw_bf[:, vh * 128:(vh + 1) * 128], U_bf[:, vh * 128:(vh + 1) * 128],
                    wcol32[:, par * 4 + vh: par * 4 + vh + 1], None, ALU.mult))
            uwv = cV.n
            em.wait("t", cV, uev)
            for vh in range(4):
                op(em, "t", cT, lambda e, vh=vh, par=par: e.matmul(
                    P_ap[:, vh * 128:(vh + 1) * 128], attnT[par][:, vh * 128:(vh + 1) * 128],
                    U_bf[:, vh * 128:(vh + 1) * 128], start=True, stop=True))
            o1m = cT.n
            em.wait("t", cS, powev[5])
            em.wait("t", cV, uwv)
            for vh in range(4):
                kk = vh // 2
                op(em, "t", cT, lambda e, vh=vh, kk=kk, g=g: e.matmul(
                    P_sqA[:, vh * 128:(vh + 1) * 128],
                    k_tok[:, (g * 2 + kk) * 128:(g * 2 + kk + 1) * 128],
                    uw_bf[:, vh * 128:(vh + 1) * 128], start=True, stop=True))
            smm = cT.n
            gpar = g % 2
            em.wait("v", cT, o1m)
            em.wait("v", cS, fe["so2"])
            em.wait("v", cst[gpar], core_dma[gpar])
            cadd = op(em, "v", cV, lambda e, gpar=gpar, par=par: e.tensor_add(
                core_bf[gpar][:], core_t1[par][:], P_ap[:]))
            em.wait("g", cV, cadd)
            core_dma[gpar] = dma(em, "g", cst[gpar],
                                 core_s[c_ * 128:(c_ + 1) * 128, g * 512:(g + 1) * 512],
                                 core_bf[gpar][:])
            em.wait("v", cT, smm)
            em.wait("v", cV, vgc)
            sstt = 0
            for vh in range(4):
                hv = g * 4 + vh
                sstt = op(em, "v", cV, lambda e, vh=vh, hv=hv, ping=ping, pong=pong: e.scalar_tensor_tensor(
                    pong[:, hv * 128:(hv + 1) * 128], ping[:, hv * 128:(hv + 1) * 128],
                    gC_sb[:, hv:hv + 1], P_sqA[:, vh * 128:(vh + 1) * 128], ALU.mult, ALU.add))
            R2["core_add"] = cadd
            R2["s_stt"] = sstt
            R2["so2"] = fe["so2"]
            R2["uwv"] = uwv
            if g == 7:
                FEprev = fe
    em.wait("g", cst[0])
    em.wait("g", cst[1])
    tbar = barrier(em)
    em.emit(nc)
    es2.close()
    esP.close()
    if stop_after == "p2":
        emit_dumps(tbar)
        return nc

    es3 = ExitStack()
    E3 = es3.enter_context
    gated = E3(nc.sbuf_tensor("gated", [128, 32 * 2048], BF16))
    cor_t = E3(nc.sbuf_tensor("cor_t", [128, 4096], BF16))
    szr_t = E3(nc.sbuf_tensor("szr_t", [128, 4096], BF16))
    ssg3 = E3(nc.sbuf_tensor("ssg3", [128, 32], F32))
    rsd3 = E3(nc.sbuf_tensor("rsd3", [128, 32], F32))
    gt_t = E3(nc.sbuf_tensor("gt_t", [128, 4096], BF16))
    wo32 = [E3(nc.sbuf_tensor("wo32_0", [128, 32 * 128], F32))] * 2
    wob = [E3(nc.sbuf_tensor(f"wob{i}", [128, 32 * 128], BF16)) for i in range(2)]
    xn3 = [E3(nc.sbuf_tensor(f"xn3_{i}", [128, 128], F32)) for i in range(2)]
    h3 = [E3(nc.sbuf_tensor(f"h3_{i}", [128, 128], F32)) for i in range(2)]
    ps3_tr = [E3(nc.psum_tensor(f"ps3_tr{i}", [128, 512], F32)) for i in range(2)]
    ps3_mm = [E3(nc.psum_tensor(f"ps3_mm{i}", [128, 128], F32)) for i in range(2)]

    em = Em()
    phase_start(em, tbar)
    for i in range(16):
        em.wait("s", cV, cV.n)
        ldc = dma(em, "s", p3l, cor_t[:], core_s[i * 128:(i + 1) * 128, :])
        ldz = dma(em, "s", p3l, szr_t[:], sz_s[i * 128:(i + 1) * 128, :])
        em.wait("v", p3l, ldz)
        vpre = op(em, "v", cV, lambda e: e.tensor_mul(gt_t[:], cor_t[:], cor_t[:]))
        em.wait("v", cV, vpre)
        vr3 = op(em, "v", cV, lambda e: e.tensor_reduce(
            ssg3[:], gt_t[:, :].rearrange("p (h d) -> p h d", d=128), AX, ALU.add))
        em.wait("v", cV, vr3)
        em.wait("s", cV, vr3)
        op(em, "s", cS, lambda e: e.activation(ssg3[:], ssg3[:], AF.Ln, bias=epsc[:, 1:2], scale=epsc[:, 2:3]))
        sr3 = op(em, "s", cS, lambda e: e.activation(rsd3[:], ssg3[:], AF.Exp, scale=-0.5))
        em.wait("v", cS, sr3)
        for hv in range(32):
            op(em, "v", cV, lambda e, hv=hv: e.tensor_scalar(
                gt_t[:, hv * 128:(hv + 1) * 128], cor_t[:, hv * 128:(hv + 1) * 128],
                rsd3[:, hv:hv + 1], None, ALU.mult))
        vg3 = op(em, "v", cV, lambda e: e.tensor_mul(gt_t[:], gt_t[:], szr_t[:]))
        em.wait("t", cV, vg3)
        tr3_evac = {0: 0, 1: 0}
        for d4 in range(8):
            pp = d4 % 2
            em.wait("t", cV, tr3_evac[pp])
            for dd in range(4):
                dvc = d4 * 4 + dd
                t3 = op(em, "t", cT, lambda e, dvc=dvc, dd=dd, pp=pp: e.matmul(
                    ps3_tr[pp][:, dd * 128:(dd + 1) * 128], gt_t[:, dvc * 128:(dvc + 1) * 128],
                    identb[:], start=True, stop=True))
            em.wait("v", cT, t3)
            tr3_evac[pp] = op(em, "v", cV, lambda e, d4=d4, i=i, pp=pp: e.tensor_copy(
                gated[:, :].rearrange("p (d t) -> p d t", t=2048)[:, d4 * 4:(d4 + 1) * 4, i * 128:(i + 1) * 128],
                ps3_tr[pp][:, :].rearrange("p (d t) -> p d t", t=128)))
    wcast3 = {}
    h_dma = {0: 0, 1: 0}
    mm3_ev = {0: 0, 1: 0}
    for jo in range(16):
        par = jo % 2
        if jo >= 1:
            em.wait("y", cV, wcast3[jo - 1])
        sd = slab_load(em, "y", sY, wo32[0], Wout_h, jo * 128, 128, nk=32)
        em.wait("v", sY, sd)
        if jo >= 2:
            em.wait("v", cT, cT.n)
        wcast3[jo] = op(em, "v", cV, lambda e, par=par: e.tensor_copy(wob[par][:], wo32[0][:]))
        em.wait("t", cV, wcast3[jo])
        for i in range(16):
            bk = i % 2
            em.wait("s", cV, mm3_ev[bk])
            xd = dma(em, "s", x3st[bk], xn3[bk][:], xn_s[i * 128:(i + 1) * 128, jo * 128:(jo + 1) * 128])
            em.wait("t", cV, mm3_ev[bk])
            for a in range(32):
                t4 = op(em, "t", cT, lambda e, a=a, i=i, bk=bk, par=par: e.matmul(
                    ps3_mm[bk][:], gated[:, a * 2048 + i * 128: a * 2048 + i * 128 + 128],
                    wob[par][:, a * 128:(a + 1) * 128], start=(a == 0), stop=(a == 31)))
            em.wait("v", cT, t4)
            em.wait("v", x3st[bk], xd)
            em.wait("v", h3st[bk], h_dma[bk])
            mm3_ev[bk] = op(em, "v", cV, lambda e, bk=bk: e.tensor_add(h3[bk][:], xn3[bk][:], ps3_mm[bk][:]))
            em.wait("g", cV, mm3_ev[bk])
            h_dma[bk] = dma(em, "g", h3st[bk], h_s[i * 128:(i + 1) * 128, jo * 128:(jo + 1) * 128], h3[bk][:])
    em.wait("g", h3st[0])
    em.wait("g", h3st[1])
    tbar = barrier(em)
    em.emit(nc)
    es3.close()
    if stop_after == "p3":
        emit_dumps(tbar)
        return nc

    es4 = ExitStack()
    E4 = es4.enter_context
    h4 = [E4(nc.sbuf_tensor(f"h4_{i}", [128, D], F32)) for i in range(2)]
    ss4 = E4(nc.sbuf_tensor("ss4", [128, 1], F32))
    rsd4 = E4(nc.sbuf_tensor("rsd4", [128, 1], F32))
    t4a = E4(nc.sbuf_tensor("t4a", [128, 1], F32))
    h2tok = E4(nc.sbuf_tensor("h2tok", [128, D], BF16))
    h2T = E4(nc.sbuf_tensor("h2T", [128, 16 * 1024], BF16))
    wg32 = E4(nc.sbuf_tensor("wg32_0", [128, 2048], F32))
    wu32s = E4(nc.sbuf_tensor("wu32_0", [128, 2048], F32))
    dump4 = E4(nc.sbuf_tensor("dump4", [128, 2048], F32))
    wgb = [E4(nc.sbuf_tensor(f"wgb{i}", [128, 2048], BF16)) for i in range(2)]
    wub = [E4(nc.sbuf_tensor(f"wub{i}", [128, 2048], BF16)) for i in range(2)]
    tmp4 = E4(nc.sbuf_tensor("tmp4", [128, 512], BF16))
    mid4 = E4(nc.sbuf_tensor("mid4", [128, 40 * 1024], BF16))
    wdb = [E4(nc.sbuf_tensor(f"wdb{i}", [128, 8 * 512], BF16)) for i in range(2)]
    hres4 = [E4(nc.sbuf_tensor(f"hres4_{i}", [128, 512], F32, side="right")) for i in range(2)]
    ao4 = [E4(nc.sbuf_tensor(f"ao4_{i}", [128, 512], F32, side="right")) for i in range(2)]

    em = Em()
    phase_start(em, tbar)
    ao_v7 = {0: 0, 1: 0}
    ao_g7 = {0: 0, 1: 0}
    mmgu = {}
    mmd = {}
    for blk in range(2):
        esp = ExitStack()
        ps4_tr = [esp.enter_context(nc.psum_tensor(f"ps4_tr{blk}{i}", [128, 512], F32)) for i in range(2)]
        ps4_g = [esp.enter_context(nc.psum_tensor(f"ps4_g{blk}{i}", [128, 512], F32)) for i in range(2)]
        ps4_u = [esp.enter_context(nc.psum_tensor(f"ps4_u{blk}{i}", [128, 512], F32)) for i in range(2)]
        tr4_ev = {0: 0, 1: 0}
        for i8 in range(8):
            i = blk * 8 + i8
            par = i8 % 2
            em.wait("s", cV, cV.n)
            em.wait("s", cS, cS.n)
            hd = dma(em, "s", h4l[par], h4[par][:], h_s[i * 128:(i + 1) * 128, :])
            em.wait("s", h4l[par], hd)
            sq4 = op(em, "s", cS, lambda e, par=par: e.activation(
                dump4[:], h4[par][:], AF.Square, accum_out=ss4[:]))
            op(em, "s", cS, lambda e: e.activation(t4a[:], ss4[:], AF.Ln, bias=epsc[:, 1:2], scale=epsc[:, 3:4]))
            sr4 = op(em, "s", cS, lambda e: e.activation(rsd4[:], t4a[:], AF.Exp, scale=-0.5))
            em.wait("v", cS, sr4)
            vht = op(em, "v", cV, lambda e, par=par: e.tensor_scalar(
                h2tok[:], h4[par][:], rsd4[:], None, ALU.mult))
            em.wait("t", cV, vht)
            for c4 in range(4):
                pp = c4 % 2
                em.wait("t", cV, tr4_ev[pp])
                for cc in range(4):
                    c = c4 * 4 + cc
                    t5 = op(em, "t", cT, lambda e, c=c, cc=cc, pp=pp: e.matmul(
                        ps4_tr[pp][:, cc * 128:(cc + 1) * 128], h2tok[:, c * 128:(c + 1) * 128],
                        identb[:], start=True, stop=True))
                em.wait("v", cT, t5)
                for cc in range(4):
                    c = c4 * 4 + cc
                    tr4_ev[pp] = op(em, "v", cV, lambda e, c=c, cc=cc, i8=i8, pp=pp: e.tensor_scalar(
                        h2T[:, c * 1024 + i8 * 128: c * 1024 + i8 * 128 + 128],
                        ps4_tr[pp][:, cc * 128:(cc + 1) * 128],
                        ln2c[:, c:c + 1], None, ALU.mult))
        guev = {0: 0, 1: 0}
        gu_dma = {}
        gu_cast = {}
        em.wait("y", cV, cV.n)
        gu_dma[0] = (slab_load(em, "y", w4l, wg32, Wg_h, 0, 128),
                     slab_load(em, "y", w4l, wu32s, Wu_h, 0, 128))
        for j in range(40):
            par = j % 2
            gj = blk * 40 + j
            em.wait("v", w4l, gu_dma[j][1])
            em.wait("v", cT, mmgu.get(gj - 2, 0))
            op(em, "v", cV, lambda e, par=par: e.tensor_copy(wgb[par][:], wg32[:]))
            gu_cast[j] = op(em, "v", cV, lambda e, par=par: e.tensor_copy(wub[par][:], wu32s[:]))
            if j + 1 < 40:
                em.wait("y", cV, gu_cast[j])
                gu_dma[j + 1] = (slab_load(em, "y", w4l, wg32, Wg_h, (j + 1) * 128, 128),
                                 slab_load(em, "y", w4l, wu32s, Wu_h, (j + 1) * 128, 128))
            em.wait("t", cV, gu_cast[j])
            for n in range(2):
                bk = n
                em.wait("t", cV, guev[bk])
                for a in range(16):
                    op(em, "t", cT, lambda e, a=a, n=n, bk=bk, par=par: e.matmul(
                        ps4_g[bk][:], wgb[par][:, a * 128:(a + 1) * 128],
                        h2T[:, a * 1024 + n * 512: a * 1024 + n * 512 + 512],
                        start=(a == 0), stop=(a == 15)))
                for a in range(16):
                    t6 = op(em, "t", cT, lambda e, a=a, n=n, bk=bk, par=par: e.matmul(
                        ps4_u[bk][:], wub[par][:, a * 128:(a + 1) * 128],
                        h2T[:, a * 1024 + n * 512: a * 1024 + n * 512 + 512],
                        start=(a == 0), stop=(a == 15)))
                em.wait("s", cT, t6)
                em.wait("s", cV, guev[bk])
                s6 = op(em, "s", cS, lambda e, bk=bk: e.activation(tmp4[:], ps4_g[bk][:], AF.Silu))
                em.wait("v", cS, s6)
                guev[bk] = op(em, "v", cV, lambda e, j=j, n=n, bk=bk: e.tensor_mul(
                    mid4[:, j * 1024 + n * 512: j * 1024 + n * 512 + 512], tmp4[:], ps4_u[bk][:]))
            mmgu[gj] = cT.n
        em.wait("t", cS, cS.n)
        if dev and blk == 0:
            em.wait("g", cV, cV.n)
            dma(em, "g", sG, h2t_dbg, h2T[:])
            dma(em, "g", sG, mid_dbg, mid4[:])
            em.wait("g", sG, sG.n)
        esp.close()
        espd = ExitStack()
        ps4_dn = [espd.enter_context(nc.psum_tensor(f"ps4_dn{blk}{i}", [128, 512], F32)) for i in range(8)]
        dn_dma = {}
        dn_cast = {}
        em.wait("y", cV, gu_cast[39])
        dn_dma[0] = (slab_load(em, "y", w4l, wg32, Wd_h, 0, 512, nk=4, a0=0),
                     slab_load(em, "y", w4l, wu32s, Wd_h, 0, 512, nk=4, a0=4))
        for nb in range(4):
            for ag in range(5):
                seq = nb * 5 + ag
                par = seq % 2
                dsq = blk * 20 + seq
                em.wait("v", w4l, dn_dma[seq][1])
                em.wait("v", cT, mmd.get(dsq - 2, 0))
                op(em, "v", cV, lambda e, par=par: e.tensor_copy(wdb[par][:, 0:2048], wg32[:]))
                dn_cast[seq] = op(em, "v", cV, lambda e, par=par: e.tensor_copy(wdb[par][:, 2048:4096], wu32s[:]))
                if seq + 1 < 20:
                    nb2, ag2 = divmod(seq + 1, 5)
                    em.wait("y", cV, dn_cast[seq])
                    dn_dma[seq + 1] = (
                        slab_load(em, "y", w4l, wg32, Wd_h, nb2 * 512, 512, nk=4, a0=ag2 * 8),
                        slab_load(em, "y", w4l, wu32s, Wd_h, nb2 * 512, 512, nk=4, a0=ag2 * 8 + 4))
                em.wait("t", cV, dn_cast[seq])
                for tb in range(8):
                    for aa in range(8):
                        a = ag * 8 + aa
                        op(em, "t", cT, lambda e, a=a, aa=aa, tb=tb, par=par: e.matmul(
                            ps4_dn[tb][:], mid4[:, a * 1024 + tb * 128: a * 1024 + tb * 128 + 128],
                            wdb[par][:, aa * 512:(aa + 1) * 512],
                            start=(a == 0), stop=(a == 39)))
                mmd[dsq] = cT.n
            em.wait("s", cV, ao_v7[0])
            hd_pre = {0: dma(em, "s", hrl[0], hres4[0][:],
                             h_s[(blk * 8) * 128:(blk * 8 + 1) * 128, nb * 512:(nb + 1) * 512])}
            for tb in range(8):
                i = blk * 8 + tb
                pq = tb % 2
                if tb + 1 < 8:
                    pq1 = (tb + 1) % 2
                    em.wait("s", cV, ao_v7[pq1])
                    hd_pre[tb + 1] = dma(em, "s", hrl[pq1], hres4[pq1][:],
                                         h_s[(i + 1) * 128:(i + 2) * 128, nb * 512:(nb + 1) * 512])
                em.wait("v", hrl[pq], hd_pre[tb])
                em.wait("v", cT, cT.n)
                em.wait("v", aos[pq], ao_g7[pq])
                ao_v7[pq] = op(em, "v", cV, lambda e, tb=tb, pq=pq: e.tensor_add(
                    ao4[pq][:], hres4[pq][:], ps4_dn[tb][:]))
                em.wait("g", cV, ao_v7[pq])
                ao_g7[pq] = dma(em, "g", aos[pq], ao_s[i * 128:(i + 1) * 128, nb * 512:(nb + 1) * 512], ao4[pq][:])
                em.wait("t", cV, ao_v7[pq])
        em.wait("t", cV, cV.n)
        espd.close()
    em.wait("g", aos[0])
    em.wait("g", aos[1])
    tbar = barrier(em)
    em.emit(nc)
    es4.close()
    if stop_after == "p4":
        emit_dumps(tbar)
        return nc

    es5 = ExitStack()
    E5 = es5.enter_context
    flnrep = E5(nc.sbuf_tensor("sb_flnrep", [128, IF], F32))
    xn5 = [E5(nc.sbuf_tensor(f"xn5_{i}", [128, D], F32)) for i in range(2)]
    mx5 = E5(nc.sbuf_tensor("mx5", [128, 1], F32))
    qxtok = E5(nc.sbuf_tensor("qxtok", [128, D], BF16))
    qxT = E5(nc.sbuf_tensor("qxT", [128, 16 * 2048], BF16))
    gsl32 = E5(nc.sbuf_tensor("gsl32", [128, 16 * 512], F32))
    gqb = E5(nc.sbuf_tensor("gqb", [128, 16 * 512], BF16))
    uqb = E5(nc.sbuf_tensor("uqb", [128, 16 * 512], BF16))
    hf5 = E5(nc.sbuf_tensor("hf5", [128, 512], F32))
    hl5 = E5(nc.sbuf_tensor("hl5", [128, 512], F32))
    sdump5 = E5(nc.sbuf_tensor("sdump5", [128, 512], F32))
    ssbig = E5(nc.sbuf_tensor("ssbig", [128, 256], F32))
    mxbig = E5(nc.sbuf_tensor("mxbig", [128, 256], F32))
    ssfin = E5(nc.sbuf_tensor("ssfin", [128, 16], F32))
    mxfin = E5(nc.sbuf_tensor("mxfin", [128, 16], F32))
    t5a = E5(nc.sbuf_tensor("t5a", [128, 16], F32))
    rs2a = E5(nc.sbuf_tensor("rs2a", [128, 16], F32))

    em = Em()
    phase_start(em, tbar)
    flnd = dma(em, "y", sY, flnrep[:], flnrep_h)
    ps5_tr = [E5(nc.psum_tensor(f"ps5_tr{i}", [128, 512], F32)) for i in range(2)]
    ps5_g = [E5(nc.psum_tensor(f"ps5_g{i}", [128, 512], F32)) for i in range(2)]
    ps5_u = [E5(nc.psum_tensor(f"ps5_u{i}", [128, 512], F32)) for i in range(2)]
    tr5_ev = {0: 0, 1: 0}
    for i in range(16):
        par = i % 2
        em.wait("y", cV, cV.n)
        xd = dma(em, "y", x5l[par], xn5[par][:], xn_s[i * 128:(i + 1) * 128, :])
        em.wait("v", x5l[par], xd)
        vmx = op(em, "v", cV, lambda e, par=par: e.reduce_max(mx5[:], xn5[par][:], AX, apply_absolute_value=True))
        em.wait("v", cV, vmx)
        vop(em, lambda e, i=i: e.tensor_scalar(mx5[:], mx5[:], 1e-5, 1.0 / 127.0, ALU.max, ALU.mult))
        vop(em, lambda e, i=i: e.tensor_copy(rsx[:, i:i + 1], mx5[:]))
        em.wait("v", cV, cV.n)
        _vr = op(em, "v", cV, lambda e, i=i: e.reciprocal(sx_all[:, i:i + 1], mx5[:]))
        em.wait("v", cV, _vr)
        op(em, "v", cV, lambda e, par=par, i=i: e.tensor_scalar(
            xn5[par][:], xn5[par][:], sx_all[:, i:i + 1], MAGIC, ALU.mult, ALU.add))
        vq5 = op(em, "v", cV, lambda e, par=par: e.tensor_scalar(
            qxtok[:], xn5[par][:], MAGIC, None, ALU.subtract))
        em.wait("t", cV, vq5)
        for c4 in range(4):
            pp = c4 % 2
            em.wait("t", cV, tr5_ev[pp])
            for cc in range(4):
                c = c4 * 4 + cc
                t8 = op(em, "t", cT, lambda e, c=c, cc=cc, pp=pp: e.matmul(
                    ps5_tr[pp][:, cc * 128:(cc + 1) * 128], qxtok[:, c * 128:(c + 1) * 128],
                    identb[:], start=True, stop=True))
            em.wait("v", cT, t8)
            tr5_ev[pp] = op(em, "v", cV, lambda e, c4=c4, i=i, pp=pp: e.tensor_copy(
                qxT[:, :].rearrange("p (c t) -> p c t", t=2048)[:, c4 * 4:(c4 + 1) * 4, i * 128:(i + 1) * 128],
                ps5_tr[pp][:, :].rearrange("p (c t) -> p c t", t=128)))
    em.wait("v", cV, cV.n)
    vop(em, lambda e: e.tensor_scalar(cc1a[:], rsx[:], rswg_c[:], None, ALU.mult))
    vop(em, lambda e: e.tensor_scalar(cc2a[:], rsx[:], rswu_c[:], None, ALU.mult))
    guev5 = {0: 0, 1: 0}
    qdone = 0
    hl_dma = 0
    for jb in range(16):
        em.wait("y", cV, qdone)
        sdg = slab_load(em, "y", g5l, gsl32, Fg_h, jb * 512, 512)
        em.wait("v", g5l, sdg)
        em.wait("v", sY, flnd)
        op(em, "v", cV, lambda e: e.tensor_scalar(gsl32[:], gsl32[:], swg_c[:], MAGIC, ALU.mult, ALU.add))
        op(em, "v", cV, lambda e: e.tensor_scalar(gsl32[:], gsl32[:], MAGIC, None, ALU.subtract))
        gqd = op(em, "v", cV, lambda e: e.tensor_scalar(gqb[:], gsl32[:], -1.0, 1.0, ALU.max, ALU.min))
        em.wait("y", cV, gqd)
        sdu = slab_load(em, "y", g5l, gsl32, Fu_h, jb * 512, 512)
        em.wait("v", g5l, sdu)
        op(em, "v", cV, lambda e: e.tensor_scalar(gsl32[:], gsl32[:], swu_c[:], MAGIC, ALU.mult, ALU.add))
        op(em, "v", cV, lambda e: e.tensor_scalar(gsl32[:], gsl32[:], MAGIC, None, ALU.subtract))
        qdone = op(em, "v", cV, lambda e: e.tensor_scalar(uqb[:], gsl32[:], -1.0, 1.0, ALU.max, ALU.min))
        em.wait("t", cV, qdone)
        for tb in range(16):
            bk = tb % 2
            em.wait("t", cV, guev5[bk])
            for a in range(16):
                op(em, "t", cT, lambda e, a=a, tb=tb, bk=bk: e.matmul(
                    ps5_g[bk][:], qxT[:, a * 2048 + tb * 128: a * 2048 + tb * 128 + 128],
                    gqb[:, a * 512:(a + 1) * 512], start=(a == 0), stop=(a == 15)))
            for a in range(16):
                t9 = op(em, "t", cT, lambda e, a=a, tb=tb, bk=bk: e.matmul(
                    ps5_u[bk][:], qxT[:, a * 2048 + tb * 128: a * 2048 + tb * 128 + 128],
                    uqb[:, a * 512:(a + 1) * 512], start=(a == 0), stop=(a == 15)))
            em.wait("s", cT, t9)
            em.wait("s", cV, guev5[bk])
            s9 = op(em, "s", cS, lambda e, tb=tb, bk=bk: e.activation(
                sdump5[:], ps5_g[bk][:], AF.Silu, scale=cc1a[:, tb:tb + 1]))
            em.wait("v", cS, s9)
            v9 = op(em, "v", cV, lambda e, tb=tb, bk=bk: e.scalar_tensor_tensor(
                hf5[:], ps5_u[bk][:], cc2a[:, tb:tb + 1], sdump5[:], ALU.mult, ALU.mult))
            em.wait("v", cV, v9)
            em.wait("s", cV, v9)
            s10 = op(em, "s", cS, lambda e, jb=jb, tb=tb: e.activation(
                sdump5[:], hf5[:], AF.Square, accum_out=ssbig[:, tb * 16 + jb: tb * 16 + jb + 1]))
            v10 = op(em, "v", cV, lambda e, jb=jb: e.tensor_mul(hl5[:], hf5[:], flnrep[:, jb * 512:(jb + 1) * 512]))
            em.wait("v", cV, v10)
            vr = op(em, "v", cV, lambda e, jb=jb, tb=tb: e.reduce_max(
                mxbig[:, tb * 16 + jb: tb * 16 + jb + 1], hl5[:], AX, apply_absolute_value=True))
            em.wait("v", cV, vr)
            em.wait("g", cV, vr)
            em.wait("g", cS, s10)
            hl_dma = dma(em, "g", hls, hl_s[tb * 128:(tb + 1) * 128, jb * 512:(jb + 1) * 512], hl5[:])
            guev5[bk] = vr
            em.wait("v", hls, hl_dma)
    em.wait("v", cS, cS.n)
    em.wait("v", cV, cV.n)
    op(em, "v", cV, lambda e: e.tensor_reduce(
        ssfin[:], ssbig[:, :].rearrange("p (t j) -> p t j", j=16), AX, ALU.add))
    vsf = op(em, "v", cV, lambda e: e.tensor_reduce(
        mxfin[:], mxbig[:, :].rearrange("p (t j) -> p t j", j=16), AX, ALU.max))
    em.wait("v", cV, vsf)
    em.wait("s", cV, vsf)
    op(em, "s", cS, lambda e: e.activation(t5a[:], ssfin[:], AF.Ln, bias=epsc[:, 1:2], scale=epsc[:, 4:5]))
    srst = op(em, "s", cS, lambda e: e.activation(t5a[:], t5a[:], AF.Exp, scale=-0.5))
    em.wait("v", cS, srst)
    vop(em, lambda e: e.tensor_mul(mxfin[:], mxfin[:], t5a[:]))
    vop(em, lambda e: e.tensor_scalar(rs2a[:], mxfin[:], 1e-5, 1.0 / 127.0, ALU.max, ALU.mult))
    em.wait("v", cV, cV.n)
    _vr2 = op(em, "v", cV, lambda e: e.reciprocal(cs2a[:], rs2a[:]))
    em.wait("v", cV, _vr2)
    vop(em, lambda e: e.tensor_mul(cs2a[:], cs2a[:], t5a[:]))
    vcd = vop(em, lambda e: e.tensor_scalar(cdna[:], rs2a[:], rswd_c[:], None, ALU.mult))
    em.wait("t", cV, vcd)
    tbar = barrier(em)
    em.emit(nc)
    es5.close()
    if stop_after == "p5":
        emit_dumps(tbar)
        return nc

    es6 = ExitStack()
    E6 = es6.enter_context
    hlr = E6(nc.sbuf_tensor("hlr", [128, IF], F32))
    q2tok = E6(nc.sbuf_tensor("q2tok", [128, IF], BF16))
    hqT = E6(nc.sbuf_tensor("hqT", [128, 64 * 1024], BF16))
    wq5 = [E6(nc.sbuf_tensor(f"wq5_{i}", [128, 4 * 512], BF16)) for i in range(2)]
    aox = [E6(nc.sbuf_tensor(f"aox{i}", [128, 512], F32)) for i in range(2)]
    xf = [E6(nc.sbuf_tensor(f"xf{i}", [128, 512], F32)) for i in range(2)]
    out5b = [E6(nc.sbuf_tensor(f"out5b{i}", [128, 512], F32)) for i in range(2)]
    t5fb = [E6(nc.sbuf_tensor(f"t5fb{i}", [128, 512], F32)) for i in range(2)]

    em = Em()
    phase_start(em, tbar)
    o_vo = {0: 0, 1: 0}
    o_go = {0: 0, 1: 0}
    for blk in range(2):
        estr = ExitStack()
        ps6_tr = [estr.enter_context(nc.psum_tensor(f"ps6_tr{blk}{i}", [128, 512], F32)) for i in range(2)]
        tr6_ev = {0: 0, 1: 0}
        for i8 in range(8):
            i = blk * 8 + i8
            em.wait("y", cV, cV.n)
            hd = dma(em, "y", h6l, hlr[:], hl_s[i * 128:(i + 1) * 128, :])
            em.wait("v", h6l, hd)
            op(em, "v", cV, lambda e, i=i: e.tensor_scalar(
                hlr[:], hlr[:], cs2a[:, i:i + 1], MAGIC, ALU.mult, ALU.add))
            vq6 = op(em, "v", cV, lambda e: e.tensor_scalar(q2tok[:], hlr[:], MAGIC, None, ALU.subtract))
            em.wait("t", cV, vq6)
            for c4 in range(16):
                pp = c4 % 2
                em.wait("t", cV, tr6_ev[pp])
                for cc in range(4):
                    c = c4 * 4 + cc
                    ta = op(em, "t", cT, lambda e, c=c, cc=cc, pp=pp: e.matmul(
                        ps6_tr[pp][:, cc * 128:(cc + 1) * 128], q2tok[:, c * 128:(c + 1) * 128],
                        identb[:], start=True, stop=True))
                em.wait("v", cT, ta)
                tr6_ev[pp] = op(em, "v", cV, lambda e, c4=c4, i8=i8, pp=pp: e.tensor_copy(
                    hqT[:, :].rearrange("p (c t) -> p c t", t=1024)[:, c4 * 4:(c4 + 1) * 4, i8 * 128:(i8 + 1) * 128],
                    ps6_tr[pp][:, :].rearrange("p (c t) -> p c t", t=128)))
        em.wait("t", cV, cV.n)
        estr.close()
        espd = ExitStack()
        ps6_dn = [espd.enter_context(nc.psum_tensor(f"ps6_dn{blk}{i}", [128, 512], F32)) for i in range(8)]
        last6p = {0: 0, 1: 0}
        for nb in range(4):
            for ag in range(16):
                par = ag % 2
                em.wait("y", cT, last6p[par])
                srcv = wdq_s.rearrange("(a p) n -> p a n", p=128)
                sdd = dma(em, "y", w6l[par], wq5[par][:, :].rearrange("p (a n) -> p a n", n=512),
                          srcv[:, ag * 4:(ag + 1) * 4, nb * 512:(nb + 1) * 512])
                em.wait("t", w6l[par], sdd)
                for tb in range(8):
                    for aa in range(4):
                        a = ag * 4 + aa
                        tz = op(em, "t", cT, lambda e, a=a, aa=aa, tb=tb, par=par: e.matmul(
                            ps6_dn[tb][:], hqT[:, a * 1024 + tb * 128: a * 1024 + tb * 128 + 128],
                            wq5[par][:, aa * 512:(aa + 1) * 512],
                            start=(a == 0), stop=(a == 63)))
                last6p[par] = tz
            for tb in range(8):
                i = blk * 8 + tb
                pp = tb % 2
                em.wait("s", cV, o_vo[pp])
                ad = dma(em, "s", a6l, aox[pp][:], ao_s[i * 128:(i + 1) * 128, nb * 512:(nb + 1) * 512])
                xd = dma(em, "s", a6l, xf[pp][:], x_h[i * 128:(i + 1) * 128, nb * 512:(nb + 1) * 512])
                em.wait("s", cT, cT.n)
                sa = op(em, "s", cS, lambda e, tb=tb, i=i, pp=pp: e.activation(
                    t5fb[pp][:], ps6_dn[tb][:], AF.Copy, scale=cdna[:, i:i + 1]))
                em.wait("v", cS, sa)
                em.wait("v", a6l, xd)
                em.wait("v", o6s[pp], o_go[pp])
                va1 = op(em, "v", cV, lambda e, pp=pp: e.tensor_add(out5b[pp][:], t5fb[pp][:], aox[pp][:]))
                em.wait("v", cV, va1)
                o_vo[pp] = op(em, "v", cV, lambda e, pp=pp: e.tensor_add(out5b[pp][:], out5b[pp][:], xf[pp][:]))
                em.wait("g", cV, o_vo[pp])
                o_go[pp] = dma(em, "g", o6s[pp], out_h[i * 128:(i + 1) * 128, nb * 512:(nb + 1) * 512], out5b[pp][:])
                em.wait("t", cV, o_vo[pp])
        espd.close()
    em.wait("g", o6s[0])
    em.wait("g", o6s[1])
    if dev:
        for dap, sap in dump_specs:
            dma(em, "g", sG, dap, sap)
    em.wait("g", sG, sG.n)
    barrier(em)
    em.emit(nc)
    es6.close()
    gstk.close()
    return nc


def _host_layout(inputs):
    import ml_dtypes

    def g(k):
        return np.ascontiguousarray(np.asarray(inputs[k], dtype=np.float32))

    Wqkvz = g("L0_attn_W_qkvz").reshape(D, HK, 2 * DK + 2 * 2 * DV)
    out = {
        "Wq": np.ascontiguousarray(Wqkvz[:, :, :DK].reshape(D, HK * DK)),
        "Wk": np.ascontiguousarray(Wqkvz[:, :, DK:2 * DK].reshape(D, HK * DK)),
        "Wv": np.ascontiguousarray(Wqkvz[:, :, 2 * DK:2 * DK + 2 * DV].reshape(D, HV * DV)),
        "Wz": np.ascontiguousarray(Wqkvz[:, :, 2 * DK + 2 * DV:].reshape(D, HV * DV)),
    }
    Wba4 = g("L0_attn_W_ba").reshape(D, HK, 4)
    out["Wba"] = np.ascontiguousarray(np.concatenate(
        [Wba4[:, :, :2].reshape(D, HV), Wba4[:, :, 2:].reshape(D, HV)], axis=1))
    out["convw"] = g("L0_attn_conv_w")
    out["Wout"] = g("L0_attn_W_out")
    out["Wg"] = g("L0_attn_W_gate")
    out["Wu"] = g("L0_attn_W_up")
    out["Wd"] = g("L0_attn_W_down")
    out["Fg"] = np.ascontiguousarray(g("L0_ffn_w_gate").T)
    out["Fu"] = np.ascontiguousarray(g("L0_ffn_w_up").T)
    out["Fd"] = np.ascontiguousarray(g("L0_ffn_w_down").T)
    out["ln1c"] = np.ascontiguousarray(g("L0_attn_ln1_w").reshape(16, 128).T)
    out["ln2c"] = np.ascontiguousarray(g("L0_attn_ln2_w").reshape(16, 128).T)
    out["dtrep"] = np.ascontiguousarray(np.tile(g("L0_attn_dt_bias")[None, :], (128, 1)))
    out["alogrep"] = np.ascontiguousarray(np.tile(g("L0_attn_A_log")[None, :], (128, 1)))
    out["gnwz"] = np.ascontiguousarray(np.tile(g("L0_attn_gated_norm_w")[None, :], (128, 4)))
    out["flnrep"] = np.ascontiguousarray(np.tile(g("L0_ffn_ln_weight")[None, :], (128, 1)))
    out["identf"] = np.eye(128, dtype=np.float32)
    out["identb"] = np.eye(128, dtype=ml_dtypes.bfloat16)
    tt = np.arange(128)
    m0 = np.where(tt[None, :] < tt[:, None], 0.0, NEGC).astype(np.float32)
    m1 = np.where(tt[None, :] > tt[:, None], 0.0, NEGC).astype(np.float32)
    m2 = np.where(tt[None, :] >= tt[:, None], 0.0, NEGC).astype(np.float32)
    out["m0T"] = np.ascontiguousarray(m0.T)
    out["m1T"] = np.ascontiguousarray(m1.T)
    out["m2T"] = np.ascontiguousarray(m2.T)
    out["negup"] = np.ascontiguousarray(
        np.where(tt[None, :] >= tt[:, None], -1.0, 0.0).astype(np.float32))
    ep = np.zeros((128, 8), np.float32)
    ep[:, 0] = 1e-5; ep[:, 1] = 1e-6; ep[:, 2] = 1.0 / 128.0
    ep[:, 3] = 1.0 / D; ep[:, 4] = 1.0 / IF; ep[:, 5] = LNQ
    out["epsc"] = ep
    return out


def kernel(**inputs):
    shared = _host_layout(inputs)
    X = np.ascontiguousarray(np.asarray(inputs["L0_attn_hidden_states"], dtype=np.float32))
    nc = build_program(dev=DEV)
    in_maps = []
    ncore = 8
    for b in range(ncore):
        m = dict(shared)
        m["x"] = np.ascontiguousarray(X[b])
        in_maps.append(m)
    res = run_bass_kernel_spmd(nc, in_maps, core_ids=list(range(ncore)))
    if DEV:
        kernel.dev_results = res.results
    return np.stack([r["out"] for r in res.results], axis=0)


# version 04
# speedup vs baseline: 1.0267x; 1.0267x over previous
import numpy as np
from contextlib import ExitStack
import concourse.bass as bass
import concourse.mybir as mybir
from concourse.bass_utils import run_bass_kernel_spmd

F32 = mybir.dt.float32
BF16 = mybir.dt.bfloat16
AX = mybir.AxisListType.X
AF = mybir.ActivationFunctionType
ALU = mybir.AluOpType

D = 2048
L = 2048
HK, HV = 16, 32
DK = DV = 128
C = 128
NCH = L // C
IA = 5120
IF = 8192
NEGC = -1e30
MAGIC = float(3 * 2 ** 22)
LNQ = float(np.log(DK ** -0.5))

DEV = False


class Sem:
    def __init__(self, stk, nc, name):
        self.h = stk.enter_context(nc.semaphore(name))
        self.n = 0


class Em:
    ENGS = ("y", "t", "v", "s", "g")

    def __init__(self):
        self.q = {e: [] for e in self.ENGS}
        self.lastw = {e: {} for e in self.ENGS}

    def on(self, eng, fn):
        self.q[eng].append(fn)

    def wait(self, eng, sem, val=None):
        v = sem.n if val is None else val
        if v <= 0:
            return
        if self.lastw[eng].get(id(sem), -1) >= v:
            return
        self.lastw[eng][id(sem)] = v
        self.q[eng].append(lambda e, v=v, s=sem: e.wait_ge(s.h, v))

    def emit(self, nc):
        with nc.Block() as blk:
            for eng, deco in (("y", blk.sync), ("t", blk.tensor), ("v", blk.vector),
                              ("s", blk.scalar), ("g", blk.gpsimd)):
                fns = self.q[eng]

                def mk(fns):
                    def _(e):
                        for f in fns:
                            f(e)
                    return _
                deco(mk(fns))


def build_program(dev=False, stop_after=None):
    nc = bass.Bass("TRN2", target_bir_lowering=False, detect_race_conditions=False)

    def inp(name, shape, dt=F32):
        return nc.dram_tensor(name, shape, dt, kind="ExternalInput").ap()

    x_h = inp("x", [L, D])
    Wq_h = inp("Wq", [D, HK * DK])
    Wk_h = inp("Wk", [D, HK * DK])
    Wv_h = inp("Wv", [D, HV * DV])
    Wz_h = inp("Wz", [D, HV * DV])
    Wba_h = inp("Wba", [D, 64])
    convw_h = inp("convw", [IF, 4])
    Wout_h = inp("Wout", [HV * DV, D])
    Wg_h = inp("Wg", [D, IA])
    Wu_h = inp("Wu", [D, IA])
    Wd_h = inp("Wd", [IA, D])
    Fg_h = inp("Fg", [D, IF])
    Fu_h = inp("Fu", [D, IF])
    Fd_h = inp("Fd", [IF, D])
    ln1c_h = inp("ln1c", [128, 16])
    ln2c_h = inp("ln2c", [128, 16])
    dtrep_h = inp("dtrep", [128, HV])
    alogrep_h = inp("alogrep", [128, HV])
    gnwz_h = inp("gnwz", [128, 512])
    flnrep_h = inp("flnrep", [128, IF])
    identf_h = inp("identf", [128, 128])
    identb_h = inp("identb", [128, 128], BF16)
    m0T_h = inp("m0T", [128, 128])
    m1T_h = inp("m1T", [128, 128])
    m2T_h = inp("m2T", [128, 128])
    negup_h = inp("negup", [128, 128])
    epsc_h = inp("epsc", [128, 8])

    out_h = nc.dram_tensor("out", [L, D], F32, kind="ExternalOutput").ap()

    def scr(name, shape, dt):
        return nc.dram_tensor(name, shape, dt, kind="Internal").ap()

    xn_s = scr("xn_s", [L, D], F32)
    qh_s = scr("qh_s", [HK * DK, L], BF16)
    kh_s = scr("kh_s", [HK * DK, L], BF16)
    v_s = scr("v_s", [L, HV * DV], BF16)
    sz_s = scr("sz_s", [L, HV * DV], BF16)
    core_s = scr("core_s", [L, HV * DV], BF16)
    h_s = scr("h_s", [L, D], F32)
    ao_s = scr("ao_s", [L, D], F32)
    hl_s = scr("hl_s", [L, IF], F32)
    wdq_s = scr("wdq_s", [IF, D], BF16)

    dump_specs = []
    hat_dbg = None
    if dev:
        hat_dbg = nc.dram_tensor("d_hAT", [128, 16 * L], BF16, kind="ExternalOutput").ap()
        mix_dbg = nc.dram_tensor("d_mix0", [128, 3 + L], BF16, kind="ExternalOutput").ap()
        cvs_dbg = nc.dram_tensor("d_cvs0", [128, L], BF16, kind="ExternalOutput").ap()
        st_dbg = nc.dram_tensor("d_stats", [128, 64], F32, kind="ExternalOutput").ap()
        h2t_dbg = nc.dram_tensor("d_h2T", [128, 16 * 1024], BF16, kind="ExternalOutput").ap()
        mid_dbg = nc.dram_tensor("d_mid", [128, 40 * 1024], BF16, kind="ExternalOutput").ap()
    if dev:
        for nm, ap_, shape, dt in [
            ("d_xn", xn_s, [L, D], F32), ("d_qh", qh_s, [HK * DK, L], BF16),
            ("d_kh", kh_s, [HK * DK, L], BF16), ("d_v", v_s, [L, HV * DV], BF16),
            ("d_sz", sz_s, [L, HV * DV], BF16), ("d_core", core_s, [L, HV * DV], BF16),
            ("d_h", h_s, [L, D], F32), ("d_ao", ao_s, [L, D], F32),
            ("d_hl", hl_s, [L, IF], F32), ("d_wdq", wdq_s, [IF, D], BF16),
        ]:
            dump_specs.append((nc.dram_tensor(nm, shape, dt, kind="ExternalOutput").ap(), ap_))

    gstk = ExitStack()
    E = gstk.enter_context

    def emit_dumps(em_prev_bar):
        emd = Em()
        for eng in Em.ENGS:
            emd.wait(eng, bar, em_prev_bar)
        if dev:
            for dap, sap in dump_specs:
                sG.n += 16
                emd.on("g", lambda e, o=dap, i=sap, s=sG: e.dma_start(out=o, in_=i).then_inc(s.h, 16))
        emd.on("g", lambda e, v=sG.n, s=sG: e.wait_ge(s.h, v))
        emd.emit(nc)

    identf = E(nc.sbuf_tensor("sb_identf", [128, 128], F32))
    identb = E(nc.sbuf_tensor("sb_identb", [128, 128], BF16))
    m0T = E(nc.sbuf_tensor("sb_m0T", [128, 128], F32))
    m1T = E(nc.sbuf_tensor("sb_m1T", [128, 128], F32))
    m2T = E(nc.sbuf_tensor("sb_m2T", [128, 128], F32))
    negup = E(nc.sbuf_tensor("sb_negup", [128, 128], F32))
    epsc = E(nc.sbuf_tensor("sb_epsc", [128, 8], F32))
    ln1c = E(nc.sbuf_tensor("sb_ln1c", [128, 16], F32))
    ln2c = E(nc.sbuf_tensor("sb_ln2c", [128, 16], F32))
    dtrep = E(nc.sbuf_tensor("sb_dtrep", [128, HV], F32))
    alogrep = E(nc.sbuf_tensor("sb_alogrep", [128, HV], F32))
    gnwz = E(nc.sbuf_tensor("sb_gnwz", [128, 512], BF16))
    ones_col = E(nc.sbuf_tensor("ones_col", [128, 1], F32))
    ones_colb = E(nc.sbuf_tensor("ones_colb", [128, 1], BF16))
    ones_row = E(nc.sbuf_tensor("ones_row", [1, 128], F32))
    nones_row = E(nc.sbuf_tensor("nones_row", [1, 128], F32))
    expAl = E(nc.sbuf_tensor("expAl", [128, HV], F32))
    mstat = E(nc.sbuf_tensor("mstat", [128, 16], F32))
    qstat = E(nc.sbuf_tensor("qstat", [128, 16], F32))
    rstdS = E(nc.sbuf_tensor("rstdS", [128, 16], F32))
    nmrS = E(nc.sbuf_tensor("nmrS", [128, 16], F32))
    rrS = E(nc.sbuf_tensor("rrS", [128, 16], F32))
    partW = E(nc.sbuf_tensor("partW", [128, 192], F32, side="right"))
    pcol = E(nc.sbuf_tensor("pcol", [128, 4], F32, side="right"))
    esP = ExitStack()
    EP = esP.enter_context
    ba_all = EP(nc.sbuf_tensor("ba_all", [128, NCH * 64], F32, side="right"))
    beta_all = EP(nc.sbuf_tensor("beta_all", [128, NCH * HV], F32, side="right"))
    gam_all = EP(nc.sbuf_tensor("gam_all", [128, NCH * HV], F32, side="right"))
    nbg_all = EP(nc.sbuf_tensor("nbg_all", [128, NCH * HV], F32, side="right"))
    rows_all = EP(nc.sbuf_tensor("rows_all", [64, NCH * 128], F32, side="right"))
    glast = EP(nc.sbuf_tensor("glast", [1, NCH * HV], F32, side="right"))
    swg_c = E(nc.sbuf_tensor("swg_c", [128, 1], F32))
    swu_c = E(nc.sbuf_tensor("swu_c", [128, 1], F32))
    swd_c = E(nc.sbuf_tensor("swd_c", [128, 1], F32))
    rswg_c = E(nc.sbuf_tensor("rswg_c", [128, 1], F32))
    rswu_c = E(nc.sbuf_tensor("rswu_c", [128, 1], F32))
    rswd_c = E(nc.sbuf_tensor("rswd_c", [128, 1], F32))
    rsx = E(nc.sbuf_tensor("rsx", [128, 16], F32))
    sx_all = E(nc.sbuf_tensor("sx_all", [128, 16], F32))
    cs2a = E(nc.sbuf_tensor("cs2a", [128, 16], F32))
    cdna = E(nc.sbuf_tensor("cdna", [128, 16], F32))
    cc1a = E(nc.sbuf_tensor("cc1a", [128, 16], F32))
    cc2a = E(nc.sbuf_tensor("cc2a", [128, 16], F32))
    tA = E(nc.sbuf_tensor("tA", [128, 16], F32))
    tB = E(nc.sbuf_tensor("tB", [128, 16], F32))
    t1x1 = E(nc.sbuf_tensor("t1x1", [1, 4], F32))

    sY = Sem(gstk, nc, "sY")
    sS = Sem(gstk, nc, "sS")
    sG = Sem(gstk, nc, "sG")
    cT = Sem(gstk, nc, "cT")
    cV = Sem(gstk, nc, "cV")
    cS = Sem(gstk, nc, "cS")
    bar = Sem(gstk, nc, "bar")
    dx = [Sem(gstk, nc, f"dx{i}") for i in range(2)]
    gxs = [Sem(gstk, nc, f"gxs{i}") for i in range(2)]
    wst = [Sem(gstk, nc, f"wst{i}") for i in range(2)]
    qst = [Sem(gstk, nc, f"qst{i}") for i in range(2)]
    vst = [Sem(gstk, nc, f"vst{i}") for i in range(2)]
    zst = Sem(gstk, nc, "zst")
    szst = [Sem(gstk, nc, f"szst{i}") for i in range(2)]
    fl_st = [Sem(gstk, nc, f"fl{i}") for i in range(2)]
    fqs = [Sem(gstk, nc, f"fqs{i}") for i in range(2)]
    p2l = [Sem(gstk, nc, f"p2l{i}") for i in range(2)]
    cst = [Sem(gstk, nc, f"cst{i}") for i in range(2)]
    p3l = Sem(gstk, nc, "p3l")
    x3st = [Sem(gstk, nc, f"x3st{i}") for i in range(2)]
    h3st = [Sem(gstk, nc, f"h3st{i}") for i in range(2)]
    h4l = [Sem(gstk, nc, f"h4l{i}") for i in range(2)]
    w4l = Sem(gstk, nc, "w4l")
    hrl = [Sem(gstk, nc, f"hrl{i}") for i in range(2)]
    aos = [Sem(gstk, nc, f"aos{i}") for i in range(2)]
    x5l = [Sem(gstk, nc, f"x5l{i}") for i in range(2)]
    g5l = Sem(gstk, nc, "g5l")
    hls = Sem(gstk, nc, "hls")
    h6l = Sem(gstk, nc, "h6l")
    w6l = [Sem(gstk, nc, f"w6l{i}") for i in range(2)]
    a6l = Sem(gstk, nc, "a6l")
    o6s = [Sem(gstk, nc, f"o6s{i}") for i in range(2)]

    def dma(em, eng, sem, out, in_):
        sem.n += 16
        em.on(eng, lambda e, o=out, i=in_, s=sem: e.dma_start(out=o, in_=i).then_inc(s.h, 16))
        return sem.n

    def op(em, eng, sem, fn):
        sem.n += 1
        em.on(eng, lambda e, f=fn, s=sem: f(e).then_inc(s.h, 1))
        return sem.n

    def vop(em, fn):
        n = op(em, "v", cV, fn)
        em.wait("v", cV, n)
        return n

    def vrecip(em, out_ap, in_ap):
        em.wait("v", cV, cV.n)
        n = op(em, "v", cV, lambda e, o=out_ap, i=in_ap: e.reciprocal(o, i))
        em.wait("v", cV, n)
        return n

    def barrier(em):
        bar.n += 5
        for eng in Em.ENGS:
            em.on(eng, lambda e: e.sem_inc(bar.h, 1))
        return bar.n

    def phase_start(em, tgt):
        for eng in Em.ENGS:
            em.wait(eng, bar, tgt)

    def slab_load(em, eng, sem, slab, Wh, j0, w, nk=16, a0=0, ncols=None):
        ncols = nk * w if ncols is None else ncols
        tot = Wh.shape[0] // 128
        srcv = Wh.rearrange("(a p) n -> p a n", p=128)
        last = None
        nhalf = 2 if nk > 4 else 1
        for half in range(nhalf):
            b0, b1 = half * nk // nhalf, (half + 1) * nk // nhalf
            src = srcv[:, a0 + b0:a0 + b1, j0:j0 + w]
            dst = slab[:, :].rearrange("p (a n) -> p a n", n=w)[:, b0:b1, :]
            last = dma(em, eng, sem, dst, src)
        return last

    esI = ExitStack()
    gnwz32 = esI.enter_context(nc.sbuf_tensor("gnwz32", [128, 512], F32))
    em = Em()
    for dst, src in [(identf, identf_h), (identb, identb_h), (m0T, m0T_h), (m1T, m1T_h),
                     (m2T, m2T_h), (negup, negup_h), (epsc, epsc_h), (ln1c, ln1c_h), (ln2c, ln2c_h),
                     (dtrep, dtrep_h), (alogrep, alogrep_h), (gnwz32, gnwz_h)]:
        dma(em, "y", sY, dst[:], src)
    em.wait("v", sY)
    op(em, "v", cV, lambda e: e.memset(ones_col[:], 1.0))
    op(em, "v", cV, lambda e: e.memset(ones_colb[:], 1.0))
    op(em, "v", cV, lambda e: e.memset(ones_row[:], 1.0))
    op(em, "v", cV, lambda e: e.memset(nones_row[:], -1.0))
    op(em, "v", cV, lambda e: e.tensor_copy(gnwz[:], gnwz32[:]))
    em.wait("s", sY)
    op(em, "s", cS, lambda e: e.activation(expAl[:], alogrep[:], AF.Exp))
    tbar = barrier(em)
    em.emit(nc)
    esI.close()
    if stop_after == "init":
        emit_dumps(tbar)
        return nc

    esA = ExitStack()
    hAT = esA.enter_context(nc.sbuf_tensor("hAT", [128, 16 * L], BF16))
    es0 = ExitStack()
    E0 = es0.enter_context
    xt = [E0(nc.sbuf_tensor(f"xt{i}", [128, D], F32)) for i in range(2)]
    sqdump = E0(nc.sbuf_tensor("sqdump", [128, D], F32))
    xn_sb = [E0(nc.sbuf_tensor(f"xn_sb{i}", [128, D], F32)) for i in range(2)]
    hat_tok = [E0(nc.sbuf_tensor(f"hat_tok{i}", [128, D], BF16)) for i in range(2)]
    ps_tr = [E0(nc.psum_tensor(f"ps_tr{i}", [128, 128], F32)) for i in range(4)]

    em = Em()
    phase_start(em, tbar)
    for i in range(16):
        xd = dma(em, "y", dx[i % 2], xt[i % 2][:], x_h[i * 128:(i + 1) * 128, :])
        em.wait("v", dx[i % 2], xd)
        vred = op(em, "v", cV, lambda e, i=i: e.reduce_sum(mstat[:, i:i + 1], xt[i % 2][:], AX))
        em.wait("s", dx[i % 2], xd)
        sred = op(em, "s", cS, lambda e, i=i: e.activation(
            sqdump[:], xt[i % 2][:], AF.Square, accum_out=qstat[:, i:i + 1]))
        if i < 15:
            em.wait("y", cV, vred)
            em.wait("y", cS, sred)
    em.wait("v", cS, cS.n)
    em.wait("v", cV, cV.n)
    vop(em, lambda e: e.tensor_scalar(mstat[:], mstat[:], 1.0 / D, None, ALU.mult))
    vop(em, lambda e: e.tensor_scalar(qstat[:], qstat[:], 1.0 / D, None, ALU.mult))
    vop(em, lambda e: e.tensor_mul(tA[:], mstat[:], mstat[:]))
    vv = vop(em, lambda e: e.tensor_sub(qstat[:], qstat[:], tA[:]))
    em.wait("s", cV, vv)
    op(em, "s", cS, lambda e: e.activation(tB[:], qstat[:], AF.Ln, bias=epsc[:, 0:1]))
    ss1 = op(em, "s", cS, lambda e: e.activation(rstdS[:], tB[:], AF.Exp, scale=-0.5))
    em.wait("v", cS, ss1)
    vop(em, lambda e: e.tensor_scalar(tA[:], qstat[:], 1e-5, None, ALU.add))
    vrecip(em, tB[:], tA[:])
    vv2 = vop(em, lambda e: e.tensor_mul(tA[:], qstat[:], tB[:]))
    em.wait("s", cV, vv2)
    op(em, "s", cS, lambda e: e.activation(tB[:], tA[:], AF.Ln, bias=epsc[:, 1:2]))
    ss2 = op(em, "s", cS, lambda e: e.activation(rrS[:], tB[:], AF.Exp, scale=-0.5))
    tr_evac = []
    for i in range(16):
        xd = dma(em, "y", dx[i % 2], xt[i % 2][:], x_h[i * 128:(i + 1) * 128, :])
        em.wait("v", dx[i % 2], xd)
        em.wait("v", cS, ss2)
        em.wait("v", gxs[i % 2])
        sxn = op(em, "v", cV, lambda e, i=i: e.tensor_scalar(
            xn_sb[i % 2][:], xt[i % 2][:], mstat[:, i:i + 1], rstdS[:, i:i + 1],
            ALU.subtract, ALU.mult))
        em.wait("g", cV, sxn)
        gx = dma(em, "g", gxs[i % 2], xn_s[i * 128:(i + 1) * 128, :], xn_sb[i % 2][:])
        vht = op(em, "v", cV, lambda e, i=i: e.tensor_scalar(
            hat_tok[i % 2][:], xn_sb[i % 2][:], rrS[:, i:i + 1], None, ALU.mult))
        em.wait("t", cV, vht)
        for c in range(16):
            n = i * 16 + c
            if n >= 4:
                em.wait("t", cV, tr_evac[n - 4])
            op(em, "t", cT, lambda e, c=c, i=i, n=n: e.matmul(
                ps_tr[n % 4][:], hat_tok[i % 2][:, c * 128:(c + 1) * 128], identb[:],
                start=True, stop=True))
            em.wait("v", cT, cT.n)
            ev = op(em, "v", cV, lambda e, c=c, i=i, n=n: e.tensor_scalar(
                hAT[:, c * L + i * 128: c * L + i * 128 + 128], ps_tr[n % 4][:],
                ln1c[:, c:c + 1], None, ALU.mult))
            tr_evac.append(ev)
        if i < 15:
            em.wait("y", cV, cV.n)
    em.wait("g", gxs[0])
    em.wait("g", gxs[1])
    if dev:
        em.wait("g", cV, cV.n)
        em.wait("g", cS, cS.n)
        dma(em, "g", sG, hat_dbg, hAT[:])
        dma(em, "g", sG, st_dbg[:, 0:16], mstat[:])
        dma(em, "g", sG, st_dbg[:, 16:32], qstat[:])
        dma(em, "g", sG, st_dbg[:, 32:48], rstdS[:])
        dma(em, "g", sG, st_dbg[:, 48:64], rrS[:])
        em.wait("g", sG, sG.n)
    tbar = barrier(em)
    em.emit(nc)
    es0.close()
    if stop_after == "p0":
        emit_dumps(tbar)
        return nc

    es1 = ExitStack()
    E0 = es1.enter_context
    wsl32 = [E0(nc.sbuf_tensor(f"wsl32{i}", [128, 2048], F32)) for i in range(2)]
    wslb = [E0(nc.sbuf_tensor(f"wslb{i}", [128, 2048], BF16)) for i in range(2)]
    mix = E0(nc.sbuf_tensor("mix", [128, 3 + L], BF16))
    cacc = [E0(nc.sbuf_tensor(f"cacc{i}", [128, L], BF16)) for i in range(2)]
    convs = E0(nc.sbuf_tensor("convs", [128, L], BF16))
    sqb = E0(nc.sbuf_tensor("sqb", [128, L], BF16))
    cw4 = [E0(nc.sbuf_tensor(f"cw4{i}", [128, 4], F32)) for i in range(2)]
    lnrow = E0(nc.sbuf_tensor("lnrow", [1, 2048], F32))
    rstrow = E0(nc.sbuf_tensor("rstrow", [1, L], F32))
    qk_sb = [E0(nc.sbuf_tensor(f"qk_sb{i}", [128, L], BF16)) for i in range(2)]
    vtr = [E0(nc.sbuf_tensor(f"vtr{i}", [128, 512], BF16)) for i in range(2)]
    ft32 = [E0(nc.sbuf_tensor(f"ft32_{i}", [128, 2048], F32)) for i in range(2)]
    fdump = E0(nc.sbuf_tensor("fdump", [128, 2048], F32))

    ps_mm = [E0(nc.psum_tensor(f"ps_mm{i}", [128, 512], F32)) for i in range(2)]
    ps_s4 = [E0(nc.psum_tensor(f"ps_s4_{i}", [1, 512], F32)) for i in range(4)]
    ps_bc = [E0(nc.psum_tensor(f"ps_bc{i}", [128, 512], F32)) for i in range(2)]

    em = Em()
    phase_start(em, tbar)
    cast_of = {}
    conv_of = {}
    mm_evac = {0: 0, 1: 0}
    bc_evac = {0: 0, 1: 0}
    ss_evac = {}
    vtr_dma = {0: 0, 1: 0}
    qk_dma = {0: 0, 1: 0}
    sred_of = {}

    def emit_wred(idx):
        if idx >= 192:
            return
        if idx < 64:
            Wh, a, sgi = Fg_h, idx // 4, idx % 4
        elif idx < 128:
            Wh, a, sgi = Fu_h, (idx - 64) // 4, idx % 4
        else:
            Wh, a, sgi = Fd_h, idx - 128, 0
        pp = idx % 2
        if idx >= 2:
            em.wait("y", cS, sred_of[idx - 2])
        if idx >= 128:
            fd_ = dma(em, "y", fl_st[pp], ft32[pp][:], Wh[a * 128:(a + 1) * 128, :])
        else:
            fd_ = dma(em, "y", fl_st[pp], ft32[pp][:],
                      Wh[a * 128:(a + 1) * 128, sgi * 2048:(sgi + 1) * 2048])
        em.wait("s", fl_st[pp], fd_)
        sred_of[idx] = op(em, "s", cS, lambda e, pp=pp, idx=idx: e.activation(
            fdump[:], ft32[pp][:], AF.Abs,
            accum_out=partW[:, idx:idx + 1]))

    WRED_ON = True
    for j in range(64):
        par = j % 2
        if WRED_ON:
            for m3 in range(3):
                emit_wred(j * 3 + m3)
        if j < 16:
            Wh, jj = Wq_h, j
        elif j < 32:
            Wh, jj = Wk_h, j - 16
        else:
            Wh, jj = Wv_h, j - 32
        if j >= 2:
            em.wait("y", cV, cast_of[j - 2])
            em.wait("y", cV, conv_of[j - 2])
        sd = slab_load(em, "y", wst[par], wsl32[par], Wh, jj * 128, 128)
        cwd = dma(em, "y", wst[par], cw4[par][:], convw_h[j * 128:(j + 1) * 128, :])
        em.wait("v", wst[par], cwd)
        cast = op(em, "v", cV, lambda e, par=par: e.tensor_copy(wslb[par][:], wsl32[par][:]))
        cast_of[j] = cast
        em.wait("t", cV, cast)
        for n in range(4):
            bk = n % 2
            em.wait("t", cV, mm_evac[bk])
            for a in range(16):
                t_ = op(em, "t", cT, lambda e, par=par, a=a, n=n, bk=bk: e.matmul(
                    ps_mm[bk][:], wslb[par][:, a * 128:(a + 1) * 128],
                    hAT[:, a * L + n * 512: a * L + n * 512 + 512],
                    start=(a == 0), stop=(a == 15)))
            em.wait("v", cT, t_)
            ev = op(em, "v", cV, lambda e, n=n, bk=bk: e.tensor_copy(
                mix[:, 3 + n * 512: 3 + n * 512 + 512], ps_mm[bk][:]))
            mm_evac[bk] = ev
        op(em, "v", cV, lambda e: e.memset(mix[:, 0:3], 0.0))
        em.wait("v", wst[par], cwd)
        em.wait("v", cS, cS.n)
        op(em, "v", cV, lambda e, par=par: e.tensor_scalar(
            cacc[0][:], mix[:, 0:L], cw4[par][:, 0:1], None, ALU.mult))
        for i_ in range(1, 4):
            vcv = op(em, "v", cV, lambda e, i_=i_, par=par: e.scalar_tensor_tensor(
                cacc[i_ % 2][:], mix[:, i_:i_ + L], cw4[par][:, i_:i_ + 1],
                cacc[(i_ + 1) % 2][:], ALU.mult, ALU.add))
        conv_of[j] = vcv
        em.wait("s", cV, vcv)
        ssil = op(em, "s", cS, lambda e: e.activation(convs[:], cacc[1][:], AF.Silu))
        if dev and j == 0:
            em.wait("g", cV, vcv)
            em.wait("g", cS, ssil)
            dma(em, "g", sG, mix_dbg, mix[:])
            dma(em, "g", sG, cvs_dbg, convs[:])
            em.wait("g", sG, sG.n)
        if j < 32:
            em.wait("v", cS, ssil)
            em.wait("v", qst[par], qk_dma[par])
            vsq = op(em, "v", cV, lambda e: e.tensor_mul(sqb[:], convs[:], convs[:]))
            em.wait("t", cV, vsq)
            em.wait("t", cV, ss_evac.get(j - 2, 0))
            for n in range(4):
                tss = op(em, "t", cT, lambda e, n=n: e.matmul(
                    ps_s4[n][:], ones_colb[:], sqb[:, n * 512:(n + 1) * 512],
                    start=True, stop=True))
            em.wait("v", cT, tss)
            for n in range(4):
                sev = op(em, "v", cV, lambda e, n=n: e.tensor_copy(
                    lnrow[0:1, n * 512:(n + 1) * 512], ps_s4[n][:]))
            ss_evac[j] = sev
            em.wait("s", cV, sev)
            op(em, "s", cS, lambda e: e.activation(rstrow[0:1, :], lnrow[0:1, :], AF.Ln, bias=epsc[0:1, 1:2]))
            srs = op(em, "s", cS, lambda e, j=j: e.activation(
                rstrow[0:1, :], rstrow[0:1, :], AF.Exp, scale=-0.5,
                bias=(epsc[0:1, 5:6] if j < 16 else 0.0)))
            em.wait("t", cS, srs)
            vsc = 0
            for n in range(4):
                bk = n % 2
                em.wait("t", cV, bc_evac[bk])
                tbc = op(em, "t", cT, lambda e, n=n, bk=bk: e.matmul(
                    ps_bc[bk][:], ones_row[:], rstrow[0:1, n * 512:(n + 1) * 512],
                    start=True, stop=True))
                em.wait("v", cT, tbc)
                bc_evac[bk] = vsc = op(em, "v", cV, lambda e, n=n, bk=bk, par=par: e.tensor_mul(
                    qk_sb[par][:, n * 512:(n + 1) * 512], convs[:, n * 512:(n + 1) * 512], ps_bc[bk][:]))
            em.wait("g", cV, vsc)
            if j < 16:
                qk_dma[par] = dma(em, "g", qst[par], qh_s[j * 128:(j + 1) * 128, :], qk_sb[par][:])
            else:
                qk_dma[par] = dma(em, "g", qst[par], kh_s[(j - 16) * 128:(j - 15) * 128, :], qk_sb[par][:])
        else:
            jj32 = j - 32
            em.wait("t", cS, ssil)
            for ii0 in range(0, 16, 4):
                pp = (ii0 // 4) % 2
                em.wait("t", cV, bc_evac[pp])
                em.wait("v", vst[pp], vtr_dma[pp])
                for ii in range(ii0, ii0 + 4):
                    ttr = op(em, "t", cT, lambda e, ii=ii, pp=pp: e.matmul(
                        ps_bc[pp][:, (ii % 4) * 128:(ii % 4) * 128 + 128],
                        convs[:, ii * 128:(ii + 1) * 128], identb[:], start=True, stop=True))
                em.wait("v", cT, ttr)
                bc_evac[pp] = op(em, "v", cV, lambda e, pp=pp: e.tensor_copy(vtr[pp][:], ps_bc[pp][:]))
                em.wait("g", cV, bc_evac[pp])
                vtr_dma[pp] = dma(
                    em, "g", vst[pp],
                    v_s.rearrange("(i p) f -> p i f", p=128)[:, ii0:ii0 + 4, jj32 * 128:(jj32 + 1) * 128],
                    vtr[pp][:, :].rearrange("p (i f) -> p i f", f=128))
    tbar = barrier(em)
    em.emit(nc)
    es1.close()
    if stop_after == "p1q":
        emit_dumps(tbar)
        return nc

    esz = ExitStack()
    E0 = esz.enter_context
    zsl32 = E0(nc.sbuf_tensor("zsl32", [128, 16 * 512], F32))
    zslb = E0(nc.sbuf_tensor("zslb", [128, 16 * 512], BF16))
    szg_sb = [E0(nc.sbuf_tensor(f"szg_sb{i}", [128, 512], BF16)) for i in range(2)]
    ps_mm = [E0(nc.psum_tensor(f"ps_zmm{i}", [128, 512], F32)) for i in range(2)]

    em = Em()
    phase_start(em, tbar)
    zcast = 0
    mm_evac = {0: 0, 1: 0}
    sz_dma = {0: 0, 1: 0}
    sz_mul = {0: 0, 1: 0}
    z_ev = {0: 0, 1: 0}
    for zc in range(8):
        em.wait("y", cV, zcast)
        zd = slab_load(em, "y", zst, zsl32, Wz_h, zc * 512, 512)
        em.wait("v", zst, zd)
        zcast = op(em, "v", cV, lambda e: e.tensor_copy(zslb[:], zsl32[:]))
        em.wait("t", cV, zcast)
        for i in range(16):
            bk = i % 2
            em.wait("t", cV, mm_evac[bk])
            em.wait("t", cS, z_ev[bk])
            for a in range(16):
                tz = op(em, "t", cT, lambda e, a=a, i=i, bk=bk: e.matmul(
                    ps_mm[bk][:], hAT[:, a * L + i * 128: a * L + i * 128 + 128],
                    zslb[:, a * 512:(a + 1) * 512], start=(a == 0), stop=(a == 15)))
            em.wait("s", cT, tz)
            em.wait("s", cV, sz_mul[bk])
            em.wait("s", szst[bk], sz_dma[bk])
            z_ev[bk] = op(em, "s", cS, lambda e, bk=bk: e.activation(
                szg_sb[bk][:], ps_mm[bk][:], AF.Silu))
            em.wait("v", cS, z_ev[bk])
            sz_mul[bk] = op(em, "v", cV, lambda e, bk=bk: e.tensor_mul(
                szg_sb[bk][:], szg_sb[bk][:], gnwz[:]))
            em.wait("g", cV, sz_mul[bk])
            sz_dma[bk] = dma(em, "g", szst[bk], sz_s[i * 128:(i + 1) * 128, zc * 512:(zc + 1) * 512], szg_sb[bk][:])
    tbar = barrier(em)
    em.emit(nc)
    esz.close()
    if stop_after == "p1z":
        emit_dumps(tbar)
        return nc

    esb = ExitStack()
    E0 = esb.enter_context
    wba32 = E0(nc.sbuf_tensor("wba32", [128, 1024], F32))
    wbab = E0(nc.sbuf_tensor("wbab", [128, 1024], BF16))
    spt = [E0(nc.sbuf_tensor(f"spt{i}", [128, HV], F32)) for i in range(4)]
    tin = E0(nc.sbuf_tensor("tin", [128, 64], F32))
    ps_mm = [E0(nc.psum_tensor(f"ps_bmm{i}", [128, 512], F32)) for i in range(2)]
    ps_G = E0(nc.psum_tensor("ps_G", [128, 64], F32))
    ps_R = E0(nc.psum_tensor("ps_R", [64, 128], F32))

    em = Em()
    phase_start(em, tbar)
    bad = slab_load(em, "y", sY, wba32, Wba_h, 0, 64)
    em.wait("v", sY, bad)
    bcast = op(em, "v", cV, lambda e: e.tensor_copy(wbab[:], wba32[:]))
    em.wait("t", cV, bcast)
    for i in range(16):
        bk = i % 2
        for a in range(16):
            tb_ = op(em, "t", cT, lambda e, a=a, i=i, bk=bk: e.matmul(
                ps_mm[bk][:, 0:64], hAT[:, a * L + i * 128: a * L + i * 128 + 128],
                wbab[:, a * 64:(a + 1) * 64], start=(a == 0), stop=(a == 15)))
        em.wait("v", cT, tb_)
        op(em, "v", cV, lambda e, i=i, bk=bk: e.tensor_copy(
            ba_all[:, i * 64:(i + 1) * 64], ps_mm[bk][:, 0:64]))
        em.wait("t", cV, cV.n)
    em.wait("s", cV, cV.n)
    for c_ in range(NCH):
        sb1 = op(em, "s", cS, lambda e, c_=c_: e.activation(
            beta_all[:, c_ * HV:(c_ + 1) * HV], ba_all[:, c_ * 64: c_ * 64 + 32], AF.Sigmoid))
        em.wait("v", cS, sb1)
        v1 = vop(em, lambda e, c_=c_: e.tensor_add(
            spt[0][:], ba_all[:, c_ * 64 + 32: c_ * 64 + 64], dtrep[:]))
        em.wait("s", cV, v1)
        op(em, "s", cS, lambda e: e.activation(spt[1][:], spt[0][:], AF.Abs))
        op(em, "s", cS, lambda e: e.activation(spt[1][:], spt[1][:], AF.Exp, scale=-1.0))
        s2_ = op(em, "s", cS, lambda e: e.activation(spt[1][:], spt[1][:], AF.Ln, bias=1.0))
        em.wait("v", cS, s2_)
        vop(em, lambda e: e.tensor_scalar(spt[2][:], spt[0][:], 0.0, None, ALU.max))
        vop(em, lambda e: e.tensor_add(spt[2][:], spt[2][:], spt[1][:]))
        v2 = vop(em, lambda e: e.tensor_mul(spt[2][:], spt[2][:], expAl[:]))
        em.wait("t", cV, v2)
        tg = op(em, "t", cT, lambda e: e.matmul(ps_G[:, 0:HV], negup[:], spt[2][:], start=True, stop=True))
        em.wait("v", cT, tg)
        vg = vop(em, lambda e: e.tensor_copy(tin[:, 32:64], ps_G[:, 0:HV]))
        em.wait("s", cV, vg)
        sg_ = op(em, "s", cS, lambda e, c_=c_: e.activation(
            gam_all[:, c_ * HV:(c_ + 1) * HV], tin[:, 32:64], AF.Exp))
        em.wait("v", cS, sg_)
        vop(em, lambda e, c_=c_: e.tensor_mul(
            spt[3][:], gam_all[:, c_ * HV:(c_ + 1) * HV], beta_all[:, c_ * HV:(c_ + 1) * HV]))
        vop(em, lambda e, c_=c_: e.tensor_scalar(
            nbg_all[:, c_ * HV:(c_ + 1) * HV], spt[3][:], -1.0, None, ALU.mult))
        op(em, "s", cS, lambda e, c_=c_: e.activation(
            spt[1][:], ba_all[:, c_ * 64: c_ * 64 + 32], AF.Abs))
        op(em, "s", cS, lambda e: e.activation(spt[1][:], spt[1][:], AF.Exp, scale=-1.0))
        s3_ = op(em, "s", cS, lambda e: e.activation(spt[1][:], spt[1][:], AF.Ln, bias=1.0))
        em.wait("v", cS, s3_)
        vop(em, lambda e, c_=c_: e.tensor_scalar(
            spt[0][:], ba_all[:, c_ * 64: c_ * 64 + 32], -1.0, 0.0, ALU.mult, ALU.max))
        vop(em, lambda e: e.tensor_add(spt[0][:], spt[0][:], spt[1][:]))
        v3 = vop(em, lambda e: e.tensor_sub(tin[:, 0:32], tin[:, 32:64], spt[0][:]))
        em.wait("t", cV, v3)
        tr_ = op(em, "t", cT, lambda e: e.matmul(ps_R[:], tin[:], identf[:], start=True, stop=True))
        em.wait("v", cT, tr_)
        op(em, "v", cV, lambda e, c_=c_: e.tensor_copy(
            rows_all[0:32, c_ * 128:(c_ + 1) * 128], ps_R[0:32, :]))
        v4 = op(em, "v", cV, lambda e, c_=c_: e.tensor_scalar(
            rows_all[32:64, c_ * 128:(c_ + 1) * 128], ps_R[32:64, :], -1.0, None, ALU.mult))
        em.wait("t", cV, v4)
    em.wait("g", cV, cV.n)
    em.wait("g", cS, cS.n)
    dma(em, "g", sG, glast[:], gam_all[127:128, :])
    em.wait("g", sG, sG.n)
    tbar = barrier(em)
    em.emit(nc)
    esb.close()
    esA.close()
    if stop_after == "p1b":
        emit_dumps(tbar)
        return nc

    es15 = ExitStack()
    E15 = es15.enter_context
    ftp = [E15(nc.sbuf_tensor(f"ftp{i}", [128, 2048], F32)) for i in range(2)]
    fq1 = E15(nc.sbuf_tensor("fq1", [128, 2048], F32))
    fqb = [E15(nc.sbuf_tensor(f"fqb{i}", [128, 2048], BF16)) for i in range(2)]
    ps_w = E15(nc.psum_tensor("ps_w", [1, 4], F32))
    ps_wb = E15(nc.psum_tensor("ps_wb", [128, 4], F32))
    wred15 = {}

    def emit_wred_p15(em, idx):
        if idx < 64:
            Wh, a, sgi = Fg_h, idx // 4, idx % 4
        elif idx < 128:
            Wh, a, sgi = Fu_h, (idx - 64) // 4, idx % 4
        else:
            Wh, a, sgi = Fd_h, idx - 128, 0
        pp = idx % 2
        if idx >= 2:
            em.wait("y", cS, wred15[idx - 2])
        if idx >= 128:
            fd_ = dma(em, "y", fl_st[pp], ftp[pp][:], Wh[a * 128:(a + 1) * 128, :])
        else:
            fd_ = dma(em, "y", fl_st[pp], ftp[pp][:],
                      Wh[a * 128:(a + 1) * 128, sgi * 2048:(sgi + 1) * 2048])
        em.wait("s", fl_st[pp], fd_)
        wred15[idx] = op(em, "s", cS, lambda e, pp=pp, idx=idx: e.activation(
            fq1[:], ftp[pp][:], AF.Abs, accum_out=partW[:, idx:idx + 1]))

    em = Em()
    phase_start(em, tbar)
    if not WRED_ON:
        for idx in range(192):
            emit_wred_p15(em, idx)
    for wi, Wh in enumerate((Fg_h, Fu_h, Fd_h)):
        first = wi * 64
        vp = op(em, "v", cV, lambda e, first=first, wi=wi: e.tensor_reduce(
            pcol[:, wi:wi + 1], partW[:, first:first + 64], AX, ALU.add))
        em.wait("v", cV, vp)
        em.wait("t", cV, vp)
        tw = op(em, "t", cT, lambda e, wi=wi: e.matmul(
            ps_w[0:1, wi:wi + 1], ones_col[:], pcol[:, wi:wi + 1], start=True, stop=True))
        em.wait("v", cT, tw)
        nel = float(Wh.shape[0] * Wh.shape[1])
        vop(em, lambda e, wi=wi, nel=nel: e.tensor_scalar(
            t1x1[0:1, wi:wi + 1], ps_w[0:1, wi:wi + 1], 1.0 / nel, 1e-5, ALU.mult, ALU.max))
        em.wait("v", cV, cV.n)
        vq = op(em, "v", cV, lambda e, wi=wi: e.reciprocal(t1x1[0:1, 3:4], t1x1[0:1, wi:wi + 1]))
        em.wait("v", cV, vq)
        em.wait("t", cV, vq)
        tb1 = op(em, "t", cT, lambda e, wi=wi: e.matmul(
            ps_wb[:, 0:1], ones_row[:], t1x1[0:1, 3:4], start=True, stop=True))
        tb2 = op(em, "t", cT, lambda e, wi=wi: e.matmul(
            ps_wb[:, 1:2], ones_row[:], t1x1[0:1, wi:wi + 1], start=True, stop=True))
        em.wait("v", cT, tb2)
        sw_dst = (swg_c, swu_c, swd_c)[wi]
        rsw_dst = (rswg_c, rswu_c, rswd_c)[wi]
        op(em, "v", cV, lambda e, sw_dst=sw_dst: e.tensor_copy(sw_dst[:], ps_wb[:, 0:1]))
        vb2 = op(em, "v", cV, lambda e, rsw_dst=rsw_dst: e.tensor_copy(rsw_dst[:], ps_wb[:, 1:2]))
        em.wait("v", cV, vb2)
        em.wait("t", cV, vb2)
    fq_dma = {0: 0, 1: 0}
    fq_hist = {}
    for a in range(IF // 128):
        pp = a % 2
        em.wait("y", cV, fq_hist[a - 2] if a >= 2 else 0)
        fd_ = dma(em, "y", fl_st[pp], ftp[pp][:], Fd_h[a * 128:(a + 1) * 128, :])
        em.wait("v", fl_st[pp], fd_)
        em.wait("v", fqs[pp], fq_dma[pp])
        op(em, "v", cV, lambda e, pp=pp: e.tensor_scalar(
            fq1[:], ftp[pp][:], swd_c[:], MAGIC, ALU.mult, ALU.add))
        vq1 = op(em, "v", cV, lambda e, pp=pp: e.tensor_scalar(
            fq1[:], fq1[:], MAGIC, None, ALU.subtract))
        fq_hist[a] = vq1
        vq2 = op(em, "v", cV, lambda e, pp=pp: e.tensor_scalar(
            fqb[pp][:], fq1[:], -1.0, 1.0, ALU.max, ALU.min))
        em.wait("g", cV, vq2)
        fq_dma[pp] = dma(em, "g", fqs[pp], wdq_s[a * 128:(a + 1) * 128, :], fqb[pp][:])
    em.wait("g", fqs[0])
    em.wait("g", fqs[1])
    tbar = barrier(em)
    em.emit(nc)
    es15.close()
    if stop_after == "p15":
        emit_dumps(tbar)
        return nc

    es2 = ExitStack()
    E2 = es2.enter_context
    k_fm = [E2(nc.sbuf_tensor(f"k_fm{i}", [128, 2048], BF16)) for i in range(2)]
    q_fm = [E2(nc.sbuf_tensor(f"q_fm{i}", [128, 2048], BF16)) for i in range(2)]
    v_tok = [E2(nc.sbuf_tensor(f"v_tok{i}", [128, 4096], BF16)) for i in range(2)]
    rows_cur = [E2(nc.sbuf_tensor(f"rows_cur{i}", [1, 8192], F32)) for i in range(2)]
    k_tok = E2(nc.sbuf_tensor("k_tok", [128, 2048], BF16))
    vb_sb = E2(nc.sbuf_tensor("vb_sb", [128, 4096], BF16))
    S32 = [E2(nc.sbuf_tensor(f"S32_{i}", [128, 4096], F32)) for i in range(2)]
    S0b = E2(nc.sbuf_tensor("S0b", [128, 4096], BF16))
    gC_sb = E2(nc.sbuf_tensor("gC_sb", [128, 32], F32))
    wcol32 = E2(nc.sbuf_tensor("wcol32", [128, 8], F32))
    d1b = E2(nc.sbuf_tensor("d1b", [128, 512], BF16))
    d3b = E2(nc.sbuf_tensor("d3b", [128, 512], BF16))
    mtb = [E2(nc.sbuf_tensor(f"mtb{i}", [128, 512], BF16)) for i in range(2)]
    mttb = [E2(nc.sbuf_tensor(f"mttb{i}", [128, 512], BF16)) for i in range(2)]
    attnT = [E2(nc.sbuf_tensor(f"attnT{i}", [128, 512], BF16)) for i in range(2)]
    bt_bf = [E2(nc.sbuf_tensor(f"bt_bf{i}", [128, 512], BF16)) for i in range(2)]
    core_t1 = [E2(nc.sbuf_tensor(f"core_t1_{i}", [128, 512], F32)) for i in range(2)]
    pow_u = E2(nc.sbuf_tensor("pow_u", [128, 5 * 512], BF16))
    pow_t = E2(nc.sbuf_tensor("pow_t", [128, 6 * 512], BF16))
    tch = [E2(nc.sbuf_tensor(f"tch{i}", [128, 512], BF16)) for i in range(2)]
    U_bf = E2(nc.sbuf_tensor("U_bf", [128, 512], BF16))
    uw_bf = E2(nc.sbuf_tensor("uw_bf", [128, 512], BF16))
    core_bf = [E2(nc.sbuf_tensor(f"core_bf{i}", [128, 512], BF16)) for i in range(2)]

    P_kkq = E2(nc.psum_tensor("P_kkq", [128, 512], F32))
    P_d1 = E2(nc.psum_tensor("P_d1", [128, 512], F32))
    P_d3 = E2(nc.psum_tensor("P_d3", [128, 512], F32))
    P_bt = E2(nc.psum_tensor("P_bt", [128, 512], F32))
    P_o2 = E2(nc.psum_tensor("P_o2", [128, 512], F32))
    P_sqA = E2(nc.psum_tensor("P_sqA", [128, 512], F32))
    P_sqB = E2(nc.psum_tensor("P_sqB", [128, 512], F32))
    P_ap = E2(nc.psum_tensor("P_ap", [128, 512], F32))

    em = Em()
    phase_start(em, tbar)
    op(em, "v", cV, lambda e: e.memset(S32[1][:], 0.0))
    szero = op(em, "v", cV, lambda e: e.memset(S0b[:], 0.0))
    em.wait("t", cV, szero)

    def chunk_pre_dma(c_, par):
        em.wait("s", cT, cT.n)
        em.wait("s", cV, cV.n)
        for h in range(16):
            dma(em, "s", p2l[par], k_fm[par][:, h * 128:(h + 1) * 128],
                kh_s[h * 128:(h + 1) * 128, c_ * 128:(c_ + 1) * 128])
        for h in range(16):
            dma(em, "s", p2l[par], q_fm[par][:, h * 128:(h + 1) * 128],
                qh_s[h * 128:(h + 1) * 128, c_ * 128:(c_ + 1) * 128])
        dma(em, "s", p2l[par], v_tok[par][:], v_s[c_ * 128:(c_ + 1) * 128, :])
        dma(em, "s", p2l[par], rows_cur[par][0:1, :].rearrange("o (r t) -> o r t", t=128),
            rows_all[:, c_ * 128:(c_ + 1) * 128])
        return p2l[par].n

    R2 = {"core_add": 0, "s_stt": 0, "pow6_ev": 0, "so2": 0, "uwv": 0}
    FE = {}

    def emit_fe_T_p123(gi, cpar, prev_tt_v, prev_exps):
        kh0, hv0 = gi * 2, gi * 4
        em.wait("t", cV, prev_tt_v)
        for kk in range(2):
            op(em, "t", cT, lambda e, kk=kk, kh0=kh0, cpar=cpar: e.matmul(
                P_kkq[:, kk * 128:(kk + 1) * 128],
                k_fm[cpar][:, (kh0 + kk) * 128:(kh0 + kk + 1) * 128],
                k_fm[cpar][:, (kh0 + kk) * 128:(kh0 + kk + 1) * 128], start=True, stop=True))
        for kk in range(2):
            op(em, "t", cT, lambda e, kk=kk, kh0=kh0, cpar=cpar: e.matmul(
                P_kkq[:, 256 + kk * 128: 256 + (kk + 1) * 128],
                k_fm[cpar][:, (kh0 + kk) * 128:(kh0 + kk + 1) * 128],
                q_fm[cpar][:, (kh0 + kk) * 128:(kh0 + kk + 1) * 128], start=True, stop=True))
        FE[gi] = {"kkq": cT.n}
        em.wait("t", cS, prev_exps)
        for vh in range(4):
            hv = hv0 + vh
            R1 = rows_cur[cpar][0:1, hv * 128:(hv + 1) * 128]
            NG = rows_cur[cpar][0:1, (32 + hv) * 128:(33 + hv) * 128]
            op(em, "t", cT, lambda e, vh=vh, R1=R1: e.matmul(
                P_d1[:, vh * 128:(vh + 1) * 128], R1, ones_row[:], start=True, stop=False))
            op(em, "t", cT, lambda e, vh=vh, NG=NG: e.matmul(
                P_d1[:, vh * 128:(vh + 1) * 128], ones_row[:], NG, start=False, stop=False))
            op(em, "t", cT, lambda e, vh=vh: e.matmul(
                P_d1[:, vh * 128:(vh + 1) * 128], m0T[:], identf[:], start=False, stop=True))
        for vh in range(4):
            hv = hv0 + vh
            NG = rows_cur[cpar][0:1, (32 + hv) * 128:(33 + hv) * 128]
            op(em, "t", cT, lambda e, vh=vh, NG=NG: e.matmul(
                P_d3[:, vh * 128:(vh + 1) * 128], NG, ones_row[:], start=True, stop=False))
            op(em, "t", cT, lambda e, vh=vh, NG=NG: e.matmul(
                P_d3[:, vh * 128:(vh + 1) * 128], nones_row[:], NG, start=False, stop=False))
            op(em, "t", cT, lambda e, vh=vh: e.matmul(
                P_d3[:, vh * 128:(vh + 1) * 128], m2T[:], identf[:], start=False, stop=True))
        FE[gi]["dmm"] = cT.n

    def emit_fe_S(gi, par, prev_tt_v):
        em.wait("s", cT, FE[gi]["dmm"])
        em.wait("s", cV, prev_tt_v)
        em.wait("s", cV, R2["uwv"])
        op(em, "s", cS, lambda e: e.activation(d1b[:], P_d1[:], AF.Exp))
        op(em, "s", cS, lambda e: e.activation(d3b[:], P_d3[:], AF.Exp))
        wbase = par * 4
        FE[gi]["exps"] = op(em, "s", cS, lambda e, wbase=wbase: e.activation(
            wcol32[:, wbase:wbase + 4],
            P_d3[:, :].rearrange("p (v t) -> p v t", t=128)[:, :, 127:128], AF.Exp))

    def emit_fe_V_A(gi, par):
        em.wait("v", cT, FE[gi]["kkq"])
        em.wait("v", cS, FE[gi]["exps"])
        for vh in range(4):
            kk = vh // 2
            op(em, "v", cV, lambda e, vh=vh, kk=kk, par=par: e.tensor_mul(
                mtb[par][:, vh * 128:(vh + 1) * 128], P_kkq[:, kk * 128:(kk + 1) * 128],
                d1b[:, vh * 128:(vh + 1) * 128]))
        FE[gi]["mtb_tt"] = cV.n
        for vh in range(4):
            kk = vh // 2
            op(em, "v", cV, lambda e, vh=vh, kk=kk, par=par: e.tensor_mul(
                attnT[par][:, vh * 128:(vh + 1) * 128],
                P_kkq[:, 256 + kk * 128: 256 + (kk + 1) * 128],
                d3b[:, vh * 128:(vh + 1) * 128]))
        FE[gi]["tt_v"] = cV.n

    def emit_fe_T_p4a(gi, par, prev_bt_stt):
        em.wait("t", cV, FE[gi]["mtb_tt"])
        em.wait("t", cV, prev_bt_stt)
        for vh in range(4):
            op(em, "t", cT, lambda e, vh=vh, par=par: e.matmul(
                P_bt[:, vh * 128:(vh + 1) * 128], mtb[par][:, vh * 128:(vh + 1) * 128],
                identb[:], start=True, stop=True))
        FE[gi]["mtT"] = cT.n

    def emit_fe_V_B(gi, par):
        em.wait("v", cT, FE[gi]["mtT"])
        op(em, "v", cV, lambda e, par=par: e.tensor_copy(mttb[par][:], P_bt[:]))
        FE[gi]["mttb_ev"] = cV.n

    def emit_fe_T_p4b(gi, par, cpar, prev_so2, extra_o2_cv):
        hv0, kh0 = gi * 4, gi * 2
        em.wait("t", cV, FE[gi]["mttb_ev"])
        for vh in range(4):
            hv = hv0 + vh
            kk = vh // 2
            op(em, "t", cT, lambda e, vh=vh, hv=hv, kk=kk, kh0=kh0, cpar=cpar: e.matmul(
                P_bt[:, vh * 128:(vh + 1) * 128],
                k_fm[cpar][:, (kh0 + kk) * 128:(kh0 + kk + 1) * 128],
                S0b[:, hv * 128:(hv + 1) * 128], start=True, stop=True))
        FE[gi]["btm"] = cT.n
        em.wait("t", cS, prev_so2)
        em.wait("t", cV, extra_o2_cv)
        for vh in range(4):
            hv = hv0 + vh
            kk = vh // 2
            op(em, "t", cT, lambda e, vh=vh, hv=hv, kk=kk, kh0=kh0, cpar=cpar: e.matmul(
                P_o2[:, vh * 128:(vh + 1) * 128],
                q_fm[cpar][:, (kh0 + kk) * 128:(kh0 + kk + 1) * 128],
                S0b[:, hv * 128:(hv + 1) * 128], start=True, stop=True))
        FE[gi]["o2m"] = cT.n

    def emit_fe_S_so2(gi, par, c_, prev_core_add_cv):
        hv0 = gi * 4
        em.wait("s", cT, FE[gi]["o2m"])
        em.wait("s", cV, prev_core_add_cv)
        so2 = 0
        for vh in range(4):
            hv = hv0 + vh
            so2 = op(em, "s", cS, lambda e, vh=vh, hv=hv, par=par, c_=c_: e.activation(
                core_t1[par][:, vh * 128:(vh + 1) * 128], P_o2[:, vh * 128:(vh + 1) * 128],
                AF.Copy, scale=gam_all[:, c_ * 32 + hv: c_ * 32 + hv + 1]))
        FE[gi]["so2"] = so2

    def emit_fe_V_C(gi, par, c_):
        hv0 = gi * 4
        em.wait("v", cT, FE[gi]["btm"])
        for vh in range(4):
            hv = hv0 + vh
            op(em, "v", cV, lambda e, vh=vh, hv=hv, c_=c_, par=par: e.scalar_tensor_tensor(
                bt_bf[par][:, vh * 128:(vh + 1) * 128], P_bt[:, vh * 128:(vh + 1) * 128],
                nbg_all[:, c_ * 32 + hv: c_ * 32 + hv + 1], vb_sb[:, hv * 128:(hv + 1) * 128],
                ALU.mult, ALU.add))
        FE[gi]["bt_stt"] = cV.n

    core_dma = {0: 0, 1: 0}
    pre_done = {0: chunk_pre_dma(0, 0)}
    for c_ in range(NCH):
        cpar = c_ % 2
        ping, pong = S32[(c_ + 1) % 2], S32[c_ % 2]
        if c_ + 1 < NCH:
            pre_done[c_ + 1] = chunk_pre_dma(c_ + 1, 1 - cpar)
        em.wait("t", p2l[cpar], pre_done[c_])
        em.wait("v", p2l[cpar], pre_done[c_])
        em.wait("t", cS, R2["so2"])
        tgc = op(em, "t", cT, lambda e, c_=c_: e.matmul(
            P_o2[:, 0:32], ones_row[:], glast[0:1, c_ * 32:(c_ + 1) * 32], start=True, stop=True))
        em.wait("v", cT, tgc)
        vgc = op(em, "v", cV, lambda e: e.tensor_copy(gC_sb[:], P_o2[:, 0:32]))
        em.wait("t", cV, R2["s_stt"])
        for g4 in range(4):
            for hh in range(4):
                h = g4 * 4 + hh
                tkt = op(em, "t", cT, lambda e, h=h, hh=hh, cpar=cpar: e.matmul(
                    P_sqA[:, hh * 128:(hh + 1) * 128], k_fm[cpar][:, h * 128:(h + 1) * 128],
                    identb[:], start=True, stop=True))
            em.wait("v", cT, tkt)
            vkt = op(em, "v", cV, lambda e, g4=g4: e.tensor_copy(
                k_tok[:, g4 * 512:(g4 + 1) * 512], P_sqA[:]))
            em.wait("t", cV, vkt)
        em.wait("v", cT, cT.n)
        for hv in range(32):
            op(em, "v", cV, lambda e, hv=hv, c_=c_, cpar=cpar: e.tensor_scalar(
                vb_sb[:, hv * 128:(hv + 1) * 128], v_tok[cpar][:, hv * 128:(hv + 1) * 128],
                beta_all[:, c_ * 32 + hv: c_ * 32 + hv + 1], None, ALU.mult))
        if c_ > 0:
            for seg in range(8):
                op(em, "v", cV, lambda e, seg=seg, ping=ping: e.tensor_copy(
                    S0b[:, seg * 512:(seg + 1) * 512], ping[:, seg * 512:(seg + 1) * 512]))
        s0_ready = cV.n
        em.wait("t", cV, s0_ready)

        prev = FEprev if c_ > 0 else {"tt_v": 0, "exps": 0, "bt_stt": 0, "so2": 0}
        emit_fe_T_p123(0, cpar, prev["tt_v"], prev["exps"])
        emit_fe_S(0, 0, prev["tt_v"])
        emit_fe_V_A(0, 0)
        emit_fe_T_p4a(0, 0, prev["bt_stt"])
        emit_fe_V_B(0, 0)
        emit_fe_T_p4b(0, 0, cpar, prev["so2"], vgc)
        emit_fe_S_so2(0, 0, c_, R2["core_add"])
        emit_fe_V_C(0, 0, c_)

        for g in range(8):
            par = g % 2
            npar = 1 - par
            fe = FE.pop(g)
            em.wait("t", cV, fe["mttb_ev"])
            em.wait("t", cV, fe["bt_stt"])
            em.wait("t", cV, R2["core_add"])
            for vh in range(4):
                op(em, "t", cT, lambda e, vh=vh, par=par: e.matmul(
                    P_ap[:, vh * 128:(vh + 1) * 128], mttb[par][:, vh * 128:(vh + 1) * 128],
                    bt_bf[par][:, vh * 128:(vh + 1) * 128], start=True, stop=True))
            r0m = cT.n
            if g < 7:
                emit_fe_T_p123(g + 1, cpar, fe["tt_v"], fe["exps"])
            em.wait("v", cT, r0m)
            tev = op(em, "v", cV, lambda e, par=par: e.tensor_sub(tch[0][:], bt_bf[par][:], P_ap[:]))
            powev = {}
            for k in range(1, 7):
                if k == 1:
                    em.wait("t", cV, R2["s_stt"])
                    em.wait("t", cS, R2["pow6_ev"])
                    pu, pt = mtb[par], mttb[par]
                else:
                    em.wait("t", cS, powev[k - 1])
                    pu = pow_u[:, (k - 2) * 512:(k - 1) * 512]
                    pt = pow_t[:, (k - 2) * 512:(k - 1) * 512]
                for vh in range(4):
                    if k < 6:
                        op(em, "t", cT, lambda e, vh=vh, pu=pu, pt=pt: e.matmul(
                            P_sqA[:, vh * 128:(vh + 1) * 128], pt[:, vh * 128:(vh + 1) * 128],
                            pu[:, vh * 128:(vh + 1) * 128], start=True, stop=True))
                    op(em, "t", cT, lambda e, vh=vh, pu=pu, pt=pt: e.matmul(
                        P_sqB[:, vh * 128:(vh + 1) * 128], pu[:, vh * 128:(vh + 1) * 128],
                        pt[:, vh * 128:(vh + 1) * 128], start=True, stop=True))
                sqm = cT.n
                if k >= 2:
                    em.wait("t", cV, tev)
                    ptap = pow_t[:, (k - 2) * 512:(k - 1) * 512]
                    tin_, tout_ = tch[(k - 2) % 2], tch[(k - 1) % 2]
                    for vh in range(4):
                        op(em, "t", cT, lambda e, vh=vh, ptap=ptap, tin_=tin_: e.matmul(
                            P_ap[:, vh * 128:(vh + 1) * 128], ptap[:, vh * 128:(vh + 1) * 128],
                            tin_[:, vh * 128:(vh + 1) * 128], start=True, stop=True))
                    apm = cT.n
                    em.wait("v", cT, apm)
                    tev = op(em, "v", cV, lambda e, tin_=tin_, tout_=tout_: e.tensor_add(
                        tout_[:], tin_[:], P_ap[:]))
                em.wait("s", cT, sqm)
                if k < 6:
                    op(em, "s", cS, lambda e, k=k: e.activation(
                        pow_u[:, (k - 1) * 512: k * 512], P_sqA[:], AF.Copy))
                powev[k] = op(em, "s", cS, lambda e, k=k: e.activation(
                    pow_t[:, (k - 1) * 512: k * 512], P_sqB[:], AF.Copy))
                if g < 7:
                    if k == 1:
                        emit_fe_S(g + 1, npar, fe["tt_v"])
                        emit_fe_V_A(g + 1, npar)
                    elif k == 3:
                        emit_fe_T_p4a(g + 1, npar, fe["bt_stt"])
                        emit_fe_V_B(g + 1, npar)
                        emit_fe_T_p4b(g + 1, npar, cpar, fe["so2"], 0)
                        emit_fe_S_so2(g + 1, npar, c_, R2["core_add"])
                        emit_fe_V_C(g + 1, npar, c_)
            R2["pow6_ev"] = powev[6]
            em.wait("t", cS, powev[6])
            em.wait("t", cV, tev)
            ptap = pow_t[:, 5 * 512: 6 * 512]
            tin_ = tch[1]
            for vh in range(4):
                op(em, "t", cT, lambda e, vh=vh, ptap=ptap, tin_=tin_: e.matmul(
                    P_ap[:, vh * 128:(vh + 1) * 128], ptap[:, vh * 128:(vh + 1) * 128],
                    tin_[:, vh * 128:(vh + 1) * 128], start=True, stop=True))
            ap6 = cT.n
            em.wait("v", cT, ap6)
            uev = op(em, "v", cV, lambda e, tin_=tin_: e.tensor_add(U_bf[:], tin_[:], P_ap[:]))
            em.wait("v", cS, fe["exps"])
            for vh in range(4):
                op(em, "v", cV, lambda e, vh=vh, par=par: e.tensor_scalar(
                    uw_bf[:, vh * 128:(vh + 1) * 128], U_bf[:, vh * 128:(vh + 1) * 128],
                    wcol32[:, par * 4 + vh: par * 4 + vh + 1], None, ALU.mult))
            uwv = cV.n
            em.wait("t", cV, uev)
            for vh in range(4):
                op(em, "t", cT, lambda e, vh=vh, par=par: e.matmul(
                    P_ap[:, vh * 128:(vh + 1) * 128], attnT[par][:, vh * 128:(vh + 1) * 128],
                    U_bf[:, vh * 128:(vh + 1) * 128], start=True, stop=True))
            o1m = cT.n
            em.wait("t", cS, powev[5])
            em.wait("t", cV, uwv)
            for vh in range(4):
                kk = vh // 2
                op(em, "t", cT, lambda e, vh=vh, kk=kk, g=g: e.matmul(
                    P_sqA[:, vh * 128:(vh + 1) * 128],
                    k_tok[:, (g * 2 + kk) * 128:(g * 2 + kk + 1) * 128],
                    uw_bf[:, vh * 128:(vh + 1) * 128], start=True, stop=True))
            smm = cT.n
            gpar = g % 2
            em.wait("v", cT, o1m)
            em.wait("v", cS, fe["so2"])
            em.wait("v", cst[gpar], core_dma[gpar])
            cadd = op(em, "v", cV, lambda e, gpar=gpar, par=par: e.tensor_add(
                core_bf[gpar][:], core_t1[par][:], P_ap[:]))
            em.wait("g", cV, cadd)
            core_dma[gpar] = dma(em, "g", cst[gpar],
                                 core_s[c_ * 128:(c_ + 1) * 128, g * 512:(g + 1) * 512],
                                 core_bf[gpar][:])
            em.wait("v", cT, smm)
            em.wait("v", cV, vgc)
            sstt = 0
            for vh in range(4):
                hv = g * 4 + vh
                sstt = op(em, "v", cV, lambda e, vh=vh, hv=hv, ping=ping, pong=pong: e.scalar_tensor_tensor(
                    pong[:, hv * 128:(hv + 1) * 128], ping[:, hv * 128:(hv + 1) * 128],
                    gC_sb[:, hv:hv + 1], P_sqA[:, vh * 128:(vh + 1) * 128], ALU.mult, ALU.add))
            R2["core_add"] = cadd
            R2["s_stt"] = sstt
            R2["so2"] = fe["so2"]
            R2["uwv"] = uwv
            if g == 7:
                FEprev = fe
    em.wait("g", cst[0])
    em.wait("g", cst[1])
    tbar = barrier(em)
    em.emit(nc)
    es2.close()
    esP.close()
    if stop_after == "p2":
        emit_dumps(tbar)
        return nc

    es3 = ExitStack()
    E3 = es3.enter_context
    gated = E3(nc.sbuf_tensor("gated", [128, 32 * 2048], BF16))
    cor_t = E3(nc.sbuf_tensor("cor_t", [128, 4096], BF16))
    szr_t = E3(nc.sbuf_tensor("szr_t", [128, 4096], BF16))
    ssg3 = E3(nc.sbuf_tensor("ssg3", [128, 32], F32))
    rsd3 = E3(nc.sbuf_tensor("rsd3", [128, 32], F32))
    gt_t = E3(nc.sbuf_tensor("gt_t", [128, 4096], BF16))
    wo32 = [E3(nc.sbuf_tensor("wo32_0", [128, 32 * 128], F32))] * 2
    wob = [E3(nc.sbuf_tensor(f"wob{i}", [128, 32 * 128], BF16)) for i in range(2)]
    xn3 = [E3(nc.sbuf_tensor(f"xn3_{i}", [128, 128], F32)) for i in range(2)]
    h3 = [E3(nc.sbuf_tensor(f"h3_{i}", [128, 128], F32)) for i in range(2)]
    ps3_tr = [E3(nc.psum_tensor(f"ps3_tr{i}", [128, 512], F32)) for i in range(2)]
    ps3_mm = [E3(nc.psum_tensor(f"ps3_mm{i}", [128, 128], F32)) for i in range(2)]

    em = Em()
    phase_start(em, tbar)
    for i in range(16):
        em.wait("s", cV, cV.n)
        ldc = dma(em, "s", p3l, cor_t[:], core_s[i * 128:(i + 1) * 128, :])
        ldz = dma(em, "s", p3l, szr_t[:], sz_s[i * 128:(i + 1) * 128, :])
        em.wait("v", p3l, ldz)
        vpre = op(em, "v", cV, lambda e: e.tensor_mul(gt_t[:], cor_t[:], cor_t[:]))
        em.wait("v", cV, vpre)
        vr3 = op(em, "v", cV, lambda e: e.tensor_reduce(
            ssg3[:], gt_t[:, :].rearrange("p (h d) -> p h d", d=128), AX, ALU.add))
        em.wait("v", cV, vr3)
        em.wait("s", cV, vr3)
        op(em, "s", cS, lambda e: e.activation(ssg3[:], ssg3[:], AF.Ln, bias=epsc[:, 1:2], scale=epsc[:, 2:3]))
        sr3 = op(em, "s", cS, lambda e: e.activation(rsd3[:], ssg3[:], AF.Exp, scale=-0.5))
        em.wait("v", cS, sr3)
        for hv in range(32):
            op(em, "v", cV, lambda e, hv=hv: e.tensor_scalar(
                gt_t[:, hv * 128:(hv + 1) * 128], cor_t[:, hv * 128:(hv + 1) * 128],
                rsd3[:, hv:hv + 1], None, ALU.mult))
        vg3 = op(em, "v", cV, lambda e: e.tensor_mul(gt_t[:], gt_t[:], szr_t[:]))
        em.wait("t", cV, vg3)
        tr3_evac = {0: 0, 1: 0}
        for d4 in range(8):
            pp = d4 % 2
            em.wait("t", cV, tr3_evac[pp])
            for dd in range(4):
                dvc = d4 * 4 + dd
                t3 = op(em, "t", cT, lambda e, dvc=dvc, dd=dd, pp=pp: e.matmul(
                    ps3_tr[pp][:, dd * 128:(dd + 1) * 128], gt_t[:, dvc * 128:(dvc + 1) * 128],
                    identb[:], start=True, stop=True))
            em.wait("v", cT, t3)
            tr3_evac[pp] = op(em, "v", cV, lambda e, d4=d4, i=i, pp=pp: e.tensor_copy(
                gated[:, :].rearrange("p (d t) -> p d t", t=2048)[:, d4 * 4:(d4 + 1) * 4, i * 128:(i + 1) * 128],
                ps3_tr[pp][:, :].rearrange("p (d t) -> p d t", t=128)))
    wcast3 = {}
    h_dma = {0: 0, 1: 0}
    mm3_ev = {0: 0, 1: 0}
    for jo in range(16):
        par = jo % 2
        if jo >= 1:
            em.wait("y", cV, wcast3[jo - 1])
        sd = slab_load(em, "y", sY, wo32[0], Wout_h, jo * 128, 128, nk=32)
        em.wait("v", sY, sd)
        if jo >= 2:
            em.wait("v", cT, cT.n)
        wcast3[jo] = op(em, "v", cV, lambda e, par=par: e.tensor_copy(wob[par][:], wo32[0][:]))
        em.wait("t", cV, wcast3[jo])
        for i in range(16):
            bk = i % 2
            em.wait("s", cV, mm3_ev[bk])
            xd = dma(em, "s", x3st[bk], xn3[bk][:], xn_s[i * 128:(i + 1) * 128, jo * 128:(jo + 1) * 128])
            em.wait("t", cV, mm3_ev[bk])
            for a in range(32):
                t4 = op(em, "t", cT, lambda e, a=a, i=i, bk=bk, par=par: e.matmul(
                    ps3_mm[bk][:], gated[:, a * 2048 + i * 128: a * 2048 + i * 128 + 128],
                    wob[par][:, a * 128:(a + 1) * 128], start=(a == 0), stop=(a == 31)))
            em.wait("v", cT, t4)
            em.wait("v", x3st[bk], xd)
            em.wait("v", h3st[bk], h_dma[bk])
            mm3_ev[bk] = op(em, "v", cV, lambda e, bk=bk: e.tensor_add(h3[bk][:], xn3[bk][:], ps3_mm[bk][:]))
            em.wait("g", cV, mm3_ev[bk])
            h_dma[bk] = dma(em, "g", h3st[bk], h_s[i * 128:(i + 1) * 128, jo * 128:(jo + 1) * 128], h3[bk][:])
    em.wait("g", h3st[0])
    em.wait("g", h3st[1])
    tbar = barrier(em)
    em.emit(nc)
    es3.close()
    if stop_after == "p3":
        emit_dumps(tbar)
        return nc

    es4 = ExitStack()
    E4 = es4.enter_context
    h4 = [E4(nc.sbuf_tensor(f"h4_{i}", [128, D], F32)) for i in range(2)]
    ss4 = E4(nc.sbuf_tensor("ss4", [128, 1], F32))
    rsd4 = E4(nc.sbuf_tensor("rsd4", [128, 1], F32))
    t4a = E4(nc.sbuf_tensor("t4a", [128, 1], F32))
    h2tok = E4(nc.sbuf_tensor("h2tok", [128, D], BF16))
    h2T = E4(nc.sbuf_tensor("h2T", [128, 16 * 1024], BF16))
    wg32 = E4(nc.sbuf_tensor("wg32_0", [128, 2048], F32))
    wu32s = E4(nc.sbuf_tensor("wu32_0", [128, 2048], F32))
    dump4 = E4(nc.sbuf_tensor("dump4", [128, 2048], F32))
    wgb = [E4(nc.sbuf_tensor(f"wgb{i}", [128, 2048], BF16)) for i in range(2)]
    wub = [E4(nc.sbuf_tensor(f"wub{i}", [128, 2048], BF16)) for i in range(2)]
    tmp4 = E4(nc.sbuf_tensor("tmp4", [128, 512], BF16))
    mid4 = E4(nc.sbuf_tensor("mid4", [128, 40 * 1024], BF16))
    wdb = [E4(nc.sbuf_tensor(f"wdb{i}", [128, 8 * 512], BF16)) for i in range(2)]
    hres4 = [E4(nc.sbuf_tensor(f"hres4_{i}", [128, 512], F32, side="right")) for i in range(2)]
    ao4 = [E4(nc.sbuf_tensor(f"ao4_{i}", [128, 512], F32, side="right")) for i in range(2)]

    em = Em()
    phase_start(em, tbar)
    ao_v7 = {0: 0, 1: 0}
    ao_g7 = {0: 0, 1: 0}
    mmgu = {}
    mmd = {}
    for blk in range(2):
        esp = ExitStack()
        ps4_tr = [esp.enter_context(nc.psum_tensor(f"ps4_tr{blk}{i}", [128, 512], F32)) for i in range(2)]
        ps4_g = [esp.enter_context(nc.psum_tensor(f"ps4_g{blk}{i}", [128, 512], F32)) for i in range(2)]
        ps4_u = [esp.enter_context(nc.psum_tensor(f"ps4_u{blk}{i}", [128, 512], F32)) for i in range(2)]
        tr4_ev = {0: 0, 1: 0}
        for i8 in range(8):
            i = blk * 8 + i8
            par = i8 % 2
            em.wait("s", cV, cV.n)
            em.wait("s", cS, cS.n)
            hd = dma(em, "s", h4l[par], h4[par][:], h_s[i * 128:(i + 1) * 128, :])
            em.wait("s", h4l[par], hd)
            sq4 = op(em, "s", cS, lambda e, par=par: e.activation(
                dump4[:], h4[par][:], AF.Square, accum_out=ss4[:]))
            op(em, "s", cS, lambda e: e.activation(t4a[:], ss4[:], AF.Ln, bias=epsc[:, 1:2], scale=epsc[:, 3:4]))
            sr4 = op(em, "s", cS, lambda e: e.activation(rsd4[:], t4a[:], AF.Exp, scale=-0.5))
            em.wait("v", cS, sr4)
            vht = op(em, "v", cV, lambda e, par=par: e.tensor_scalar(
                h2tok[:], h4[par][:], rsd4[:], None, ALU.mult))
            em.wait("t", cV, vht)
            for c4 in range(4):
                pp = c4 % 2
                em.wait("t", cV, tr4_ev[pp])
                for cc in range(4):
                    c = c4 * 4 + cc
                    t5 = op(em, "t", cT, lambda e, c=c, cc=cc, pp=pp: e.matmul(
                        ps4_tr[pp][:, cc * 128:(cc + 1) * 128], h2tok[:, c * 128:(c + 1) * 128],
                        identb[:], start=True, stop=True))
                em.wait("v", cT, t5)
                for cc in range(4):
                    c = c4 * 4 + cc
                    tr4_ev[pp] = op(em, "v", cV, lambda e, c=c, cc=cc, i8=i8, pp=pp: e.tensor_scalar(
                        h2T[:, c * 1024 + i8 * 128: c * 1024 + i8 * 128 + 128],
                        ps4_tr[pp][:, cc * 128:(cc + 1) * 128],
                        ln2c[:, c:c + 1], None, ALU.mult))
        guev = {0: 0, 1: 0}
        gu_dma = {}
        gu_cast = {}
        em.wait("y", cV, cV.n)
        gu_dma[0] = (slab_load(em, "y", w4l, wg32, Wg_h, 0, 128),
                     slab_load(em, "y", w4l, wu32s, Wu_h, 0, 128))
        for j in range(40):
            par = j % 2
            gj = blk * 40 + j
            em.wait("v", w4l, gu_dma[j][1])
            em.wait("v", cT, mmgu.get(gj - 2, 0))
            op(em, "v", cV, lambda e, par=par: e.tensor_copy(wgb[par][:], wg32[:]))
            gu_cast[j] = op(em, "v", cV, lambda e, par=par: e.tensor_copy(wub[par][:], wu32s[:]))
            if j + 1 < 40:
                em.wait("y", cV, gu_cast[j])
                gu_dma[j + 1] = (slab_load(em, "y", w4l, wg32, Wg_h, (j + 1) * 128, 128),
                                 slab_load(em, "y", w4l, wu32s, Wu_h, (j + 1) * 128, 128))
            em.wait("t", cV, gu_cast[j])
            for n in range(2):
                bk = n
                em.wait("t", cV, guev[bk])
                for a in range(16):
                    op(em, "t", cT, lambda e, a=a, n=n, bk=bk, par=par: e.matmul(
                        ps4_g[bk][:], wgb[par][:, a * 128:(a + 1) * 128],
                        h2T[:, a * 1024 + n * 512: a * 1024 + n * 512 + 512],
                        start=(a == 0), stop=(a == 15)))
                for a in range(16):
                    t6 = op(em, "t", cT, lambda e, a=a, n=n, bk=bk, par=par: e.matmul(
                        ps4_u[bk][:], wub[par][:, a * 128:(a + 1) * 128],
                        h2T[:, a * 1024 + n * 512: a * 1024 + n * 512 + 512],
                        start=(a == 0), stop=(a == 15)))
                em.wait("s", cT, t6)
                em.wait("s", cV, guev[bk])
                s6 = op(em, "s", cS, lambda e, bk=bk: e.activation(tmp4[:], ps4_g[bk][:], AF.Silu))
                em.wait("v", cS, s6)
                guev[bk] = op(em, "v", cV, lambda e, j=j, n=n, bk=bk: e.tensor_mul(
                    mid4[:, j * 1024 + n * 512: j * 1024 + n * 512 + 512], tmp4[:], ps4_u[bk][:]))
            mmgu[gj] = cT.n
        em.wait("t", cS, cS.n)
        if dev and blk == 0:
            em.wait("g", cV, cV.n)
            dma(em, "g", sG, h2t_dbg, h2T[:])
            dma(em, "g", sG, mid_dbg, mid4[:])
            em.wait("g", sG, sG.n)
        esp.close()
        espd = ExitStack()
        ps4_dn = [espd.enter_context(nc.psum_tensor(f"ps4_dn{blk}{i}", [128, 512], F32)) for i in range(8)]
        dn_dma = {}
        dn_cast = {}
        em.wait("y", cV, gu_cast[39])
        dn_dma[0] = (slab_load(em, "y", w4l, wg32, Wd_h, 0, 512, nk=4, a0=0),
                     slab_load(em, "y", w4l, wu32s, Wd_h, 0, 512, nk=4, a0=4))
        for nb in range(4):
            for ag in range(5):
                seq = nb * 5 + ag
                par = seq % 2
                dsq = blk * 20 + seq
                em.wait("v", w4l, dn_dma[seq][1])
                em.wait("v", cT, mmd.get(dsq - 2, 0))
                op(em, "v", cV, lambda e, par=par: e.tensor_copy(wdb[par][:, 0:2048], wg32[:]))
                dn_cast[seq] = op(em, "v", cV, lambda e, par=par: e.tensor_copy(wdb[par][:, 2048:4096], wu32s[:]))
                if seq + 1 < 20:
                    nb2, ag2 = divmod(seq + 1, 5)
                    em.wait("y", cV, dn_cast[seq])
                    dn_dma[seq + 1] = (
                        slab_load(em, "y", w4l, wg32, Wd_h, nb2 * 512, 512, nk=4, a0=ag2 * 8),
                        slab_load(em, "y", w4l, wu32s, Wd_h, nb2 * 512, 512, nk=4, a0=ag2 * 8 + 4))
                em.wait("t", cV, dn_cast[seq])
                for tb in range(8):
                    for aa in range(8):
                        a = ag * 8 + aa
                        op(em, "t", cT, lambda e, a=a, aa=aa, tb=tb, par=par: e.matmul(
                            ps4_dn[tb][:], mid4[:, a * 1024 + tb * 128: a * 1024 + tb * 128 + 128],
                            wdb[par][:, aa * 512:(aa + 1) * 512],
                            start=(a == 0), stop=(a == 39)))
                mmd[dsq] = cT.n
            em.wait("s", cV, ao_v7[0])
            hd_pre = {0: dma(em, "s", hrl[0], hres4[0][:],
                             h_s[(blk * 8) * 128:(blk * 8 + 1) * 128, nb * 512:(nb + 1) * 512])}
            for tb in range(8):
                i = blk * 8 + tb
                pq = tb % 2
                if tb + 1 < 8:
                    pq1 = (tb + 1) % 2
                    em.wait("s", cV, ao_v7[pq1])
                    hd_pre[tb + 1] = dma(em, "s", hrl[pq1], hres4[pq1][:],
                                         h_s[(i + 1) * 128:(i + 2) * 128, nb * 512:(nb + 1) * 512])
                em.wait("v", hrl[pq], hd_pre[tb])
                em.wait("v", cT, cT.n)
                em.wait("v", aos[pq], ao_g7[pq])
                ao_v7[pq] = op(em, "v", cV, lambda e, tb=tb, pq=pq: e.tensor_add(
                    ao4[pq][:], hres4[pq][:], ps4_dn[tb][:]))
                em.wait("g", cV, ao_v7[pq])
                ao_g7[pq] = dma(em, "g", aos[pq], ao_s[i * 128:(i + 1) * 128, nb * 512:(nb + 1) * 512], ao4[pq][:])
                em.wait("t", cV, ao_v7[pq])
        em.wait("t", cV, cV.n)
        espd.close()
    em.wait("g", aos[0])
    em.wait("g", aos[1])
    tbar = barrier(em)
    em.emit(nc)
    es4.close()
    if stop_after == "p4":
        emit_dumps(tbar)
        return nc

    es5 = ExitStack()
    E5 = es5.enter_context
    flnrep = E5(nc.sbuf_tensor("sb_flnrep", [128, IF], F32))
    xn5 = [E5(nc.sbuf_tensor(f"xn5_{i}", [128, D], F32)) for i in range(2)]
    mx5 = E5(nc.sbuf_tensor("mx5", [128, 1], F32))
    qxtok = E5(nc.sbuf_tensor("qxtok", [128, D], BF16))
    qxT = E5(nc.sbuf_tensor("qxT", [128, 16 * 2048], BF16))
    gsl32 = E5(nc.sbuf_tensor("gsl32", [128, 16 * 512], F32))
    gqb = E5(nc.sbuf_tensor("gqb", [128, 16 * 512], BF16))
    uqb = E5(nc.sbuf_tensor("uqb", [128, 16 * 512], BF16))
    hf5 = E5(nc.sbuf_tensor("hf5", [128, 512], F32))
    hl5 = E5(nc.sbuf_tensor("hl5", [128, 512], F32))
    sdump5 = E5(nc.sbuf_tensor("sdump5", [128, 512], F32))
    ssbig = E5(nc.sbuf_tensor("ssbig", [128, 256], F32))
    mxbig = E5(nc.sbuf_tensor("mxbig", [128, 256], F32))
    ssfin = E5(nc.sbuf_tensor("ssfin", [128, 16], F32))
    mxfin = E5(nc.sbuf_tensor("mxfin", [128, 16], F32))
    t5a = E5(nc.sbuf_tensor("t5a", [128, 16], F32))
    rs2a = E5(nc.sbuf_tensor("rs2a", [128, 16], F32))

    em = Em()
    phase_start(em, tbar)
    flnd = dma(em, "y", sY, flnrep[:], flnrep_h)
    ps5_tr = [E5(nc.psum_tensor(f"ps5_tr{i}", [128, 512], F32)) for i in range(2)]
    ps5_g = [E5(nc.psum_tensor(f"ps5_g{i}", [128, 512], F32)) for i in range(2)]
    ps5_u = [E5(nc.psum_tensor(f"ps5_u{i}", [128, 512], F32)) for i in range(2)]
    tr5_ev = {0: 0, 1: 0}
    for i in range(16):
        par = i % 2
        em.wait("y", cV, cV.n)
        xd = dma(em, "y", x5l[par], xn5[par][:], xn_s[i * 128:(i + 1) * 128, :])
        em.wait("v", x5l[par], xd)
        vmx = op(em, "v", cV, lambda e, par=par: e.reduce_max(mx5[:], xn5[par][:], AX, apply_absolute_value=True))
        em.wait("v", cV, vmx)
        vop(em, lambda e, i=i: e.tensor_scalar(mx5[:], mx5[:], 1e-5, 1.0 / 127.0, ALU.max, ALU.mult))
        vop(em, lambda e, i=i: e.tensor_copy(rsx[:, i:i + 1], mx5[:]))
        em.wait("v", cV, cV.n)
        _vr = op(em, "v", cV, lambda e, i=i: e.reciprocal(sx_all[:, i:i + 1], mx5[:]))
        em.wait("v", cV, _vr)
        op(em, "v", cV, lambda e, par=par, i=i: e.tensor_scalar(
            xn5[par][:], xn5[par][:], sx_all[:, i:i + 1], MAGIC, ALU.mult, ALU.add))
        vq5 = op(em, "v", cV, lambda e, par=par: e.tensor_scalar(
            qxtok[:], xn5[par][:], MAGIC, None, ALU.subtract))
        em.wait("t", cV, vq5)
        for c4 in range(4):
            pp = c4 % 2
            em.wait("t", cV, tr5_ev[pp])
            for cc in range(4):
                c = c4 * 4 + cc
                t8 = op(em, "t", cT, lambda e, c=c, cc=cc, pp=pp: e.matmul(
                    ps5_tr[pp][:, cc * 128:(cc + 1) * 128], qxtok[:, c * 128:(c + 1) * 128],
                    identb[:], start=True, stop=True))
            em.wait("v", cT, t8)
            tr5_ev[pp] = op(em, "v", cV, lambda e, c4=c4, i=i, pp=pp: e.tensor_copy(
                qxT[:, :].rearrange("p (c t) -> p c t", t=2048)[:, c4 * 4:(c4 + 1) * 4, i * 128:(i + 1) * 128],
                ps5_tr[pp][:, :].rearrange("p (c t) -> p c t", t=128)))
    em.wait("v", cV, cV.n)
    vop(em, lambda e: e.tensor_scalar(cc1a[:], rsx[:], rswg_c[:], None, ALU.mult))
    vop(em, lambda e: e.tensor_scalar(cc2a[:], rsx[:], rswu_c[:], None, ALU.mult))
    guev5 = {0: 0, 1: 0}
    qdone = 0
    hl_dma = 0
    for jb in range(16):
        em.wait("y", cV, qdone)
        sdg = slab_load(em, "y", g5l, gsl32, Fg_h, jb * 512, 512)
        em.wait("v", g5l, sdg)
        em.wait("v", sY, flnd)
        op(em, "v", cV, lambda e: e.tensor_scalar(gsl32[:], gsl32[:], swg_c[:], MAGIC, ALU.mult, ALU.add))
        op(em, "v", cV, lambda e: e.tensor_scalar(gsl32[:], gsl32[:], MAGIC, None, ALU.subtract))
        gqd = op(em, "v", cV, lambda e: e.tensor_scalar(gqb[:], gsl32[:], -1.0, 1.0, ALU.max, ALU.min))
        em.wait("y", cV, gqd)
        sdu = slab_load(em, "y", g5l, gsl32, Fu_h, jb * 512, 512)
        em.wait("v", g5l, sdu)
        op(em, "v", cV, lambda e: e.tensor_scalar(gsl32[:], gsl32[:], swu_c[:], MAGIC, ALU.mult, ALU.add))
        op(em, "v", cV, lambda e: e.tensor_scalar(gsl32[:], gsl32[:], MAGIC, None, ALU.subtract))
        qdone = op(em, "v", cV, lambda e: e.tensor_scalar(uqb[:], gsl32[:], -1.0, 1.0, ALU.max, ALU.min))
        em.wait("t", cV, qdone)
        for tb in range(16):
            bk = tb % 2
            em.wait("t", cV, guev5[bk])
            for a in range(16):
                op(em, "t", cT, lambda e, a=a, tb=tb, bk=bk: e.matmul(
                    ps5_g[bk][:], qxT[:, a * 2048 + tb * 128: a * 2048 + tb * 128 + 128],
                    gqb[:, a * 512:(a + 1) * 512], start=(a == 0), stop=(a == 15)))
            for a in range(16):
                t9 = op(em, "t", cT, lambda e, a=a, tb=tb, bk=bk: e.matmul(
                    ps5_u[bk][:], qxT[:, a * 2048 + tb * 128: a * 2048 + tb * 128 + 128],
                    uqb[:, a * 512:(a + 1) * 512], start=(a == 0), stop=(a == 15)))
            em.wait("s", cT, t9)
            em.wait("s", cV, guev5[bk])
            s9 = op(em, "s", cS, lambda e, tb=tb, bk=bk: e.activation(
                sdump5[:], ps5_g[bk][:], AF.Silu, scale=cc1a[:, tb:tb + 1]))
            em.wait("v", cS, s9)
            v9 = op(em, "v", cV, lambda e, tb=tb, bk=bk: e.scalar_tensor_tensor(
                hf5[:], ps5_u[bk][:], cc2a[:, tb:tb + 1], sdump5[:], ALU.mult, ALU.mult))
            em.wait("v", cV, v9)
            em.wait("s", cV, v9)
            s10 = op(em, "s", cS, lambda e, jb=jb, tb=tb: e.activation(
                sdump5[:], hf5[:], AF.Square, accum_out=ssbig[:, tb * 16 + jb: tb * 16 + jb + 1]))
            v10 = op(em, "v", cV, lambda e, jb=jb: e.tensor_mul(hl5[:], hf5[:], flnrep[:, jb * 512:(jb + 1) * 512]))
            em.wait("v", cV, v10)
            vr = op(em, "v", cV, lambda e, jb=jb, tb=tb: e.reduce_max(
                mxbig[:, tb * 16 + jb: tb * 16 + jb + 1], hl5[:], AX, apply_absolute_value=True))
            em.wait("v", cV, vr)
            em.wait("g", cV, vr)
            em.wait("g", cS, s10)
            hl_dma = dma(em, "g", hls, hl_s[tb * 128:(tb + 1) * 128, jb * 512:(jb + 1) * 512], hl5[:])
            guev5[bk] = vr
            em.wait("v", hls, hl_dma)
    em.wait("v", cS, cS.n)
    em.wait("v", cV, cV.n)
    op(em, "v", cV, lambda e: e.tensor_reduce(
        ssfin[:], ssbig[:, :].rearrange("p (t j) -> p t j", j=16), AX, ALU.add))
    vsf = op(em, "v", cV, lambda e: e.tensor_reduce(
        mxfin[:], mxbig[:, :].rearrange("p (t j) -> p t j", j=16), AX, ALU.max))
    em.wait("v", cV, vsf)
    em.wait("s", cV, vsf)
    op(em, "s", cS, lambda e: e.activation(t5a[:], ssfin[:], AF.Ln, bias=epsc[:, 1:2], scale=epsc[:, 4:5]))
    srst = op(em, "s", cS, lambda e: e.activation(t5a[:], t5a[:], AF.Exp, scale=-0.5))
    em.wait("v", cS, srst)
    vop(em, lambda e: e.tensor_mul(mxfin[:], mxfin[:], t5a[:]))
    vop(em, lambda e: e.tensor_scalar(rs2a[:], mxfin[:], 1e-5, 1.0 / 127.0, ALU.max, ALU.mult))
    em.wait("v", cV, cV.n)
    _vr2 = op(em, "v", cV, lambda e: e.reciprocal(cs2a[:], rs2a[:]))
    em.wait("v", cV, _vr2)
    vop(em, lambda e: e.tensor_mul(cs2a[:], cs2a[:], t5a[:]))
    vcd = vop(em, lambda e: e.tensor_scalar(cdna[:], rs2a[:], rswd_c[:], None, ALU.mult))
    em.wait("t", cV, vcd)
    tbar = barrier(em)
    em.emit(nc)
    es5.close()
    if stop_after == "p5":
        emit_dumps(tbar)
        return nc

    es6 = ExitStack()
    E6 = es6.enter_context
    hlr = E6(nc.sbuf_tensor("hlr", [128, IF], F32))
    q2tok = E6(nc.sbuf_tensor("q2tok", [128, IF], BF16))
    hqT = E6(nc.sbuf_tensor("hqT", [128, 64 * 1024], BF16))
    wq5 = [E6(nc.sbuf_tensor(f"wq5_{i}", [128, 4 * 512], BF16)) for i in range(2)]
    aox = [E6(nc.sbuf_tensor(f"aox{i}", [128, 512], F32)) for i in range(2)]
    xf = [E6(nc.sbuf_tensor(f"xf{i}", [128, 512], F32)) for i in range(2)]
    out5b = [E6(nc.sbuf_tensor(f"out5b{i}", [128, 512], F32)) for i in range(2)]
    t5fb = [E6(nc.sbuf_tensor(f"t5fb{i}", [128, 512], F32)) for i in range(2)]

    em = Em()
    phase_start(em, tbar)
    o_vo = {0: 0, 1: 0}
    o_go = {0: 0, 1: 0}
    for blk in range(2):
        estr = ExitStack()
        ps6_tr = [estr.enter_context(nc.psum_tensor(f"ps6_tr{blk}{i}", [128, 512], F32)) for i in range(2)]
        tr6_ev = {0: 0, 1: 0}
        for i8 in range(8):
            i = blk * 8 + i8
            em.wait("y", cV, cV.n)
            hd = dma(em, "y", h6l, hlr[:], hl_s[i * 128:(i + 1) * 128, :])
            em.wait("v", h6l, hd)
            op(em, "v", cV, lambda e, i=i: e.tensor_scalar(
                hlr[:], hlr[:], cs2a[:, i:i + 1], MAGIC, ALU.mult, ALU.add))
            vq6 = op(em, "v", cV, lambda e: e.tensor_scalar(q2tok[:], hlr[:], MAGIC, None, ALU.subtract))
            em.wait("t", cV, vq6)
            for c4 in range(16):
                pp = c4 % 2
                em.wait("t", cV, tr6_ev[pp])
                for cc in range(4):
                    c = c4 * 4 + cc
                    ta = op(em, "t", cT, lambda e, c=c, cc=cc, pp=pp: e.matmul(
                        ps6_tr[pp][:, cc * 128:(cc + 1) * 128], q2tok[:, c * 128:(c + 1) * 128],
                        identb[:], start=True, stop=True))
                em.wait("v", cT, ta)
                tr6_ev[pp] = op(em, "v", cV, lambda e, c4=c4, i8=i8, pp=pp: e.tensor_copy(
                    hqT[:, :].rearrange("p (c t) -> p c t", t=1024)[:, c4 * 4:(c4 + 1) * 4, i8 * 128:(i8 + 1) * 128],
                    ps6_tr[pp][:, :].rearrange("p (c t) -> p c t", t=128)))
        em.wait("t", cV, cV.n)
        estr.close()
        espd = ExitStack()
        ps6_dn = [espd.enter_context(nc.psum_tensor(f"ps6_dn{blk}{i}", [128, 512], F32)) for i in range(8)]
        last6p = {0: 0, 1: 0}
        for nb in range(4):
            for ag in range(16):
                par = ag % 2
                em.wait("y", cT, last6p[par])
                srcv = wdq_s.rearrange("(a p) n -> p a n", p=128)
                sdd = dma(em, "y", w6l[par], wq5[par][:, :].rearrange("p (a n) -> p a n", n=512),
                          srcv[:, ag * 4:(ag + 1) * 4, nb * 512:(nb + 1) * 512])
                em.wait("t", w6l[par], sdd)
                for tb in range(8):
                    for aa in range(4):
                        a = ag * 4 + aa
                        tz = op(em, "t", cT, lambda e, a=a, aa=aa, tb=tb, par=par: e.matmul(
                            ps6_dn[tb][:], hqT[:, a * 1024 + tb * 128: a * 1024 + tb * 128 + 128],
                            wq5[par][:, aa * 512:(aa + 1) * 512],
                            start=(a == 0), stop=(a == 63)))
                last6p[par] = tz
            for tb in range(8):
                i = blk * 8 + tb
                pp = tb % 2
                em.wait("s", cV, o_vo[pp])
                ad = dma(em, "s", a6l, aox[pp][:], ao_s[i * 128:(i + 1) * 128, nb * 512:(nb + 1) * 512])
                xd = dma(em, "s", a6l, xf[pp][:], x_h[i * 128:(i + 1) * 128, nb * 512:(nb + 1) * 512])
                em.wait("s", cT, cT.n)
                sa = op(em, "s", cS, lambda e, tb=tb, i=i, pp=pp: e.activation(
                    t5fb[pp][:], ps6_dn[tb][:], AF.Copy, scale=cdna[:, i:i + 1]))
                em.wait("v", cS, sa)
                em.wait("v", a6l, xd)
                em.wait("v", o6s[pp], o_go[pp])
                va1 = op(em, "v", cV, lambda e, pp=pp: e.tensor_add(out5b[pp][:], t5fb[pp][:], aox[pp][:]))
                em.wait("v", cV, va1)
                o_vo[pp] = op(em, "v", cV, lambda e, pp=pp: e.tensor_add(out5b[pp][:], out5b[pp][:], xf[pp][:]))
                em.wait("g", cV, o_vo[pp])
                o_go[pp] = dma(em, "g", o6s[pp], out_h[i * 128:(i + 1) * 128, nb * 512:(nb + 1) * 512], out5b[pp][:])
                em.wait("t", cV, o_vo[pp])
        espd.close()
    em.wait("g", o6s[0])
    em.wait("g", o6s[1])
    if dev:
        for dap, sap in dump_specs:
            dma(em, "g", sG, dap, sap)
    em.wait("g", sG, sG.n)
    barrier(em)
    em.emit(nc)
    es6.close()
    gstk.close()
    return nc


def _host_layout(inputs):
    import ml_dtypes

    def g(k):
        return np.ascontiguousarray(np.asarray(inputs[k], dtype=np.float32))

    Wqkvz = g("L0_attn_W_qkvz").reshape(D, HK, 2 * DK + 2 * 2 * DV)
    out = {
        "Wq": np.ascontiguousarray(Wqkvz[:, :, :DK].reshape(D, HK * DK)),
        "Wk": np.ascontiguousarray(Wqkvz[:, :, DK:2 * DK].reshape(D, HK * DK)),
        "Wv": np.ascontiguousarray(Wqkvz[:, :, 2 * DK:2 * DK + 2 * DV].reshape(D, HV * DV)),
        "Wz": np.ascontiguousarray(Wqkvz[:, :, 2 * DK + 2 * DV:].reshape(D, HV * DV)),
    }
    Wba4 = g("L0_attn_W_ba").reshape(D, HK, 4)
    out["Wba"] = np.ascontiguousarray(np.concatenate(
        [Wba4[:, :, :2].reshape(D, HV), Wba4[:, :, 2:].reshape(D, HV)], axis=1))
    out["convw"] = g("L0_attn_conv_w")
    out["Wout"] = g("L0_attn_W_out")
    out["Wg"] = g("L0_attn_W_gate")
    out["Wu"] = g("L0_attn_W_up")
    out["Wd"] = g("L0_attn_W_down")
    out["Fg"] = np.ascontiguousarray(g("L0_ffn_w_gate").T)
    out["Fu"] = np.ascontiguousarray(g("L0_ffn_w_up").T)
    out["Fd"] = np.ascontiguousarray(g("L0_ffn_w_down").T)
    out["ln1c"] = np.ascontiguousarray(g("L0_attn_ln1_w").reshape(16, 128).T)
    out["ln2c"] = np.ascontiguousarray(g("L0_attn_ln2_w").reshape(16, 128).T)
    out["dtrep"] = np.ascontiguousarray(np.tile(g("L0_attn_dt_bias")[None, :], (128, 1)))
    out["alogrep"] = np.ascontiguousarray(np.tile(g("L0_attn_A_log")[None, :], (128, 1)))
    out["gnwz"] = np.ascontiguousarray(np.tile(g("L0_attn_gated_norm_w")[None, :], (128, 4)))
    out["flnrep"] = np.ascontiguousarray(np.tile(g("L0_ffn_ln_weight")[None, :], (128, 1)))
    out["identf"] = np.eye(128, dtype=np.float32)
    out["identb"] = np.eye(128, dtype=ml_dtypes.bfloat16)
    tt = np.arange(128)
    m0 = np.where(tt[None, :] < tt[:, None], 0.0, NEGC).astype(np.float32)
    m1 = np.where(tt[None, :] > tt[:, None], 0.0, NEGC).astype(np.float32)
    m2 = np.where(tt[None, :] >= tt[:, None], 0.0, NEGC).astype(np.float32)
    out["m0T"] = np.ascontiguousarray(m0.T)
    out["m1T"] = np.ascontiguousarray(m1.T)
    out["m2T"] = np.ascontiguousarray(m2.T)
    out["negup"] = np.ascontiguousarray(
        np.where(tt[None, :] >= tt[:, None], -1.0, 0.0).astype(np.float32))
    ep = np.zeros((128, 8), np.float32)
    ep[:, 0] = 1e-5; ep[:, 1] = 1e-6; ep[:, 2] = 1.0 / 128.0
    ep[:, 3] = 1.0 / D; ep[:, 4] = 1.0 / IF; ep[:, 5] = LNQ
    out["epsc"] = ep
    return out


def kernel(**inputs):
    shared = _host_layout(inputs)
    X = np.ascontiguousarray(np.asarray(inputs["L0_attn_hidden_states"], dtype=np.float32))
    nc = build_program(dev=DEV)
    in_maps = []
    ncore = 8
    for b in range(ncore):
        m = dict(shared)
        m["x"] = np.ascontiguousarray(X[b])
        in_maps.append(m)
    res = run_bass_kernel_spmd(nc, in_maps, core_ids=list(range(ncore)))
    if DEV:
        kernel.dev_results = res.results
    return np.stack([r["out"] for r in res.results], axis=0)


# version 05 (final)
# speedup vs baseline: 1.0380x; 1.0110x over previous
import numpy as np
from contextlib import ExitStack
import concourse.bass as bass
import concourse.mybir as mybir
from concourse.bass_utils import run_bass_kernel_spmd

F32 = mybir.dt.float32
BF16 = mybir.dt.bfloat16
AX = mybir.AxisListType.X
AF = mybir.ActivationFunctionType
ALU = mybir.AluOpType

D = 2048
L = 2048
HK, HV = 16, 32
DK = DV = 128
C = 128
NCH = L // C
IA = 5120
IF = 8192
NEGC = -1e30
MAGIC = float(3 * 2 ** 22)
LNQ = float(np.log(DK ** -0.5))

DEV = False


class Sem:
    def __init__(self, stk, nc, name):
        self.h = stk.enter_context(nc.semaphore(name))
        self.n = 0


class Em:
    ENGS = ("y", "t", "v", "s", "g")

    def __init__(self):
        self.q = {e: [] for e in self.ENGS}
        self.lastw = {e: {} for e in self.ENGS}

    def on(self, eng, fn):
        self.q[eng].append(fn)

    def wait(self, eng, sem, val=None):
        v = sem.n if val is None else val
        if v <= 0:
            return
        if self.lastw[eng].get(id(sem), -1) >= v:
            return
        self.lastw[eng][id(sem)] = v
        self.q[eng].append(lambda e, v=v, s=sem: e.wait_ge(s.h, v))

    def emit(self, nc):
        with nc.Block() as blk:
            for eng, deco in (("y", blk.sync), ("t", blk.tensor), ("v", blk.vector),
                              ("s", blk.scalar), ("g", blk.gpsimd)):
                fns = self.q[eng]

                def mk(fns):
                    def _(e):
                        for f in fns:
                            f(e)
                    return _
                deco(mk(fns))


def build_program(dev=False, stop_after=None):
    nc = bass.Bass("TRN2", target_bir_lowering=False, detect_race_conditions=False)

    def inp(name, shape, dt=F32):
        return nc.dram_tensor(name, shape, dt, kind="ExternalInput").ap()

    x_h = inp("x", [L, D])
    Wq_h = inp("Wq", [D, HK * DK])
    Wk_h = inp("Wk", [D, HK * DK])
    Wv_h = inp("Wv", [D, HV * DV])
    Wz_h = inp("Wz", [D, HV * DV])
    Wba_h = inp("Wba", [D, 64])
    convw_h = inp("convw", [IF, 4])
    Wout_h = inp("Wout", [HV * DV, D])
    Wg_h = inp("Wg", [D, IA])
    Wu_h = inp("Wu", [D, IA])
    Wd_h = inp("Wd", [IA, D])
    Fg_h = inp("Fg", [D, IF])
    Fu_h = inp("Fu", [D, IF])
    Fd_h = inp("Fd", [IF, D])
    ln1c_h = inp("ln1c", [128, 16])
    ln2c_h = inp("ln2c", [128, 16])
    dtrep_h = inp("dtrep", [128, HV])
    alogrep_h = inp("alogrep", [128, HV])
    gnwz_h = inp("gnwz", [128, 512])
    flnrep_h = inp("flnrep", [128, IF])
    identf_h = inp("identf", [128, 128])
    identb_h = inp("identb", [128, 128], BF16)
    m0T_h = inp("m0T", [128, 128])
    m1T_h = inp("m1T", [128, 128])
    m2T_h = inp("m2T", [128, 128])
    negup_h = inp("negup", [128, 128])
    epsc_h = inp("epsc", [128, 8])

    out_h = nc.dram_tensor("out", [L, D], F32, kind="ExternalOutput").ap()

    def scr(name, shape, dt):
        return nc.dram_tensor(name, shape, dt, kind="Internal").ap()

    xn_s = scr("xn_s", [L, D], F32)
    qh_s = scr("qh_s", [HK * DK, L], BF16)
    kh_s = scr("kh_s", [HK * DK, L], BF16)
    v_s = scr("v_s", [L, HV * DV], BF16)
    sz_s = scr("sz_s", [L, HV * DV], BF16)
    core_s = scr("core_s", [L, HV * DV], BF16)
    h_s = scr("h_s", [L, D], F32)
    ao_s = scr("ao_s", [L, D], F32)
    hl_s = scr("hl_s", [L, IF], F32)
    wdq_s = scr("wdq_s", [IF, D], BF16)

    dump_specs = []
    hat_dbg = None
    if dev:
        hat_dbg = nc.dram_tensor("d_hAT", [128, 16 * L], BF16, kind="ExternalOutput").ap()
        mix_dbg = nc.dram_tensor("d_mix0", [128, 3 + L], BF16, kind="ExternalOutput").ap()
        cvs_dbg = nc.dram_tensor("d_cvs0", [128, L], BF16, kind="ExternalOutput").ap()
        st_dbg = nc.dram_tensor("d_stats", [128, 64], F32, kind="ExternalOutput").ap()
        h2t_dbg = nc.dram_tensor("d_h2T", [128, 16 * 1024], BF16, kind="ExternalOutput").ap()
        mid_dbg = nc.dram_tensor("d_mid", [128, 40 * 1024], BF16, kind="ExternalOutput").ap()
    if dev:
        for nm, ap_, shape, dt in [
            ("d_xn", xn_s, [L, D], F32), ("d_qh", qh_s, [HK * DK, L], BF16),
            ("d_kh", kh_s, [HK * DK, L], BF16), ("d_v", v_s, [L, HV * DV], BF16),
            ("d_sz", sz_s, [L, HV * DV], BF16), ("d_core", core_s, [L, HV * DV], BF16),
            ("d_h", h_s, [L, D], F32), ("d_ao", ao_s, [L, D], F32),
            ("d_hl", hl_s, [L, IF], F32), ("d_wdq", wdq_s, [IF, D], BF16),
        ]:
            dump_specs.append((nc.dram_tensor(nm, shape, dt, kind="ExternalOutput").ap(), ap_))

    gstk = ExitStack()
    E = gstk.enter_context

    def emit_dumps(em_prev_bar):
        emd = Em()
        for eng in Em.ENGS:
            emd.wait(eng, bar, em_prev_bar)
        if dev:
            for dap, sap in dump_specs:
                sG.n += 16
                emd.on("g", lambda e, o=dap, i=sap, s=sG: e.dma_start(out=o, in_=i).then_inc(s.h, 16))
        emd.on("g", lambda e, v=sG.n, s=sG: e.wait_ge(s.h, v))
        emd.emit(nc)

    identf = E(nc.sbuf_tensor("sb_identf", [128, 128], F32))
    identb = E(nc.sbuf_tensor("sb_identb", [128, 128], BF16))
    m0T = E(nc.sbuf_tensor("sb_m0T", [128, 128], F32))
    m1T = E(nc.sbuf_tensor("sb_m1T", [128, 128], F32))
    m2T = E(nc.sbuf_tensor("sb_m2T", [128, 128], F32))
    negup = E(nc.sbuf_tensor("sb_negup", [128, 128], F32))
    epsc = E(nc.sbuf_tensor("sb_epsc", [128, 8], F32))
    ln1c = E(nc.sbuf_tensor("sb_ln1c", [128, 16], F32))
    ln2c = E(nc.sbuf_tensor("sb_ln2c", [128, 16], F32))
    dtrep = E(nc.sbuf_tensor("sb_dtrep", [128, HV], F32))
    alogrep = E(nc.sbuf_tensor("sb_alogrep", [128, HV], F32))
    gnwz = E(nc.sbuf_tensor("sb_gnwz", [128, 512], BF16))
    ones_col = E(nc.sbuf_tensor("ones_col", [128, 1], F32))
    ones_colb = E(nc.sbuf_tensor("ones_colb", [128, 1], BF16))
    ones_row = E(nc.sbuf_tensor("ones_row", [1, 128], F32))
    nones_row = E(nc.sbuf_tensor("nones_row", [1, 128], F32))
    expAl = E(nc.sbuf_tensor("expAl", [128, HV], F32))
    mstat = E(nc.sbuf_tensor("mstat", [128, 16], F32))
    qstat = E(nc.sbuf_tensor("qstat", [128, 16], F32))
    rstdS = E(nc.sbuf_tensor("rstdS", [128, 16], F32))
    nmrS = E(nc.sbuf_tensor("nmrS", [128, 16], F32))
    rrS = E(nc.sbuf_tensor("rrS", [128, 16], F32))
    partW = E(nc.sbuf_tensor("partW", [128, 192], F32, side="right"))
    pcol = E(nc.sbuf_tensor("pcol", [128, 4], F32, side="right"))
    esP = ExitStack()
    EP = esP.enter_context
    ba_all = EP(nc.sbuf_tensor("ba_all", [128, NCH * 64], F32, side="right"))
    beta_all = EP(nc.sbuf_tensor("beta_all", [128, NCH * HV], F32, side="right"))
    gam_all = EP(nc.sbuf_tensor("gam_all", [128, NCH * HV], F32, side="right"))
    nbg_all = EP(nc.sbuf_tensor("nbg_all", [128, NCH * HV], F32, side="right"))
    rows_all = EP(nc.sbuf_tensor("rows_all", [64, NCH * 128], F32, side="right"))
    glast = EP(nc.sbuf_tensor("glast", [1, NCH * HV], F32, side="right"))
    swg_c = E(nc.sbuf_tensor("swg_c", [128, 1], F32))
    swu_c = E(nc.sbuf_tensor("swu_c", [128, 1], F32))
    swd_c = E(nc.sbuf_tensor("swd_c", [128, 1], F32))
    rswg_c = E(nc.sbuf_tensor("rswg_c", [128, 1], F32))
    rswu_c = E(nc.sbuf_tensor("rswu_c", [128, 1], F32))
    rswd_c = E(nc.sbuf_tensor("rswd_c", [128, 1], F32))
    rsx = E(nc.sbuf_tensor("rsx", [128, 16], F32))
    sx_all = E(nc.sbuf_tensor("sx_all", [128, 16], F32))
    cs2a = E(nc.sbuf_tensor("cs2a", [128, 16], F32))
    cdna = E(nc.sbuf_tensor("cdna", [128, 16], F32))
    cc1a = E(nc.sbuf_tensor("cc1a", [128, 16], F32))
    cc2a = E(nc.sbuf_tensor("cc2a", [128, 16], F32))
    tA = E(nc.sbuf_tensor("tA", [128, 16], F32))
    tB = E(nc.sbuf_tensor("tB", [128, 16], F32))
    t1x1 = E(nc.sbuf_tensor("t1x1", [1, 4], F32))

    sY = Sem(gstk, nc, "sY")
    sS = Sem(gstk, nc, "sS")
    sG = Sem(gstk, nc, "sG")
    cT = Sem(gstk, nc, "cT")
    cV = Sem(gstk, nc, "cV")
    cS = Sem(gstk, nc, "cS")
    bar = Sem(gstk, nc, "bar")
    dx = [Sem(gstk, nc, f"dx{i}") for i in range(2)]
    gxs = [Sem(gstk, nc, f"gxs{i}") for i in range(2)]
    wst = [Sem(gstk, nc, f"wst{i}") for i in range(2)]
    qst = [Sem(gstk, nc, f"qst{i}") for i in range(2)]
    vst = [Sem(gstk, nc, f"vst{i}") for i in range(2)]
    zst = Sem(gstk, nc, "zst")
    szst = [Sem(gstk, nc, f"szst{i}") for i in range(2)]
    fl_st = [Sem(gstk, nc, f"fl{i}") for i in range(2)]
    fqs = [Sem(gstk, nc, f"fqs{i}") for i in range(2)]
    p2l = [Sem(gstk, nc, f"p2l{i}") for i in range(2)]
    cst = [Sem(gstk, nc, f"cst{i}") for i in range(2)]
    p3l = Sem(gstk, nc, "p3l")
    x3st = [Sem(gstk, nc, f"x3st{i}") for i in range(2)]
    h3st = [Sem(gstk, nc, f"h3st{i}") for i in range(2)]
    h4l = [Sem(gstk, nc, f"h4l{i}") for i in range(2)]
    w4l = Sem(gstk, nc, "w4l")
    hrl = [Sem(gstk, nc, f"hrl{i}") for i in range(2)]
    aos = [Sem(gstk, nc, f"aos{i}") for i in range(2)]
    x5l = [Sem(gstk, nc, f"x5l{i}") for i in range(2)]
    g5l = Sem(gstk, nc, "g5l")
    hls = Sem(gstk, nc, "hls")
    h6l = Sem(gstk, nc, "h6l")
    w6l = [Sem(gstk, nc, f"w6l{i}") for i in range(2)]
    a6l = Sem(gstk, nc, "a6l")
    o6s = [Sem(gstk, nc, f"o6s{i}") for i in range(2)]

    def dma(em, eng, sem, out, in_):
        sem.n += 16
        em.on(eng, lambda e, o=out, i=in_, s=sem: e.dma_start(out=o, in_=i).then_inc(s.h, 16))
        return sem.n

    def op(em, eng, sem, fn):
        sem.n += 1
        em.on(eng, lambda e, f=fn, s=sem: f(e).then_inc(s.h, 1))
        return sem.n

    def vop(em, fn):
        n = op(em, "v", cV, fn)
        em.wait("v", cV, n)
        return n

    def vrecip(em, out_ap, in_ap):
        em.wait("v", cV, cV.n)
        n = op(em, "v", cV, lambda e, o=out_ap, i=in_ap: e.reciprocal(o, i))
        em.wait("v", cV, n)
        return n

    def barrier(em):
        bar.n += 5
        for eng in Em.ENGS:
            em.on(eng, lambda e: e.sem_inc(bar.h, 1))
        return bar.n

    def phase_start(em, tgt):
        for eng in Em.ENGS:
            em.wait(eng, bar, tgt)

    def slab_load(em, eng, sem, slab, Wh, j0, w, nk=16, a0=0, ncols=None):
        ncols = nk * w if ncols is None else ncols
        tot = Wh.shape[0] // 128
        srcv = Wh.rearrange("(a p) n -> p a n", p=128)
        last = None
        nhalf = 2 if nk > 4 else 1
        for half in range(nhalf):
            b0, b1 = half * nk // nhalf, (half + 1) * nk // nhalf
            src = srcv[:, a0 + b0:a0 + b1, j0:j0 + w]
            dst = slab[:, :].rearrange("p (a n) -> p a n", n=w)[:, b0:b1, :]
            last = dma(em, eng, sem, dst, src)
        return last

    esI = ExitStack()
    gnwz32 = esI.enter_context(nc.sbuf_tensor("gnwz32", [128, 512], F32))
    em = Em()
    for dst, src in [(identf, identf_h), (identb, identb_h), (m0T, m0T_h), (m1T, m1T_h),
                     (m2T, m2T_h), (negup, negup_h), (epsc, epsc_h), (ln1c, ln1c_h), (ln2c, ln2c_h),
                     (dtrep, dtrep_h), (alogrep, alogrep_h), (gnwz32, gnwz_h)]:
        dma(em, "y", sY, dst[:], src)
    em.wait("v", sY)
    op(em, "v", cV, lambda e: e.memset(ones_col[:], 1.0))
    op(em, "v", cV, lambda e: e.memset(ones_colb[:], 1.0))
    op(em, "v", cV, lambda e: e.memset(ones_row[:], 1.0))
    op(em, "v", cV, lambda e: e.memset(nones_row[:], -1.0))
    op(em, "v", cV, lambda e: e.tensor_copy(gnwz[:], gnwz32[:]))
    em.wait("s", sY)
    op(em, "s", cS, lambda e: e.activation(expAl[:], alogrep[:], AF.Exp))
    tbar = barrier(em)
    em.emit(nc)
    esI.close()
    if stop_after == "init":
        emit_dumps(tbar)
        return nc

    esA = ExitStack()
    hAT = esA.enter_context(nc.sbuf_tensor("hAT", [128, 16 * L], BF16))
    es0 = ExitStack()
    E0 = es0.enter_context
    xt = [E0(nc.sbuf_tensor(f"xt{i}", [128, D], F32)) for i in range(2)]
    sqdump = E0(nc.sbuf_tensor("sqdump", [128, D], F32))
    xn_sb = [E0(nc.sbuf_tensor(f"xn_sb{i}", [128, D], F32)) for i in range(2)]
    hat_tok = [E0(nc.sbuf_tensor(f"hat_tok{i}", [128, D], BF16)) for i in range(2)]
    ps_tr = [E0(nc.psum_tensor(f"ps_tr{i}", [128, 128], F32)) for i in range(4)]

    em = Em()
    phase_start(em, tbar)
    for i in range(16):
        xd = dma(em, "y", dx[i % 2], xt[i % 2][:], x_h[i * 128:(i + 1) * 128, :])
        em.wait("v", dx[i % 2], xd)
        vred = op(em, "v", cV, lambda e, i=i: e.reduce_sum(mstat[:, i:i + 1], xt[i % 2][:], AX))
        em.wait("s", dx[i % 2], xd)
        sred = op(em, "s", cS, lambda e, i=i: e.activation(
            sqdump[:], xt[i % 2][:], AF.Square, accum_out=qstat[:, i:i + 1]))
        if i < 15:
            em.wait("y", cV, vred)
            em.wait("y", cS, sred)
    em.wait("v", cS, cS.n)
    em.wait("v", cV, cV.n)
    vop(em, lambda e: e.tensor_scalar(mstat[:], mstat[:], 1.0 / D, None, ALU.mult))
    vop(em, lambda e: e.tensor_scalar(qstat[:], qstat[:], 1.0 / D, None, ALU.mult))
    vop(em, lambda e: e.tensor_mul(tA[:], mstat[:], mstat[:]))
    vv = vop(em, lambda e: e.tensor_sub(qstat[:], qstat[:], tA[:]))
    em.wait("s", cV, vv)
    op(em, "s", cS, lambda e: e.activation(tB[:], qstat[:], AF.Ln, bias=epsc[:, 0:1]))
    ss1 = op(em, "s", cS, lambda e: e.activation(rstdS[:], tB[:], AF.Exp, scale=-0.5))
    em.wait("v", cS, ss1)
    vop(em, lambda e: e.tensor_scalar(tA[:], qstat[:], 1e-5, None, ALU.add))
    vrecip(em, tB[:], tA[:])
    vv2 = vop(em, lambda e: e.tensor_mul(tA[:], qstat[:], tB[:]))
    em.wait("s", cV, vv2)
    op(em, "s", cS, lambda e: e.activation(tB[:], tA[:], AF.Ln, bias=epsc[:, 1:2]))
    ss2 = op(em, "s", cS, lambda e: e.activation(rrS[:], tB[:], AF.Exp, scale=-0.5))
    tr_evac = []
    for i in range(16):
        xd = dma(em, "y", dx[i % 2], xt[i % 2][:], x_h[i * 128:(i + 1) * 128, :])
        em.wait("v", dx[i % 2], xd)
        em.wait("v", cS, ss2)
        em.wait("v", gxs[i % 2])
        sxn = op(em, "v", cV, lambda e, i=i: e.tensor_scalar(
            xn_sb[i % 2][:], xt[i % 2][:], mstat[:, i:i + 1], rstdS[:, i:i + 1],
            ALU.subtract, ALU.mult))
        em.wait("g", cV, sxn)
        gx = dma(em, "g", gxs[i % 2], xn_s[i * 128:(i + 1) * 128, :], xn_sb[i % 2][:])
        vht = op(em, "v", cV, lambda e, i=i: e.tensor_scalar(
            hat_tok[i % 2][:], xn_sb[i % 2][:], rrS[:, i:i + 1], None, ALU.mult))
        em.wait("t", cV, vht)
        for c in range(16):
            n = i * 16 + c
            if n >= 4:
                em.wait("t", cV, tr_evac[n - 4])
            op(em, "t", cT, lambda e, c=c, i=i, n=n: e.matmul(
                ps_tr[n % 4][:], hat_tok[i % 2][:, c * 128:(c + 1) * 128], identb[:],
                start=True, stop=True))
            em.wait("v", cT, cT.n)
            ev = op(em, "v", cV, lambda e, c=c, i=i, n=n: e.tensor_scalar(
                hAT[:, c * L + i * 128: c * L + i * 128 + 128], ps_tr[n % 4][:],
                ln1c[:, c:c + 1], None, ALU.mult))
            tr_evac.append(ev)
        if i < 15:
            em.wait("y", cV, cV.n)
    em.wait("g", gxs[0])
    em.wait("g", gxs[1])
    if dev:
        em.wait("g", cV, cV.n)
        em.wait("g", cS, cS.n)
        dma(em, "g", sG, hat_dbg, hAT[:])
        dma(em, "g", sG, st_dbg[:, 0:16], mstat[:])
        dma(em, "g", sG, st_dbg[:, 16:32], qstat[:])
        dma(em, "g", sG, st_dbg[:, 32:48], rstdS[:])
        dma(em, "g", sG, st_dbg[:, 48:64], rrS[:])
        em.wait("g", sG, sG.n)
    tbar = barrier(em)
    em.emit(nc)
    es0.close()
    if stop_after == "p0":
        emit_dumps(tbar)
        return nc

    es1 = ExitStack()
    E0 = es1.enter_context
    wsl32 = [E0(nc.sbuf_tensor(f"wsl32{i}", [128, 2048], F32)) for i in range(2)]
    wslb = [E0(nc.sbuf_tensor(f"wslb{i}", [128, 2048], BF16)) for i in range(2)]
    mix = E0(nc.sbuf_tensor("mix", [128, 3 + L], BF16))
    cacc = [E0(nc.sbuf_tensor(f"cacc{i}", [128, L], BF16)) for i in range(2)]
    convs = E0(nc.sbuf_tensor("convs", [128, L], BF16))
    sqb = E0(nc.sbuf_tensor("sqb", [128, L], BF16))
    cw4 = [E0(nc.sbuf_tensor(f"cw4{i}", [128, 4], F32)) for i in range(2)]
    lnrow = E0(nc.sbuf_tensor("lnrow", [1, 2048], F32))
    rstrow = E0(nc.sbuf_tensor("rstrow", [1, L], F32))
    qk_sb = [E0(nc.sbuf_tensor(f"qk_sb{i}", [128, L], BF16)) for i in range(2)]
    vtr = [E0(nc.sbuf_tensor(f"vtr{i}", [128, 512], BF16)) for i in range(2)]
    ft32 = [E0(nc.sbuf_tensor(f"ft32_{i}", [128, 2048], F32)) for i in range(2)]
    fdump = E0(nc.sbuf_tensor("fdump", [128, 2048], F32))

    ps_mm = [E0(nc.psum_tensor(f"ps_mm{i}", [128, 512], F32)) for i in range(2)]
    ps_s4 = [E0(nc.psum_tensor(f"ps_s4_{i}", [1, 512], F32)) for i in range(4)]
    ps_bc = [E0(nc.psum_tensor(f"ps_bc{i}", [128, 512], F32)) for i in range(2)]

    em = Em()
    phase_start(em, tbar)
    cast_of = {}
    conv_of = {}
    mm_evac = {0: 0, 1: 0}
    bc_evac = {0: 0, 1: 0}
    ss_evac = {}
    vtr_dma = {0: 0, 1: 0}
    qk_dma = {0: 0, 1: 0}
    sred_of = {}

    def emit_wred(idx):
        if idx >= 192:
            return
        if idx < 64:
            Wh, a, sgi = Fg_h, idx // 4, idx % 4
        elif idx < 128:
            Wh, a, sgi = Fu_h, (idx - 64) // 4, idx % 4
        else:
            Wh, a, sgi = Fd_h, idx - 128, 0
        pp = idx % 2
        if idx >= 2:
            em.wait("y", cS, sred_of[idx - 2])
        if idx >= 128:
            fd_ = dma(em, "y", fl_st[pp], ft32[pp][:], Wh[a * 128:(a + 1) * 128, :])
        else:
            fd_ = dma(em, "y", fl_st[pp], ft32[pp][:],
                      Wh[a * 128:(a + 1) * 128, sgi * 2048:(sgi + 1) * 2048])
        em.wait("s", fl_st[pp], fd_)
        sred_of[idx] = op(em, "s", cS, lambda e, pp=pp, idx=idx: e.activation(
            fdump[:], ft32[pp][:], AF.Abs,
            accum_out=partW[:, idx:idx + 1]))

    WRED_ON = True
    for j in range(64):
        par = j % 2
        if WRED_ON:
            for m3 in range(3):
                emit_wred(j * 3 + m3)
        if j < 16:
            Wh, jj = Wq_h, j
        elif j < 32:
            Wh, jj = Wk_h, j - 16
        else:
            Wh, jj = Wv_h, j - 32
        if j >= 2:
            em.wait("y", cV, cast_of[j - 2])
            em.wait("y", cV, conv_of[j - 2])
        sd = slab_load(em, "y", wst[par], wsl32[par], Wh, jj * 128, 128)
        cwd = dma(em, "y", wst[par], cw4[par][:], convw_h[j * 128:(j + 1) * 128, :])
        em.wait("v", wst[par], cwd)
        cast = op(em, "v", cV, lambda e, par=par: e.tensor_copy(wslb[par][:], wsl32[par][:]))
        cast_of[j] = cast
        em.wait("t", cV, cast)
        for n in range(4):
            bk = n % 2
            em.wait("t", cV, mm_evac[bk])
            for a in range(16):
                t_ = op(em, "t", cT, lambda e, par=par, a=a, n=n, bk=bk: e.matmul(
                    ps_mm[bk][:], wslb[par][:, a * 128:(a + 1) * 128],
                    hAT[:, a * L + n * 512: a * L + n * 512 + 512],
                    start=(a == 0), stop=(a == 15)))
            em.wait("v", cT, t_)
            ev = op(em, "v", cV, lambda e, n=n, bk=bk: e.tensor_copy(
                mix[:, 3 + n * 512: 3 + n * 512 + 512], ps_mm[bk][:]))
            mm_evac[bk] = ev
        op(em, "v", cV, lambda e: e.memset(mix[:, 0:3], 0.0))
        em.wait("v", wst[par], cwd)
        em.wait("v", cS, cS.n)
        op(em, "v", cV, lambda e, par=par: e.tensor_scalar(
            cacc[0][:], mix[:, 0:L], cw4[par][:, 0:1], None, ALU.mult))
        for i_ in range(1, 4):
            vcv = op(em, "v", cV, lambda e, i_=i_, par=par: e.scalar_tensor_tensor(
                cacc[i_ % 2][:], mix[:, i_:i_ + L], cw4[par][:, i_:i_ + 1],
                cacc[(i_ + 1) % 2][:], ALU.mult, ALU.add))
        conv_of[j] = vcv
        em.wait("s", cV, vcv)
        ssil = op(em, "s", cS, lambda e: e.activation(convs[:], cacc[1][:], AF.Silu))
        if dev and j == 0:
            em.wait("g", cV, vcv)
            em.wait("g", cS, ssil)
            dma(em, "g", sG, mix_dbg, mix[:])
            dma(em, "g", sG, cvs_dbg, convs[:])
            em.wait("g", sG, sG.n)
        if j < 32:
            em.wait("v", cS, ssil)
            em.wait("v", qst[par], qk_dma[par])
            vsq = op(em, "v", cV, lambda e: e.tensor_mul(sqb[:], convs[:], convs[:]))
            em.wait("t", cV, vsq)
            em.wait("t", cV, ss_evac.get(j - 2, 0))
            for n in range(4):
                tss = op(em, "t", cT, lambda e, n=n: e.matmul(
                    ps_s4[n][:], ones_colb[:], sqb[:, n * 512:(n + 1) * 512],
                    start=True, stop=True))
            em.wait("v", cT, tss)
            for n in range(4):
                sev = op(em, "v", cV, lambda e, n=n: e.tensor_copy(
                    lnrow[0:1, n * 512:(n + 1) * 512], ps_s4[n][:]))
            ss_evac[j] = sev
            em.wait("s", cV, sev)
            op(em, "s", cS, lambda e: e.activation(rstrow[0:1, :], lnrow[0:1, :], AF.Ln, bias=epsc[0:1, 1:2]))
            srs = op(em, "s", cS, lambda e, j=j: e.activation(
                rstrow[0:1, :], rstrow[0:1, :], AF.Exp, scale=-0.5,
                bias=(epsc[0:1, 5:6] if j < 16 else 0.0)))
            em.wait("t", cS, srs)
            vsc = 0
            for n in range(4):
                bk = n % 2
                em.wait("t", cV, bc_evac[bk])
                tbc = op(em, "t", cT, lambda e, n=n, bk=bk: e.matmul(
                    ps_bc[bk][:], ones_row[:], rstrow[0:1, n * 512:(n + 1) * 512],
                    start=True, stop=True))
                em.wait("v", cT, tbc)
                bc_evac[bk] = vsc = op(em, "v", cV, lambda e, n=n, bk=bk, par=par: e.tensor_mul(
                    qk_sb[par][:, n * 512:(n + 1) * 512], convs[:, n * 512:(n + 1) * 512], ps_bc[bk][:]))
            em.wait("g", cV, vsc)
            if j < 16:
                qk_dma[par] = dma(em, "g", qst[par], qh_s[j * 128:(j + 1) * 128, :], qk_sb[par][:])
            else:
                qk_dma[par] = dma(em, "g", qst[par], kh_s[(j - 16) * 128:(j - 15) * 128, :], qk_sb[par][:])
        else:
            jj32 = j - 32
            em.wait("t", cS, ssil)
            for ii0 in range(0, 16, 4):
                pp = (ii0 // 4) % 2
                em.wait("t", cV, bc_evac[pp])
                em.wait("v", vst[pp], vtr_dma[pp])
                for ii in range(ii0, ii0 + 4):
                    ttr = op(em, "t", cT, lambda e, ii=ii, pp=pp: e.matmul(
                        ps_bc[pp][:, (ii % 4) * 128:(ii % 4) * 128 + 128],
                        convs[:, ii * 128:(ii + 1) * 128], identb[:], start=True, stop=True))
                em.wait("v", cT, ttr)
                bc_evac[pp] = op(em, "v", cV, lambda e, pp=pp: e.tensor_copy(vtr[pp][:], ps_bc[pp][:]))
                em.wait("g", cV, bc_evac[pp])
                vtr_dma[pp] = dma(
                    em, "g", vst[pp],
                    v_s.rearrange("(i p) f -> p i f", p=128)[:, ii0:ii0 + 4, jj32 * 128:(jj32 + 1) * 128],
                    vtr[pp][:, :].rearrange("p (i f) -> p i f", f=128))
    tbar = barrier(em)
    em.emit(nc)
    es1.close()
    if stop_after == "p1q":
        emit_dumps(tbar)
        return nc

    esz = ExitStack()
    E0 = esz.enter_context
    zsl32 = E0(nc.sbuf_tensor("zsl32", [128, 16 * 512], F32))
    zslb = E0(nc.sbuf_tensor("zslb", [128, 16 * 512], BF16))
    szg_sb = [E0(nc.sbuf_tensor(f"szg_sb{i}", [128, 512], BF16)) for i in range(2)]
    ps_mm = [E0(nc.psum_tensor(f"ps_zmm{i}", [128, 512], F32)) for i in range(2)]

    em = Em()
    phase_start(em, tbar)
    zcast = 0
    mm_evac = {0: 0, 1: 0}
    sz_dma = {0: 0, 1: 0}
    sz_mul = {0: 0, 1: 0}
    z_ev = {0: 0, 1: 0}
    for zc in range(8):
        em.wait("y", cV, zcast)
        zd = slab_load(em, "y", zst, zsl32, Wz_h, zc * 512, 512)
        em.wait("v", zst, zd)
        zcast = op(em, "v", cV, lambda e: e.tensor_copy(zslb[:], zsl32[:]))
        em.wait("t", cV, zcast)
        for i in range(16):
            bk = i % 2
            em.wait("t", cV, mm_evac[bk])
            em.wait("t", cS, z_ev[bk])
            for a in range(16):
                tz = op(em, "t", cT, lambda e, a=a, i=i, bk=bk: e.matmul(
                    ps_mm[bk][:], hAT[:, a * L + i * 128: a * L + i * 128 + 128],
                    zslb[:, a * 512:(a + 1) * 512], start=(a == 0), stop=(a == 15)))
            em.wait("s", cT, tz)
            em.wait("s", cV, sz_mul[bk])
            em.wait("s", szst[bk], sz_dma[bk])
            z_ev[bk] = op(em, "s", cS, lambda e, bk=bk: e.activation(
                szg_sb[bk][:], ps_mm[bk][:], AF.Silu))
            em.wait("v", cS, z_ev[bk])
            sz_mul[bk] = op(em, "v", cV, lambda e, bk=bk: e.tensor_mul(
                szg_sb[bk][:], szg_sb[bk][:], gnwz[:]))
            em.wait("g", cV, sz_mul[bk])
            sz_dma[bk] = dma(em, "g", szst[bk], sz_s[i * 128:(i + 1) * 128, zc * 512:(zc + 1) * 512], szg_sb[bk][:])
    tbar = barrier(em)
    em.emit(nc)
    esz.close()
    if stop_after == "p1z":
        emit_dumps(tbar)
        return nc

    esb = ExitStack()
    E0 = esb.enter_context
    wba32 = E0(nc.sbuf_tensor("wba32", [128, 1024], F32))
    wbab = E0(nc.sbuf_tensor("wbab", [128, 1024], BF16))
    spt = [E0(nc.sbuf_tensor(f"spt{i}", [128, HV], F32)) for i in range(4)]
    tin = E0(nc.sbuf_tensor("tin", [128, 64], F32))
    ps_mm = [E0(nc.psum_tensor(f"ps_bmm{i}", [128, 512], F32)) for i in range(2)]
    ps_G = E0(nc.psum_tensor("ps_G", [128, 64], F32))
    ps_R = E0(nc.psum_tensor("ps_R", [64, 128], F32))

    em = Em()
    phase_start(em, tbar)
    bad = slab_load(em, "y", sY, wba32, Wba_h, 0, 64)
    em.wait("v", sY, bad)
    bcast = op(em, "v", cV, lambda e: e.tensor_copy(wbab[:], wba32[:]))
    em.wait("t", cV, bcast)
    for i in range(16):
        bk = i % 2
        for a in range(16):
            tb_ = op(em, "t", cT, lambda e, a=a, i=i, bk=bk: e.matmul(
                ps_mm[bk][:, 0:64], hAT[:, a * L + i * 128: a * L + i * 128 + 128],
                wbab[:, a * 64:(a + 1) * 64], start=(a == 0), stop=(a == 15)))
        em.wait("v", cT, tb_)
        op(em, "v", cV, lambda e, i=i, bk=bk: e.tensor_copy(
            ba_all[:, i * 64:(i + 1) * 64], ps_mm[bk][:, 0:64]))
        em.wait("t", cV, cV.n)
    em.wait("s", cV, cV.n)
    for c_ in range(NCH):
        sb1 = op(em, "s", cS, lambda e, c_=c_: e.activation(
            beta_all[:, c_ * HV:(c_ + 1) * HV], ba_all[:, c_ * 64: c_ * 64 + 32], AF.Sigmoid))
        em.wait("v", cS, sb1)
        v1 = vop(em, lambda e, c_=c_: e.tensor_add(
            spt[0][:], ba_all[:, c_ * 64 + 32: c_ * 64 + 64], dtrep[:]))
        em.wait("s", cV, v1)
        op(em, "s", cS, lambda e: e.activation(spt[1][:], spt[0][:], AF.Abs))
        op(em, "s", cS, lambda e: e.activation(spt[1][:], spt[1][:], AF.Exp, scale=-1.0))
        s2_ = op(em, "s", cS, lambda e: e.activation(spt[1][:], spt[1][:], AF.Ln, bias=1.0))
        em.wait("v", cS, s2_)
        vop(em, lambda e: e.tensor_scalar(spt[2][:], spt[0][:], 0.0, None, ALU.max))
        vop(em, lambda e: e.tensor_add(spt[2][:], spt[2][:], spt[1][:]))
        v2 = vop(em, lambda e: e.tensor_mul(spt[2][:], spt[2][:], expAl[:]))
        em.wait("t", cV, v2)
        tg = op(em, "t", cT, lambda e: e.matmul(ps_G[:, 0:HV], negup[:], spt[2][:], start=True, stop=True))
        em.wait("v", cT, tg)
        vg = vop(em, lambda e: e.tensor_copy(tin[:, 32:64], ps_G[:, 0:HV]))
        em.wait("s", cV, vg)
        sg_ = op(em, "s", cS, lambda e, c_=c_: e.activation(
            gam_all[:, c_ * HV:(c_ + 1) * HV], tin[:, 32:64], AF.Exp))
        em.wait("v", cS, sg_)
        vop(em, lambda e, c_=c_: e.tensor_mul(
            spt[3][:], gam_all[:, c_ * HV:(c_ + 1) * HV], beta_all[:, c_ * HV:(c_ + 1) * HV]))
        vop(em, lambda e, c_=c_: e.tensor_scalar(
            nbg_all[:, c_ * HV:(c_ + 1) * HV], spt[3][:], -1.0, None, ALU.mult))
        op(em, "s", cS, lambda e, c_=c_: e.activation(
            spt[1][:], ba_all[:, c_ * 64: c_ * 64 + 32], AF.Abs))
        op(em, "s", cS, lambda e: e.activation(spt[1][:], spt[1][:], AF.Exp, scale=-1.0))
        s3_ = op(em, "s", cS, lambda e: e.activation(spt[1][:], spt[1][:], AF.Ln, bias=1.0))
        em.wait("v", cS, s3_)
        vop(em, lambda e, c_=c_: e.tensor_scalar(
            spt[0][:], ba_all[:, c_ * 64: c_ * 64 + 32], -1.0, 0.0, ALU.mult, ALU.max))
        vop(em, lambda e: e.tensor_add(spt[0][:], spt[0][:], spt[1][:]))
        v3 = vop(em, lambda e: e.tensor_sub(tin[:, 0:32], tin[:, 32:64], spt[0][:]))
        em.wait("t", cV, v3)
        tr_ = op(em, "t", cT, lambda e: e.matmul(ps_R[:], tin[:], identf[:], start=True, stop=True))
        em.wait("v", cT, tr_)
        op(em, "v", cV, lambda e, c_=c_: e.tensor_copy(
            rows_all[0:32, c_ * 128:(c_ + 1) * 128], ps_R[0:32, :]))
        v4 = op(em, "v", cV, lambda e, c_=c_: e.tensor_scalar(
            rows_all[32:64, c_ * 128:(c_ + 1) * 128], ps_R[32:64, :], -1.0, None, ALU.mult))
        em.wait("t", cV, v4)
    em.wait("g", cV, cV.n)
    em.wait("g", cS, cS.n)
    dma(em, "g", sG, glast[:], gam_all[127:128, :])
    em.wait("g", sG, sG.n)
    tbar = barrier(em)
    em.emit(nc)
    esb.close()
    esA.close()
    if stop_after == "p1b":
        emit_dumps(tbar)
        return nc

    es15 = ExitStack()
    E15 = es15.enter_context
    ftp = [E15(nc.sbuf_tensor(f"ftp{i}", [128, 2048], F32)) for i in range(2)]
    fq1 = E15(nc.sbuf_tensor("fq1", [128, 2048], F32))
    fqb = [E15(nc.sbuf_tensor(f"fqb{i}", [128, 2048], BF16)) for i in range(2)]
    ps_w = E15(nc.psum_tensor("ps_w", [1, 4], F32))
    ps_wb = E15(nc.psum_tensor("ps_wb", [128, 4], F32))
    wred15 = {}

    def emit_wred_p15(em, idx):
        if idx < 64:
            Wh, a, sgi = Fg_h, idx // 4, idx % 4
        elif idx < 128:
            Wh, a, sgi = Fu_h, (idx - 64) // 4, idx % 4
        else:
            Wh, a, sgi = Fd_h, idx - 128, 0
        pp = idx % 2
        if idx >= 2:
            em.wait("y", cS, wred15[idx - 2])
        if idx >= 128:
            fd_ = dma(em, "y", fl_st[pp], ftp[pp][:], Wh[a * 128:(a + 1) * 128, :])
        else:
            fd_ = dma(em, "y", fl_st[pp], ftp[pp][:],
                      Wh[a * 128:(a + 1) * 128, sgi * 2048:(sgi + 1) * 2048])
        em.wait("s", fl_st[pp], fd_)
        wred15[idx] = op(em, "s", cS, lambda e, pp=pp, idx=idx: e.activation(
            fq1[:], ftp[pp][:], AF.Abs, accum_out=partW[:, idx:idx + 1]))

    em = Em()
    phase_start(em, tbar)
    if not WRED_ON:
        for idx in range(192):
            emit_wred_p15(em, idx)
    for wi, Wh in enumerate((Fg_h, Fu_h, Fd_h)):
        first = wi * 64
        vp = op(em, "v", cV, lambda e, first=first, wi=wi: e.tensor_reduce(
            pcol[:, wi:wi + 1], partW[:, first:first + 64], AX, ALU.add))
        em.wait("v", cV, vp)
        em.wait("t", cV, vp)
        tw = op(em, "t", cT, lambda e, wi=wi: e.matmul(
            ps_w[0:1, wi:wi + 1], ones_col[:], pcol[:, wi:wi + 1], start=True, stop=True))
        em.wait("v", cT, tw)
        nel = float(Wh.shape[0] * Wh.shape[1])
        vop(em, lambda e, wi=wi, nel=nel: e.tensor_scalar(
            t1x1[0:1, wi:wi + 1], ps_w[0:1, wi:wi + 1], 1.0 / nel, 1e-5, ALU.mult, ALU.max))
        em.wait("v", cV, cV.n)
        vq = op(em, "v", cV, lambda e, wi=wi: e.reciprocal(t1x1[0:1, 3:4], t1x1[0:1, wi:wi + 1]))
        em.wait("v", cV, vq)
        em.wait("t", cV, vq)
        tb1 = op(em, "t", cT, lambda e, wi=wi: e.matmul(
            ps_wb[:, 0:1], ones_row[:], t1x1[0:1, 3:4], start=True, stop=True))
        tb2 = op(em, "t", cT, lambda e, wi=wi: e.matmul(
            ps_wb[:, 1:2], ones_row[:], t1x1[0:1, wi:wi + 1], start=True, stop=True))
        em.wait("v", cT, tb2)
        sw_dst = (swg_c, swu_c, swd_c)[wi]
        rsw_dst = (rswg_c, rswu_c, rswd_c)[wi]
        op(em, "v", cV, lambda e, sw_dst=sw_dst: e.tensor_copy(sw_dst[:], ps_wb[:, 0:1]))
        vb2 = op(em, "v", cV, lambda e, rsw_dst=rsw_dst: e.tensor_copy(rsw_dst[:], ps_wb[:, 1:2]))
        em.wait("v", cV, vb2)
        em.wait("t", cV, vb2)
    fq_dma = {0: 0, 1: 0}
    fq_hist = {}
    for a in range(IF // 128):
        pp = a % 2
        em.wait("y", cV, fq_hist[a - 2] if a >= 2 else 0)
        fd_ = dma(em, "y", fl_st[pp], ftp[pp][:], Fd_h[a * 128:(a + 1) * 128, :])
        em.wait("v", fl_st[pp], fd_)
        em.wait("v", fqs[pp], fq_dma[pp])
        op(em, "v", cV, lambda e, pp=pp: e.tensor_scalar(
            fq1[:], ftp[pp][:], swd_c[:], MAGIC, ALU.mult, ALU.add))
        vq1 = op(em, "v", cV, lambda e, pp=pp: e.tensor_scalar(
            fq1[:], fq1[:], MAGIC, None, ALU.subtract))
        fq_hist[a] = vq1
        vq2 = op(em, "v", cV, lambda e, pp=pp: e.tensor_scalar(
            fqb[pp][:], fq1[:], -1.0, 1.0, ALU.max, ALU.min))
        em.wait("g", cV, vq2)
        fq_dma[pp] = dma(em, "g", fqs[pp], wdq_s[a * 128:(a + 1) * 128, :], fqb[pp][:])
    em.wait("g", fqs[0])
    em.wait("g", fqs[1])
    tbar = barrier(em)
    em.emit(nc)
    es15.close()
    if stop_after == "p15":
        emit_dumps(tbar)
        return nc

    es2 = ExitStack()
    E2 = es2.enter_context
    k_fm = [E2(nc.sbuf_tensor(f"k_fm{i}", [128, 2048], BF16)) for i in range(2)]
    q_fm = [E2(nc.sbuf_tensor(f"q_fm{i}", [128, 2048], BF16)) for i in range(2)]
    v_tok = [E2(nc.sbuf_tensor(f"v_tok{i}", [128, 4096], BF16)) for i in range(2)]
    rows_cur = [E2(nc.sbuf_tensor(f"rows_cur{i}", [1, 8192], F32)) for i in range(2)]
    k_tok = E2(nc.sbuf_tensor("k_tok", [128, 2048], BF16))
    vb_sb = E2(nc.sbuf_tensor("vb_sb", [128, 4096], BF16))
    S32 = [E2(nc.sbuf_tensor(f"S32_{i}", [128, 4096], F32)) for i in range(2)]
    S0b = E2(nc.sbuf_tensor("S0b", [128, 4096], BF16))
    gC_sb = E2(nc.sbuf_tensor("gC_sb", [128, 32], F32))
    wcol32 = E2(nc.sbuf_tensor("wcol32", [128, 8], F32))
    d1b = E2(nc.sbuf_tensor("d1b", [128, 512], BF16))
    d3b = E2(nc.sbuf_tensor("d3b", [128, 512], BF16))
    mtb = [E2(nc.sbuf_tensor(f"mtb{i}", [128, 512], BF16)) for i in range(2)]
    mttb = [E2(nc.sbuf_tensor(f"mttb{i}", [128, 512], BF16)) for i in range(2)]
    attnT = [E2(nc.sbuf_tensor(f"attnT{i}", [128, 512], BF16)) for i in range(2)]
    bt_bf = [E2(nc.sbuf_tensor(f"bt_bf{i}", [128, 512], BF16)) for i in range(2)]
    core_t1 = [E2(nc.sbuf_tensor(f"core_t1_{i}", [128, 512], F32)) for i in range(2)]
    pow_u = E2(nc.sbuf_tensor("pow_u", [128, 5 * 512], BF16))
    pow_t = E2(nc.sbuf_tensor("pow_t", [128, 6 * 512], BF16))
    tch = [E2(nc.sbuf_tensor(f"tch{i}", [128, 512], BF16)) for i in range(2)]
    U_bf = E2(nc.sbuf_tensor("U_bf", [128, 512], BF16))
    uw_bf = E2(nc.sbuf_tensor("uw_bf", [128, 512], BF16))
    core_bf = [E2(nc.sbuf_tensor(f"core_bf{i}", [128, 512], BF16)) for i in range(2)]

    P_kkq = E2(nc.psum_tensor("P_kkq", [128, 512], F32))
    P_d1 = E2(nc.psum_tensor("P_d1", [128, 512], F32))
    P_d3 = E2(nc.psum_tensor("P_d3", [128, 512], F32))
    P_bt = E2(nc.psum_tensor("P_bt", [128, 512], F32))
    P_o2 = E2(nc.psum_tensor("P_o2", [128, 512], F32))
    P_sqA = E2(nc.psum_tensor("P_sqA", [128, 512], F32))
    P_sqB = E2(nc.psum_tensor("P_sqB", [128, 512], F32))
    P_ap = E2(nc.psum_tensor("P_ap", [128, 512], F32))

    em = Em()
    phase_start(em, tbar)
    op(em, "v", cV, lambda e: e.memset(S32[1][:], 0.0))
    szero = op(em, "v", cV, lambda e: e.memset(S0b[:], 0.0))
    em.wait("t", cV, szero)

    def chunk_pre_dma(c_, par):
        em.wait("s", cT, cT.n)
        em.wait("s", cV, cV.n)
        for h in range(16):
            dma(em, "s", p2l[par], k_fm[par][:, h * 128:(h + 1) * 128],
                kh_s[h * 128:(h + 1) * 128, c_ * 128:(c_ + 1) * 128])
        for h in range(16):
            dma(em, "s", p2l[par], q_fm[par][:, h * 128:(h + 1) * 128],
                qh_s[h * 128:(h + 1) * 128, c_ * 128:(c_ + 1) * 128])
        dma(em, "s", p2l[par], v_tok[par][:], v_s[c_ * 128:(c_ + 1) * 128, :])
        dma(em, "s", p2l[par], rows_cur[par][0:1, :].rearrange("o (r t) -> o r t", t=128),
            rows_all[:, c_ * 128:(c_ + 1) * 128])
        return p2l[par].n

    R2 = {"core_add": 0, "s_stt": 0, "pow6_ev": 0, "so2": 0, "uwv": 0}
    FE = {}

    def emit_fe_T_p123(gi, cpar, prev_tt_v, prev_exps):
        kh0, hv0 = gi * 2, gi * 4
        em.wait("t", cV, prev_tt_v)
        for kk in range(2):
            op(em, "t", cT, lambda e, kk=kk, kh0=kh0, cpar=cpar: e.matmul(
                P_kkq[:, kk * 128:(kk + 1) * 128],
                k_fm[cpar][:, (kh0 + kk) * 128:(kh0 + kk + 1) * 128],
                k_fm[cpar][:, (kh0 + kk) * 128:(kh0 + kk + 1) * 128], start=True, stop=True))
        for kk in range(2):
            op(em, "t", cT, lambda e, kk=kk, kh0=kh0, cpar=cpar: e.matmul(
                P_kkq[:, 256 + kk * 128: 256 + (kk + 1) * 128],
                k_fm[cpar][:, (kh0 + kk) * 128:(kh0 + kk + 1) * 128],
                q_fm[cpar][:, (kh0 + kk) * 128:(kh0 + kk + 1) * 128], start=True, stop=True))
        FE[gi] = {"kkq": cT.n}
        em.wait("t", cS, prev_exps)
        for vh in range(4):
            hv = hv0 + vh
            R1 = rows_cur[cpar][0:1, hv * 128:(hv + 1) * 128]
            NG = rows_cur[cpar][0:1, (32 + hv) * 128:(33 + hv) * 128]
            op(em, "t", cT, lambda e, vh=vh, R1=R1: e.matmul(
                P_d1[:, vh * 128:(vh + 1) * 128], R1, ones_row[:], start=True, stop=False))
            op(em, "t", cT, lambda e, vh=vh, NG=NG: e.matmul(
                P_d1[:, vh * 128:(vh + 1) * 128], ones_row[:], NG, start=False, stop=False))
            op(em, "t", cT, lambda e, vh=vh: e.matmul(
                P_d1[:, vh * 128:(vh + 1) * 128], m0T[:], identf[:], start=False, stop=True))
        for vh in range(4):
            hv = hv0 + vh
            NG = rows_cur[cpar][0:1, (32 + hv) * 128:(33 + hv) * 128]
            op(em, "t", cT, lambda e, vh=vh, NG=NG: e.matmul(
                P_d3[:, vh * 128:(vh + 1) * 128], NG, ones_row[:], start=True, stop=False))
            op(em, "t", cT, lambda e, vh=vh, NG=NG: e.matmul(
                P_d3[:, vh * 128:(vh + 1) * 128], nones_row[:], NG, start=False, stop=False))
            op(em, "t", cT, lambda e, vh=vh: e.matmul(
                P_d3[:, vh * 128:(vh + 1) * 128], m2T[:], identf[:], start=False, stop=True))
        FE[gi]["dmm"] = cT.n

    def emit_fe_S(gi, par, prev_tt_v):
        em.wait("s", cT, FE[gi]["dmm"])
        em.wait("s", cV, prev_tt_v)
        em.wait("s", cV, R2["uwv"])
        op(em, "s", cS, lambda e: e.activation(d1b[:], P_d1[:], AF.Exp))
        op(em, "s", cS, lambda e: e.activation(d3b[:], P_d3[:], AF.Exp))
        wbase = par * 4
        FE[gi]["exps"] = op(em, "s", cS, lambda e, wbase=wbase: e.activation(
            wcol32[:, wbase:wbase + 4],
            P_d3[:, :].rearrange("p (v t) -> p v t", t=128)[:, :, 127:128], AF.Exp))

    def emit_fe_V_A(gi, par):
        em.wait("v", cT, FE[gi]["kkq"])
        em.wait("v", cS, FE[gi]["exps"])
        for vh in range(4):
            kk = vh // 2
            op(em, "v", cV, lambda e, vh=vh, kk=kk, par=par: e.tensor_mul(
                mtb[par][:, vh * 128:(vh + 1) * 128], P_kkq[:, kk * 128:(kk + 1) * 128],
                d1b[:, vh * 128:(vh + 1) * 128]))
        FE[gi]["mtb_tt"] = cV.n
        for vh in range(4):
            kk = vh // 2
            op(em, "v", cV, lambda e, vh=vh, kk=kk, par=par: e.tensor_mul(
                attnT[par][:, vh * 128:(vh + 1) * 128],
                P_kkq[:, 256 + kk * 128: 256 + (kk + 1) * 128],
                d3b[:, vh * 128:(vh + 1) * 128]))
        FE[gi]["tt_v"] = cV.n

    def emit_fe_T_p4a(gi, par, prev_bt_stt):
        em.wait("t", cV, FE[gi]["mtb_tt"])
        em.wait("t", cV, prev_bt_stt)
        for vh in range(4):
            op(em, "t", cT, lambda e, vh=vh, par=par: e.matmul(
                P_bt[:, vh * 128:(vh + 1) * 128], mtb[par][:, vh * 128:(vh + 1) * 128],
                identb[:], start=True, stop=True))
        FE[gi]["mtT"] = cT.n

    def emit_fe_V_B(gi, par):
        em.wait("v", cT, FE[gi]["mtT"])
        op(em, "v", cV, lambda e, par=par: e.tensor_copy(mttb[par][:], P_bt[:]))
        FE[gi]["mttb_ev"] = cV.n

    def emit_fe_T_p4b(gi, par, cpar, prev_so2, extra_o2_cv):
        hv0, kh0 = gi * 4, gi * 2
        em.wait("t", cV, FE[gi]["mttb_ev"])
        for vh in range(4):
            hv = hv0 + vh
            kk = vh // 2
            op(em, "t", cT, lambda e, vh=vh, hv=hv, kk=kk, kh0=kh0, cpar=cpar: e.matmul(
                P_bt[:, vh * 128:(vh + 1) * 128],
                k_fm[cpar][:, (kh0 + kk) * 128:(kh0 + kk + 1) * 128],
                S0b[:, hv * 128:(hv + 1) * 128], start=True, stop=True))
        FE[gi]["btm"] = cT.n
        em.wait("t", cS, prev_so2)
        em.wait("t", cV, extra_o2_cv)
        for vh in range(4):
            hv = hv0 + vh
            kk = vh // 2
            op(em, "t", cT, lambda e, vh=vh, hv=hv, kk=kk, kh0=kh0, cpar=cpar: e.matmul(
                P_o2[:, vh * 128:(vh + 1) * 128],
                q_fm[cpar][:, (kh0 + kk) * 128:(kh0 + kk + 1) * 128],
                S0b[:, hv * 128:(hv + 1) * 128], start=True, stop=True))
        FE[gi]["o2m"] = cT.n

    def emit_fe_S_so2(gi, par, c_, prev_core_add_cv):
        hv0 = gi * 4
        em.wait("s", cT, FE[gi]["o2m"])
        em.wait("s", cV, prev_core_add_cv)
        so2 = 0
        for vh in range(4):
            hv = hv0 + vh
            so2 = op(em, "s", cS, lambda e, vh=vh, hv=hv, par=par, c_=c_: e.activation(
                core_t1[par][:, vh * 128:(vh + 1) * 128], P_o2[:, vh * 128:(vh + 1) * 128],
                AF.Copy, scale=gam_all[:, c_ * 32 + hv: c_ * 32 + hv + 1]))
        FE[gi]["so2"] = so2

    def emit_fe_V_C(gi, par, c_):
        hv0 = gi * 4
        em.wait("v", cT, FE[gi]["btm"])
        for vh in range(4):
            hv = hv0 + vh
            op(em, "v", cV, lambda e, vh=vh, hv=hv, c_=c_, par=par: e.scalar_tensor_tensor(
                bt_bf[par][:, vh * 128:(vh + 1) * 128], P_bt[:, vh * 128:(vh + 1) * 128],
                nbg_all[:, c_ * 32 + hv: c_ * 32 + hv + 1], vb_sb[:, hv * 128:(hv + 1) * 128],
                ALU.mult, ALU.add))
        FE[gi]["bt_stt"] = cV.n

    core_dma = {0: 0, 1: 0}
    pre_done = {0: chunk_pre_dma(0, 0)}
    for c_ in range(NCH):
        cpar = c_ % 2
        ping, pong = S32[(c_ + 1) % 2], S32[c_ % 2]
        if c_ + 1 < NCH:
            pre_done[c_ + 1] = chunk_pre_dma(c_ + 1, 1 - cpar)
        em.wait("t", p2l[cpar], pre_done[c_])
        em.wait("v", p2l[cpar], pre_done[c_])
        em.wait("t", cS, R2["so2"])
        tgc = op(em, "t", cT, lambda e, c_=c_: e.matmul(
            P_o2[:, 0:32], ones_row[:], glast[0:1, c_ * 32:(c_ + 1) * 32], start=True, stop=True))
        em.wait("v", cT, tgc)
        vgc = op(em, "v", cV, lambda e: e.tensor_copy(gC_sb[:], P_o2[:, 0:32]))
        em.wait("t", cV, R2["s_stt"])
        for g4 in range(4):
            for hh in range(4):
                h = g4 * 4 + hh
                tkt = op(em, "t", cT, lambda e, h=h, hh=hh, cpar=cpar: e.matmul(
                    P_sqA[:, hh * 128:(hh + 1) * 128], k_fm[cpar][:, h * 128:(h + 1) * 128],
                    identb[:], start=True, stop=True))
            em.wait("v", cT, tkt)
            vkt = op(em, "v", cV, lambda e, g4=g4: e.tensor_copy(
                k_tok[:, g4 * 512:(g4 + 1) * 512], P_sqA[:]))
            em.wait("t", cV, vkt)
        em.wait("v", cT, cT.n)
        for hv in range(32):
            op(em, "v", cV, lambda e, hv=hv, c_=c_, cpar=cpar: e.tensor_scalar(
                vb_sb[:, hv * 128:(hv + 1) * 128], v_tok[cpar][:, hv * 128:(hv + 1) * 128],
                beta_all[:, c_ * 32 + hv: c_ * 32 + hv + 1], None, ALU.mult))
        if c_ > 0:
            for seg in range(8):
                op(em, "v", cV, lambda e, seg=seg, ping=ping: e.tensor_copy(
                    S0b[:, seg * 512:(seg + 1) * 512], ping[:, seg * 512:(seg + 1) * 512]))
        s0_ready = cV.n
        em.wait("t", cV, s0_ready)

        prev = FEprev if c_ > 0 else {"tt_v": 0, "exps": 0, "bt_stt": 0, "so2": 0}
        emit_fe_T_p123(0, cpar, prev["tt_v"], prev["exps"])
        emit_fe_S(0, 0, prev["tt_v"])
        emit_fe_V_A(0, 0)
        emit_fe_T_p4a(0, 0, prev["bt_stt"])
        emit_fe_V_B(0, 0)
        emit_fe_T_p4b(0, 0, cpar, prev["so2"], vgc)
        emit_fe_S_so2(0, 0, c_, R2["core_add"])
        emit_fe_V_C(0, 0, c_)

        for g in range(8):
            par = g % 2
            npar = 1 - par
            fe = FE.pop(g)
            em.wait("t", cV, fe["mttb_ev"])
            em.wait("t", cV, fe["bt_stt"])
            em.wait("t", cV, R2["core_add"])
            for vh in range(4):
                op(em, "t", cT, lambda e, vh=vh, par=par: e.matmul(
                    P_ap[:, vh * 128:(vh + 1) * 128], mttb[par][:, vh * 128:(vh + 1) * 128],
                    bt_bf[par][:, vh * 128:(vh + 1) * 128], start=True, stop=True))
            r0m = cT.n
            if g < 7:
                emit_fe_T_p123(g + 1, cpar, fe["tt_v"], fe["exps"])
            em.wait("v", cT, r0m)
            tev = op(em, "v", cV, lambda e, par=par: e.tensor_sub(tch[0][:], bt_bf[par][:], P_ap[:]))
            powev = {}
            for k in range(1, 7):
                if k == 1:
                    em.wait("t", cV, R2["s_stt"])
                    em.wait("t", cS, R2["pow6_ev"])
                    pu, pt = mtb[par], mttb[par]
                else:
                    em.wait("t", cS, powev[k - 1])
                    pu = pow_u[:, (k - 2) * 512:(k - 1) * 512]
                    pt = pow_t[:, (k - 2) * 512:(k - 1) * 512]
                for vh in range(4):
                    if k < 6:
                        op(em, "t", cT, lambda e, vh=vh, pu=pu, pt=pt: e.matmul(
                            P_sqA[:, vh * 128:(vh + 1) * 128], pt[:, vh * 128:(vh + 1) * 128],
                            pu[:, vh * 128:(vh + 1) * 128], start=True, stop=True))
                    op(em, "t", cT, lambda e, vh=vh, pu=pu, pt=pt: e.matmul(
                        P_sqB[:, vh * 128:(vh + 1) * 128], pu[:, vh * 128:(vh + 1) * 128],
                        pt[:, vh * 128:(vh + 1) * 128], start=True, stop=True))
                sqm = cT.n
                if k >= 2:
                    em.wait("t", cV, tev)
                    ptap = pow_t[:, (k - 2) * 512:(k - 1) * 512]
                    tin_, tout_ = tch[(k - 2) % 2], tch[(k - 1) % 2]
                    for vh in range(4):
                        op(em, "t", cT, lambda e, vh=vh, ptap=ptap, tin_=tin_: e.matmul(
                            P_ap[:, vh * 128:(vh + 1) * 128], ptap[:, vh * 128:(vh + 1) * 128],
                            tin_[:, vh * 128:(vh + 1) * 128], start=True, stop=True))
                    apm = cT.n
                    em.wait("v", cT, apm)
                    tev = op(em, "v", cV, lambda e, tin_=tin_, tout_=tout_: e.tensor_add(
                        tout_[:], tin_[:], P_ap[:]))
                em.wait("s", cT, sqm)
                if k < 6:
                    op(em, "s", cS, lambda e, k=k: e.activation(
                        pow_u[:, (k - 1) * 512: k * 512], P_sqA[:], AF.Copy))
                powev[k] = op(em, "s", cS, lambda e, k=k: e.activation(
                    pow_t[:, (k - 1) * 512: k * 512], P_sqB[:], AF.Copy))
                if g < 7:
                    if k == 1:
                        emit_fe_S(g + 1, npar, fe["tt_v"])
                        emit_fe_V_A(g + 1, npar)
                    elif k == 3:
                        emit_fe_T_p4a(g + 1, npar, fe["bt_stt"])
                        emit_fe_V_B(g + 1, npar)
                        emit_fe_T_p4b(g + 1, npar, cpar, fe["so2"], 0)
                        emit_fe_S_so2(g + 1, npar, c_, R2["core_add"])
                        emit_fe_V_C(g + 1, npar, c_)
            R2["pow6_ev"] = powev[6]
            em.wait("t", cS, powev[6])
            em.wait("t", cV, tev)
            ptap = pow_t[:, 5 * 512: 6 * 512]
            tin_ = tch[1]
            for vh in range(4):
                op(em, "t", cT, lambda e, vh=vh, ptap=ptap, tin_=tin_: e.matmul(
                    P_ap[:, vh * 128:(vh + 1) * 128], ptap[:, vh * 128:(vh + 1) * 128],
                    tin_[:, vh * 128:(vh + 1) * 128], start=True, stop=True))
            ap6 = cT.n
            em.wait("v", cT, ap6)
            uev = op(em, "v", cV, lambda e, tin_=tin_: e.tensor_add(U_bf[:], tin_[:], P_ap[:]))
            em.wait("v", cS, fe["exps"])
            for vh in range(4):
                op(em, "v", cV, lambda e, vh=vh, par=par: e.tensor_scalar(
                    uw_bf[:, vh * 128:(vh + 1) * 128], U_bf[:, vh * 128:(vh + 1) * 128],
                    wcol32[:, par * 4 + vh: par * 4 + vh + 1], None, ALU.mult))
            uwv = cV.n
            em.wait("t", cV, uev)
            for vh in range(4):
                op(em, "t", cT, lambda e, vh=vh, par=par: e.matmul(
                    P_ap[:, vh * 128:(vh + 1) * 128], attnT[par][:, vh * 128:(vh + 1) * 128],
                    U_bf[:, vh * 128:(vh + 1) * 128], start=True, stop=True))
            o1m = cT.n
            em.wait("t", cS, powev[5])
            em.wait("t", cV, uwv)
            for vh in range(4):
                kk = vh // 2
                op(em, "t", cT, lambda e, vh=vh, kk=kk, g=g: e.matmul(
                    P_sqA[:, vh * 128:(vh + 1) * 128],
                    k_tok[:, (g * 2 + kk) * 128:(g * 2 + kk + 1) * 128],
                    uw_bf[:, vh * 128:(vh + 1) * 128], start=True, stop=True))
            smm = cT.n
            gpar = g % 2
            em.wait("v", cT, o1m)
            em.wait("v", cS, fe["so2"])
            em.wait("v", cst[gpar], core_dma[gpar])
            cadd = op(em, "v", cV, lambda e, gpar=gpar, par=par: e.tensor_add(
                core_bf[gpar][:], core_t1[par][:], P_ap[:]))
            em.wait("g", cV, cadd)
            core_dma[gpar] = dma(em, "g", cst[gpar],
                                 core_s[c_ * 128:(c_ + 1) * 128, g * 512:(g + 1) * 512],
                                 core_bf[gpar][:])
            em.wait("v", cT, smm)
            em.wait("v", cV, vgc)
            sstt = 0
            for vh in range(4):
                hv = g * 4 + vh
                sstt = op(em, "v", cV, lambda e, vh=vh, hv=hv, ping=ping, pong=pong: e.scalar_tensor_tensor(
                    pong[:, hv * 128:(hv + 1) * 128], ping[:, hv * 128:(hv + 1) * 128],
                    gC_sb[:, hv:hv + 1], P_sqA[:, vh * 128:(vh + 1) * 128], ALU.mult, ALU.add))
            R2["core_add"] = cadd
            R2["s_stt"] = sstt
            R2["so2"] = fe["so2"]
            R2["uwv"] = uwv
            if g == 7:
                FEprev = fe
    em.wait("g", cst[0])
    em.wait("g", cst[1])
    tbar = barrier(em)
    em.emit(nc)
    es2.close()
    esP.close()
    if stop_after == "p2":
        emit_dumps(tbar)
        return nc

    es3 = ExitStack()
    E3 = es3.enter_context
    gated = E3(nc.sbuf_tensor("gated", [128, 32 * 2048], BF16))
    cor_t = E3(nc.sbuf_tensor("cor_t", [128, 4096], BF16))
    szr_t = E3(nc.sbuf_tensor("szr_t", [128, 4096], BF16))
    ssg3 = E3(nc.sbuf_tensor("ssg3", [128, 32], F32))
    rsd3 = E3(nc.sbuf_tensor("rsd3", [128, 32], F32))
    gt_t = E3(nc.sbuf_tensor("gt_t", [128, 4096], BF16))
    wo32 = [E3(nc.sbuf_tensor("wo32_0", [128, 32 * 128], F32))] * 2
    wob = [E3(nc.sbuf_tensor(f"wob{i}", [128, 32 * 128], BF16)) for i in range(2)]
    xn3 = [E3(nc.sbuf_tensor(f"xn3_{i}", [128, 128], F32)) for i in range(2)]
    h3 = [E3(nc.sbuf_tensor(f"h3_{i}", [128, 128], F32)) for i in range(2)]
    ps3_tr = [E3(nc.psum_tensor(f"ps3_tr{i}", [128, 512], F32)) for i in range(2)]
    ps3_mm = [E3(nc.psum_tensor(f"ps3_mm{i}", [128, 128], F32)) for i in range(2)]

    em = Em()
    phase_start(em, tbar)
    for i in range(16):
        em.wait("s", cV, cV.n)
        ldc = dma(em, "s", p3l, cor_t[:], core_s[i * 128:(i + 1) * 128, :])
        ldz = dma(em, "s", p3l, szr_t[:], sz_s[i * 128:(i + 1) * 128, :])
        em.wait("v", p3l, ldz)
        vpre = op(em, "v", cV, lambda e: e.tensor_mul(gt_t[:], cor_t[:], cor_t[:]))
        em.wait("v", cV, vpre)
        vr3 = op(em, "v", cV, lambda e: e.tensor_reduce(
            ssg3[:], gt_t[:, :].rearrange("p (h d) -> p h d", d=128), AX, ALU.add))
        em.wait("v", cV, vr3)
        em.wait("s", cV, vr3)
        op(em, "s", cS, lambda e: e.activation(ssg3[:], ssg3[:], AF.Ln, bias=epsc[:, 1:2], scale=epsc[:, 2:3]))
        sr3 = op(em, "s", cS, lambda e: e.activation(rsd3[:], ssg3[:], AF.Exp, scale=-0.5))
        em.wait("v", cS, sr3)
        for hv in range(32):
            op(em, "v", cV, lambda e, hv=hv: e.tensor_scalar(
                gt_t[:, hv * 128:(hv + 1) * 128], cor_t[:, hv * 128:(hv + 1) * 128],
                rsd3[:, hv:hv + 1], None, ALU.mult))
        vg3 = op(em, "v", cV, lambda e: e.tensor_mul(gt_t[:], gt_t[:], szr_t[:]))
        em.wait("t", cV, vg3)
        tr3_evac = {0: 0, 1: 0}
        for d4 in range(8):
            pp = d4 % 2
            em.wait("t", cV, tr3_evac[pp])
            for dd in range(4):
                dvc = d4 * 4 + dd
                t3 = op(em, "t", cT, lambda e, dvc=dvc, dd=dd, pp=pp: e.matmul(
                    ps3_tr[pp][:, dd * 128:(dd + 1) * 128], gt_t[:, dvc * 128:(dvc + 1) * 128],
                    identb[:], start=True, stop=True))
            em.wait("v", cT, t3)
            tr3_evac[pp] = op(em, "v", cV, lambda e, d4=d4, i=i, pp=pp: e.tensor_copy(
                gated[:, :].rearrange("p (d t) -> p d t", t=2048)[:, d4 * 4:(d4 + 1) * 4, i * 128:(i + 1) * 128],
                ps3_tr[pp][:, :].rearrange("p (d t) -> p d t", t=128)))
    wcast3 = {}
    h_dma = {0: 0, 1: 0}
    mm3_ev = {0: 0, 1: 0}
    for jo in range(16):
        par = jo % 2
        if jo >= 1:
            em.wait("y", cV, wcast3[jo - 1])
        sd = slab_load(em, "y", sY, wo32[0], Wout_h, jo * 128, 128, nk=32)
        em.wait("v", sY, sd)
        if jo >= 2:
            em.wait("v", cT, cT.n)
        wcast3[jo] = op(em, "v", cV, lambda e, par=par: e.tensor_copy(wob[par][:], wo32[0][:]))
        em.wait("t", cV, wcast3[jo])
        for i in range(16):
            bk = i % 2
            em.wait("s", cV, mm3_ev[bk])
            xd = dma(em, "s", x3st[bk], xn3[bk][:], xn_s[i * 128:(i + 1) * 128, jo * 128:(jo + 1) * 128])
            em.wait("t", cV, mm3_ev[bk])
            for a in range(32):
                t4 = op(em, "t", cT, lambda e, a=a, i=i, bk=bk, par=par: e.matmul(
                    ps3_mm[bk][:], gated[:, a * 2048 + i * 128: a * 2048 + i * 128 + 128],
                    wob[par][:, a * 128:(a + 1) * 128], start=(a == 0), stop=(a == 31)))
            em.wait("v", cT, t4)
            em.wait("v", x3st[bk], xd)
            em.wait("v", h3st[bk], h_dma[bk])
            mm3_ev[bk] = op(em, "v", cV, lambda e, bk=bk: e.tensor_add(h3[bk][:], xn3[bk][:], ps3_mm[bk][:]))
            em.wait("g", cV, mm3_ev[bk])
            h_dma[bk] = dma(em, "g", h3st[bk], h_s[i * 128:(i + 1) * 128, jo * 128:(jo + 1) * 128], h3[bk][:])
    em.wait("g", h3st[0])
    em.wait("g", h3st[1])
    tbar = barrier(em)
    em.emit(nc)
    es3.close()
    if stop_after == "p3":
        emit_dumps(tbar)
        return nc

    es4 = ExitStack()
    E4 = es4.enter_context
    h4 = [E4(nc.sbuf_tensor(f"h4_{i}", [128, D], F32)) for i in range(2)]
    ss4 = E4(nc.sbuf_tensor("ss4", [128, 1], F32))
    rsd4 = E4(nc.sbuf_tensor("rsd4", [128, 1], F32))
    t4a = E4(nc.sbuf_tensor("t4a", [128, 1], F32))
    h2tok = E4(nc.sbuf_tensor("h2tok", [128, D], BF16))
    h2T = E4(nc.sbuf_tensor("h2T", [128, 16 * 1024], BF16))
    wg32 = E4(nc.sbuf_tensor("wg32_0", [128, 2048], F32))
    wu32s = E4(nc.sbuf_tensor("wu32_0", [128, 2048], F32))
    dump4 = E4(nc.sbuf_tensor("dump4", [128, 2048], F32))
    wgb = [E4(nc.sbuf_tensor(f"wgb{i}", [128, 2048], BF16)) for i in range(2)]
    wub = [E4(nc.sbuf_tensor(f"wub{i}", [128, 2048], BF16)) for i in range(2)]
    tmp4 = E4(nc.sbuf_tensor("tmp4", [128, 512], BF16))
    mid4 = E4(nc.sbuf_tensor("mid4", [128, 40 * 1024], BF16))
    wdb = [E4(nc.sbuf_tensor(f"wdb{i}", [128, 8 * 512], BF16)) for i in range(2)]
    hres4 = [E4(nc.sbuf_tensor(f"hres4_{i}", [128, 512], F32, side="right")) for i in range(2)]
    ao4 = [E4(nc.sbuf_tensor(f"ao4_{i}", [128, 512], F32, side="right")) for i in range(2)]

    em = Em()
    phase_start(em, tbar)
    ao_v7 = {0: 0, 1: 0}
    ao_g7 = {0: 0, 1: 0}
    mmgu = {}
    mmd = {}
    vht_of = {}
    for blk in range(2):
        esp = ExitStack()
        ps4_tr = [esp.enter_context(nc.psum_tensor(f"ps4_tr{blk}{i}", [128, 512], F32)) for i in range(2)]
        ps4_g = [esp.enter_context(nc.psum_tensor(f"ps4_g{blk}{i}", [128, 512], F32)) for i in range(2)]
        ps4_u = [esp.enter_context(nc.psum_tensor(f"ps4_u{blk}{i}", [128, 512], F32)) for i in range(2)]
        tr4_ev = {0: 0, 1: 0}
        for i8 in range(8):
            i = blk * 8 + i8
            par = i8 % 2
            em.wait("s", cV, vht_of.get(blk * 8 + i8 - 2, 0))
            hd = dma(em, "s", h4l[par], h4[par][:], h_s[i * 128:(i + 1) * 128, :])
            em.wait("s", cV, vht_of.get(blk * 8 + i8 - 1, 0))
            em.wait("s", h4l[par], hd)
            sq4 = op(em, "s", cS, lambda e, par=par: e.activation(
                dump4[:], h4[par][:], AF.Square, accum_out=ss4[:]))
            op(em, "s", cS, lambda e: e.activation(t4a[:], ss4[:], AF.Ln, bias=epsc[:, 1:2], scale=epsc[:, 3:4]))
            sr4 = op(em, "s", cS, lambda e: e.activation(rsd4[:], t4a[:], AF.Exp, scale=-0.5))
            em.wait("v", cS, sr4)
            vht = vht_of[blk * 8 + i8] = op(em, "v", cV, lambda e, par=par: e.tensor_scalar(
                h2tok[:], h4[par][:], rsd4[:], None, ALU.mult))
            em.wait("t", cV, vht)
            for c4 in range(4):
                pp = c4 % 2
                em.wait("t", cV, tr4_ev[pp])
                for cc in range(4):
                    c = c4 * 4 + cc
                    t5 = op(em, "t", cT, lambda e, c=c, cc=cc, pp=pp: e.matmul(
                        ps4_tr[pp][:, cc * 128:(cc + 1) * 128], h2tok[:, c * 128:(c + 1) * 128],
                        identb[:], start=True, stop=True))
                em.wait("v", cT, t5)
                for cc in range(4):
                    c = c4 * 4 + cc
                    tr4_ev[pp] = op(em, "v", cV, lambda e, c=c, cc=cc, i8=i8, pp=pp: e.tensor_scalar(
                        h2T[:, c * 1024 + i8 * 128: c * 1024 + i8 * 128 + 128],
                        ps4_tr[pp][:, cc * 128:(cc + 1) * 128],
                        ln2c[:, c:c + 1], None, ALU.mult))
        guev = {0: 0, 1: 0}
        gu_dma = {}
        gu_cast = {}
        em.wait("y", cV, cV.n)
        gu_dma[0] = (slab_load(em, "y", w4l, wg32, Wg_h, 0, 128),
                     slab_load(em, "y", w4l, wu32s, Wu_h, 0, 128))
        for j in range(40):
            par = j % 2
            gj = blk * 40 + j
            em.wait("v", w4l, gu_dma[j][1])
            em.wait("v", cT, mmgu.get(gj - 2, 0))
            op(em, "v", cV, lambda e, par=par: e.tensor_copy(wgb[par][:], wg32[:]))
            gu_cast[j] = op(em, "v", cV, lambda e, par=par: e.tensor_copy(wub[par][:], wu32s[:]))
            if j + 1 < 40:
                em.wait("y", cV, gu_cast[j])
                gu_dma[j + 1] = (slab_load(em, "y", w4l, wg32, Wg_h, (j + 1) * 128, 128),
                                 slab_load(em, "y", w4l, wu32s, Wu_h, (j + 1) * 128, 128))
            em.wait("t", cV, gu_cast[j])
            for n in range(2):
                bk = n
                em.wait("t", cV, guev[bk])
                for a in range(16):
                    op(em, "t", cT, lambda e, a=a, n=n, bk=bk, par=par: e.matmul(
                        ps4_g[bk][:], wgb[par][:, a * 128:(a + 1) * 128],
                        h2T[:, a * 1024 + n * 512: a * 1024 + n * 512 + 512],
                        start=(a == 0), stop=(a == 15)))
                for a in range(16):
                    t6 = op(em, "t", cT, lambda e, a=a, n=n, bk=bk, par=par: e.matmul(
                        ps4_u[bk][:], wub[par][:, a * 128:(a + 1) * 128],
                        h2T[:, a * 1024 + n * 512: a * 1024 + n * 512 + 512],
                        start=(a == 0), stop=(a == 15)))
                em.wait("s", cT, t6)
                em.wait("s", cV, guev[bk])
                s6 = op(em, "s", cS, lambda e, bk=bk: e.activation(tmp4[:], ps4_g[bk][:], AF.Silu))
                em.wait("v", cS, s6)
                guev[bk] = op(em, "v", cV, lambda e, j=j, n=n, bk=bk: e.tensor_mul(
                    mid4[:, j * 1024 + n * 512: j * 1024 + n * 512 + 512], tmp4[:], ps4_u[bk][:]))
            mmgu[gj] = cT.n
        em.wait("t", cS, cS.n)
        if dev and blk == 0:
            em.wait("g", cV, cV.n)
            dma(em, "g", sG, h2t_dbg, h2T[:])
            dma(em, "g", sG, mid_dbg, mid4[:])
            em.wait("g", sG, sG.n)
        esp.close()
        espd = ExitStack()
        ps4_dn = [espd.enter_context(nc.psum_tensor(f"ps4_dn{blk}{i}", [128, 512], F32)) for i in range(8)]
        dn_dma = {}
        dn_cast = {}
        em.wait("y", cV, gu_cast[39])
        dn_dma[0] = (slab_load(em, "y", w4l, wg32, Wd_h, 0, 512, nk=4, a0=0),
                     slab_load(em, "y", w4l, wu32s, Wd_h, 0, 512, nk=4, a0=4))
        for nb in range(4):
            for ag in range(5):
                seq = nb * 5 + ag
                par = seq % 2
                dsq = blk * 20 + seq
                em.wait("v", w4l, dn_dma[seq][1])
                em.wait("v", cT, mmd.get(dsq - 2, 0))
                op(em, "v", cV, lambda e, par=par: e.tensor_copy(wdb[par][:, 0:2048], wg32[:]))
                dn_cast[seq] = op(em, "v", cV, lambda e, par=par: e.tensor_copy(wdb[par][:, 2048:4096], wu32s[:]))
                if seq + 1 < 20:
                    nb2, ag2 = divmod(seq + 1, 5)
                    em.wait("y", cV, dn_cast[seq])
                    dn_dma[seq + 1] = (
                        slab_load(em, "y", w4l, wg32, Wd_h, nb2 * 512, 512, nk=4, a0=ag2 * 8),
                        slab_load(em, "y", w4l, wu32s, Wd_h, nb2 * 512, 512, nk=4, a0=ag2 * 8 + 4))
                em.wait("t", cV, dn_cast[seq])
                for tb in range(8):
                    for aa in range(8):
                        a = ag * 8 + aa
                        op(em, "t", cT, lambda e, a=a, aa=aa, tb=tb, par=par: e.matmul(
                            ps4_dn[tb][:], mid4[:, a * 1024 + tb * 128: a * 1024 + tb * 128 + 128],
                            wdb[par][:, aa * 512:(aa + 1) * 512],
                            start=(a == 0), stop=(a == 39)))
                mmd[dsq] = cT.n
            em.wait("s", cV, ao_v7[0])
            hd_pre = {0: dma(em, "s", hrl[0], hres4[0][:],
                             h_s[(blk * 8) * 128:(blk * 8 + 1) * 128, nb * 512:(nb + 1) * 512])}
            for tb in range(8):
                i = blk * 8 + tb
                pq = tb % 2
                if tb + 1 < 8:
                    pq1 = (tb + 1) % 2
                    em.wait("s", cV, ao_v7[pq1])
                    hd_pre[tb + 1] = dma(em, "s", hrl[pq1], hres4[pq1][:],
                                         h_s[(i + 1) * 128:(i + 2) * 128, nb * 512:(nb + 1) * 512])
                em.wait("v", hrl[pq], hd_pre[tb])
                em.wait("v", cT, cT.n)
                em.wait("v", aos[pq], ao_g7[pq])
                ao_v7[pq] = op(em, "v", cV, lambda e, tb=tb, pq=pq: e.tensor_add(
                    ao4[pq][:], hres4[pq][:], ps4_dn[tb][:]))
                em.wait("g", cV, ao_v7[pq])
                ao_g7[pq] = dma(em, "g", aos[pq], ao_s[i * 128:(i + 1) * 128, nb * 512:(nb + 1) * 512], ao4[pq][:])
                em.wait("t", cV, ao_v7[pq])
        em.wait("t", cV, cV.n)
        espd.close()
    em.wait("g", aos[0])
    em.wait("g", aos[1])
    tbar = barrier(em)
    em.emit(nc)
    es4.close()
    if stop_after == "p4":
        emit_dumps(tbar)
        return nc

    es5 = ExitStack()
    E5 = es5.enter_context
    flnrep = E5(nc.sbuf_tensor("sb_flnrep", [128, IF], F32))
    xn5 = [E5(nc.sbuf_tensor(f"xn5_{i}", [128, D], F32)) for i in range(2)]
    mx5 = E5(nc.sbuf_tensor("mx5", [128, 1], F32))
    qxtok = E5(nc.sbuf_tensor("qxtok", [128, D], BF16))
    qxT = E5(nc.sbuf_tensor("qxT", [128, 16 * 2048], BF16))
    gsl32 = E5(nc.sbuf_tensor("gsl32", [128, 16 * 512], F32))
    gqb = E5(nc.sbuf_tensor("gqb", [128, 16 * 512], BF16))
    uqb = E5(nc.sbuf_tensor("uqb", [128, 16 * 512], BF16))
    hf5 = E5(nc.sbuf_tensor("hf5", [128, 512], F32))
    hl5 = E5(nc.sbuf_tensor("hl5", [128, 512], F32))
    sdump5 = E5(nc.sbuf_tensor("sdump5", [128, 512], F32))
    ssbig = E5(nc.sbuf_tensor("ssbig", [128, 256], F32))
    mxbig = E5(nc.sbuf_tensor("mxbig", [128, 256], F32))
    ssfin = E5(nc.sbuf_tensor("ssfin", [128, 16], F32))
    mxfin = E5(nc.sbuf_tensor("mxfin", [128, 16], F32))
    t5a = E5(nc.sbuf_tensor("t5a", [128, 16], F32))
    rs2a = E5(nc.sbuf_tensor("rs2a", [128, 16], F32))

    em = Em()
    phase_start(em, tbar)
    flnd = dma(em, "y", sY, flnrep[:], flnrep_h)
    ps5_tr = [E5(nc.psum_tensor(f"ps5_tr{i}", [128, 512], F32)) for i in range(2)]
    ps5_g = [E5(nc.psum_tensor(f"ps5_g{i}", [128, 512], F32)) for i in range(2)]
    ps5_u = [E5(nc.psum_tensor(f"ps5_u{i}", [128, 512], F32)) for i in range(2)]
    tr5_ev = {0: 0, 1: 0}
    for i in range(16):
        par = i % 2
        em.wait("y", cV, cV.n)
        xd = dma(em, "y", x5l[par], xn5[par][:], xn_s[i * 128:(i + 1) * 128, :])
        em.wait("v", x5l[par], xd)
        vmx = op(em, "v", cV, lambda e, par=par: e.reduce_max(mx5[:], xn5[par][:], AX, apply_absolute_value=True))
        em.wait("v", cV, vmx)
        vop(em, lambda e, i=i: e.tensor_scalar(mx5[:], mx5[:], 1e-5, 1.0 / 127.0, ALU.max, ALU.mult))
        vop(em, lambda e, i=i: e.tensor_copy(rsx[:, i:i + 1], mx5[:]))
        em.wait("v", cV, cV.n)
        _vr = op(em, "v", cV, lambda e, i=i: e.reciprocal(sx_all[:, i:i + 1], mx5[:]))
        em.wait("v", cV, _vr)
        op(em, "v", cV, lambda e, par=par, i=i: e.tensor_scalar(
            xn5[par][:], xn5[par][:], sx_all[:, i:i + 1], MAGIC, ALU.mult, ALU.add))
        vq5 = op(em, "v", cV, lambda e, par=par: e.tensor_scalar(
            qxtok[:], xn5[par][:], MAGIC, None, ALU.subtract))
        em.wait("t", cV, vq5)
        for c4 in range(4):
            pp = c4 % 2
            em.wait("t", cV, tr5_ev[pp])
            for cc in range(4):
                c = c4 * 4 + cc
                t8 = op(em, "t", cT, lambda e, c=c, cc=cc, pp=pp: e.matmul(
                    ps5_tr[pp][:, cc * 128:(cc + 1) * 128], qxtok[:, c * 128:(c + 1) * 128],
                    identb[:], start=True, stop=True))
            em.wait("v", cT, t8)
            tr5_ev[pp] = op(em, "v", cV, lambda e, c4=c4, i=i, pp=pp: e.tensor_copy(
                qxT[:, :].rearrange("p (c t) -> p c t", t=2048)[:, c4 * 4:(c4 + 1) * 4, i * 128:(i + 1) * 128],
                ps5_tr[pp][:, :].rearrange("p (c t) -> p c t", t=128)))
    em.wait("v", cV, cV.n)
    vop(em, lambda e: e.tensor_scalar(cc1a[:], rsx[:], rswg_c[:], None, ALU.mult))
    vop(em, lambda e: e.tensor_scalar(cc2a[:], rsx[:], rswu_c[:], None, ALU.mult))
    guev5 = {0: 0, 1: 0}
    qdone = 0
    hl_dma = 0
    for jb in range(16):
        em.wait("y", cV, qdone)
        sdg = slab_load(em, "y", g5l, gsl32, Fg_h, jb * 512, 512)
        em.wait("v", g5l, sdg)
        em.wait("v", sY, flnd)
        op(em, "v", cV, lambda e: e.tensor_scalar(gsl32[:], gsl32[:], swg_c[:], MAGIC, ALU.mult, ALU.add))
        op(em, "v", cV, lambda e: e.tensor_scalar(gsl32[:], gsl32[:], MAGIC, None, ALU.subtract))
        gqd = op(em, "v", cV, lambda e: e.tensor_scalar(gqb[:], gsl32[:], -1.0, 1.0, ALU.max, ALU.min))
        em.wait("y", cV, gqd)
        sdu = slab_load(em, "y", g5l, gsl32, Fu_h, jb * 512, 512)
        em.wait("v", g5l, sdu)
        op(em, "v", cV, lambda e: e.tensor_scalar(gsl32[:], gsl32[:], swu_c[:], MAGIC, ALU.mult, ALU.add))
        op(em, "v", cV, lambda e: e.tensor_scalar(gsl32[:], gsl32[:], MAGIC, None, ALU.subtract))
        qdone = op(em, "v", cV, lambda e: e.tensor_scalar(uqb[:], gsl32[:], -1.0, 1.0, ALU.max, ALU.min))
        em.wait("t", cV, qdone)
        for tb in range(16):
            bk = tb % 2
            em.wait("t", cV, guev5[bk])
            for a in range(16):
                op(em, "t", cT, lambda e, a=a, tb=tb, bk=bk: e.matmul(
                    ps5_g[bk][:], qxT[:, a * 2048 + tb * 128: a * 2048 + tb * 128 + 128],
                    gqb[:, a * 512:(a + 1) * 512], start=(a == 0), stop=(a == 15)))
            for a in range(16):
                t9 = op(em, "t", cT, lambda e, a=a, tb=tb, bk=bk: e.matmul(
                    ps5_u[bk][:], qxT[:, a * 2048 + tb * 128: a * 2048 + tb * 128 + 128],
                    uqb[:, a * 512:(a + 1) * 512], start=(a == 0), stop=(a == 15)))
            em.wait("s", cT, t9)
            em.wait("s", cV, guev5[bk])
            s9 = op(em, "s", cS, lambda e, tb=tb, bk=bk: e.activation(
                sdump5[:], ps5_g[bk][:], AF.Silu, scale=cc1a[:, tb:tb + 1]))
            em.wait("v", cS, s9)
            v9 = op(em, "v", cV, lambda e, tb=tb, bk=bk: e.scalar_tensor_tensor(
                hf5[:], ps5_u[bk][:], cc2a[:, tb:tb + 1], sdump5[:], ALU.mult, ALU.mult))
            em.wait("v", cV, v9)
            em.wait("s", cV, v9)
            s10 = op(em, "s", cS, lambda e, jb=jb, tb=tb: e.activation(
                sdump5[:], hf5[:], AF.Square, accum_out=ssbig[:, tb * 16 + jb: tb * 16 + jb + 1]))
            v10 = op(em, "v", cV, lambda e, jb=jb: e.tensor_mul(hl5[:], hf5[:], flnrep[:, jb * 512:(jb + 1) * 512]))
            em.wait("v", cV, v10)
            vr = op(em, "v", cV, lambda e, jb=jb, tb=tb: e.reduce_max(
                mxbig[:, tb * 16 + jb: tb * 16 + jb + 1], hl5[:], AX, apply_absolute_value=True))
            em.wait("v", cV, vr)
            em.wait("g", cV, vr)
            em.wait("g", cS, s10)
            hl_dma = dma(em, "g", hls, hl_s[tb * 128:(tb + 1) * 128, jb * 512:(jb + 1) * 512], hl5[:])
            guev5[bk] = vr
            em.wait("v", hls, hl_dma)
    em.wait("v", cS, cS.n)
    em.wait("v", cV, cV.n)
    op(em, "v", cV, lambda e: e.tensor_reduce(
        ssfin[:], ssbig[:, :].rearrange("p (t j) -> p t j", j=16), AX, ALU.add))
    vsf = op(em, "v", cV, lambda e: e.tensor_reduce(
        mxfin[:], mxbig[:, :].rearrange("p (t j) -> p t j", j=16), AX, ALU.max))
    em.wait("v", cV, vsf)
    em.wait("s", cV, vsf)
    op(em, "s", cS, lambda e: e.activation(t5a[:], ssfin[:], AF.Ln, bias=epsc[:, 1:2], scale=epsc[:, 4:5]))
    srst = op(em, "s", cS, lambda e: e.activation(t5a[:], t5a[:], AF.Exp, scale=-0.5))
    em.wait("v", cS, srst)
    vop(em, lambda e: e.tensor_mul(mxfin[:], mxfin[:], t5a[:]))
    vop(em, lambda e: e.tensor_scalar(rs2a[:], mxfin[:], 1e-5, 1.0 / 127.0, ALU.max, ALU.mult))
    em.wait("v", cV, cV.n)
    _vr2 = op(em, "v", cV, lambda e: e.reciprocal(cs2a[:], rs2a[:]))
    em.wait("v", cV, _vr2)
    vop(em, lambda e: e.tensor_mul(cs2a[:], cs2a[:], t5a[:]))
    vcd = vop(em, lambda e: e.tensor_scalar(cdna[:], rs2a[:], rswd_c[:], None, ALU.mult))
    em.wait("t", cV, vcd)
    tbar = barrier(em)
    em.emit(nc)
    es5.close()
    if stop_after == "p5":
        emit_dumps(tbar)
        return nc

    es6 = ExitStack()
    E6 = es6.enter_context
    hlr = E6(nc.sbuf_tensor("hlr", [128, IF], F32))
    q2tok = E6(nc.sbuf_tensor("q2tok", [128, IF], BF16))
    hqT = E6(nc.sbuf_tensor("hqT", [128, 64 * 1024], BF16))
    wq5 = [E6(nc.sbuf_tensor(f"wq5_{i}", [128, 4 * 512], BF16)) for i in range(2)]
    aox = [E6(nc.sbuf_tensor(f"aox{i}", [128, 512], F32)) for i in range(2)]
    xf = [E6(nc.sbuf_tensor(f"xf{i}", [128, 512], F32)) for i in range(2)]
    out5b = [E6(nc.sbuf_tensor(f"out5b{i}", [128, 512], F32)) for i in range(2)]
    t5fb = [E6(nc.sbuf_tensor(f"t5fb{i}", [128, 512], F32)) for i in range(2)]

    em = Em()
    phase_start(em, tbar)
    o_vo = {0: 0, 1: 0}
    o_go = {0: 0, 1: 0}
    vq6_last = 0
    for blk in range(2):
        estr = ExitStack()
        ps6_tr = [estr.enter_context(nc.psum_tensor(f"ps6_tr{blk}{i}", [128, 512], F32)) for i in range(2)]
        tr6_ev = {0: 0, 1: 0}
        for i8 in range(8):
            i = blk * 8 + i8
            em.wait("y", cV, vq6_last)
            hd = dma(em, "y", h6l, hlr[:], hl_s[i * 128:(i + 1) * 128, :])
            em.wait("v", h6l, hd)
            op(em, "v", cV, lambda e, i=i: e.tensor_scalar(
                hlr[:], hlr[:], cs2a[:, i:i + 1], MAGIC, ALU.mult, ALU.add))
            vq6_last = vq6 = op(em, "v", cV, lambda e: e.tensor_scalar(q2tok[:], hlr[:], MAGIC, None, ALU.subtract))
            em.wait("t", cV, vq6)
            for c4 in range(16):
                pp = c4 % 2
                em.wait("t", cV, tr6_ev[pp])
                for cc in range(4):
                    c = c4 * 4 + cc
                    ta = op(em, "t", cT, lambda e, c=c, cc=cc, pp=pp: e.matmul(
                        ps6_tr[pp][:, cc * 128:(cc + 1) * 128], q2tok[:, c * 128:(c + 1) * 128],
                        identb[:], start=True, stop=True))
                em.wait("v", cT, ta)
                tr6_ev[pp] = op(em, "v", cV, lambda e, c4=c4, i8=i8, pp=pp: e.tensor_copy(
                    hqT[:, :].rearrange("p (c t) -> p c t", t=1024)[:, c4 * 4:(c4 + 1) * 4, i8 * 128:(i8 + 1) * 128],
                    ps6_tr[pp][:, :].rearrange("p (c t) -> p c t", t=128)))
        em.wait("t", cV, cV.n)
        estr.close()
        espd = ExitStack()
        ps6_dn = [espd.enter_context(nc.psum_tensor(f"ps6_dn{blk}{i}", [128, 512], F32)) for i in range(8)]
        last6p = {0: 0, 1: 0}
        for nb in range(4):
            for ag in range(16):
                par = ag % 2
                em.wait("y", cT, last6p[par])
                srcv = wdq_s.rearrange("(a p) n -> p a n", p=128)
                sdd = dma(em, "y", w6l[par], wq5[par][:, :].rearrange("p (a n) -> p a n", n=512),
                          srcv[:, ag * 4:(ag + 1) * 4, nb * 512:(nb + 1) * 512])
                em.wait("t", w6l[par], sdd)
                for tb in range(8):
                    for aa in range(4):
                        a = ag * 4 + aa
                        tz = op(em, "t", cT, lambda e, a=a, aa=aa, tb=tb, par=par: e.matmul(
                            ps6_dn[tb][:], hqT[:, a * 1024 + tb * 128: a * 1024 + tb * 128 + 128],
                            wq5[par][:, aa * 512:(aa + 1) * 512],
                            start=(a == 0), stop=(a == 63)))
                last6p[par] = tz
            for tb in range(8):
                i = blk * 8 + tb
                pp = tb % 2
                em.wait("s", cV, o_vo[pp])
                ad = dma(em, "s", a6l, aox[pp][:], ao_s[i * 128:(i + 1) * 128, nb * 512:(nb + 1) * 512])
                xd = dma(em, "s", a6l, xf[pp][:], x_h[i * 128:(i + 1) * 128, nb * 512:(nb + 1) * 512])
                em.wait("s", cT, cT.n)
                sa = op(em, "s", cS, lambda e, tb=tb, i=i, pp=pp: e.activation(
                    t5fb[pp][:], ps6_dn[tb][:], AF.Copy, scale=cdna[:, i:i + 1]))
                em.wait("v", cS, sa)
                em.wait("v", a6l, xd)
                em.wait("v", o6s[pp], o_go[pp])
                va1 = op(em, "v", cV, lambda e, pp=pp: e.tensor_add(out5b[pp][:], t5fb[pp][:], aox[pp][:]))
                em.wait("v", cV, va1)
                o_vo[pp] = op(em, "v", cV, lambda e, pp=pp: e.tensor_add(out5b[pp][:], out5b[pp][:], xf[pp][:]))
                em.wait("g", cV, o_vo[pp])
                o_go[pp] = dma(em, "g", o6s[pp], out_h[i * 128:(i + 1) * 128, nb * 512:(nb + 1) * 512], out5b[pp][:])
                em.wait("t", cV, o_vo[pp])
        espd.close()
    em.wait("g", o6s[0])
    em.wait("g", o6s[1])
    if dev:
        for dap, sap in dump_specs:
            dma(em, "g", sG, dap, sap)
    em.wait("g", sG, sG.n)
    barrier(em)
    em.emit(nc)
    es6.close()
    gstk.close()
    return nc


def _host_layout(inputs):
    import ml_dtypes

    def g(k):
        return np.ascontiguousarray(np.asarray(inputs[k], dtype=np.float32))

    Wqkvz = g("L0_attn_W_qkvz").reshape(D, HK, 2 * DK + 2 * 2 * DV)
    out = {
        "Wq": np.ascontiguousarray(Wqkvz[:, :, :DK].reshape(D, HK * DK)),
        "Wk": np.ascontiguousarray(Wqkvz[:, :, DK:2 * DK].reshape(D, HK * DK)),
        "Wv": np.ascontiguousarray(Wqkvz[:, :, 2 * DK:2 * DK + 2 * DV].reshape(D, HV * DV)),
        "Wz": np.ascontiguousarray(Wqkvz[:, :, 2 * DK + 2 * DV:].reshape(D, HV * DV)),
    }
    Wba4 = g("L0_attn_W_ba").reshape(D, HK, 4)
    out["Wba"] = np.ascontiguousarray(np.concatenate(
        [Wba4[:, :, :2].reshape(D, HV), Wba4[:, :, 2:].reshape(D, HV)], axis=1))
    out["convw"] = g("L0_attn_conv_w")
    out["Wout"] = g("L0_attn_W_out")
    out["Wg"] = g("L0_attn_W_gate")
    out["Wu"] = g("L0_attn_W_up")
    out["Wd"] = g("L0_attn_W_down")
    out["Fg"] = np.ascontiguousarray(g("L0_ffn_w_gate").T)
    out["Fu"] = np.ascontiguousarray(g("L0_ffn_w_up").T)
    out["Fd"] = np.ascontiguousarray(g("L0_ffn_w_down").T)
    out["ln1c"] = np.ascontiguousarray(g("L0_attn_ln1_w").reshape(16, 128).T)
    out["ln2c"] = np.ascontiguousarray(g("L0_attn_ln2_w").reshape(16, 128).T)
    out["dtrep"] = np.ascontiguousarray(np.tile(g("L0_attn_dt_bias")[None, :], (128, 1)))
    out["alogrep"] = np.ascontiguousarray(np.tile(g("L0_attn_A_log")[None, :], (128, 1)))
    out["gnwz"] = np.ascontiguousarray(np.tile(g("L0_attn_gated_norm_w")[None, :], (128, 4)))
    out["flnrep"] = np.ascontiguousarray(np.tile(g("L0_ffn_ln_weight")[None, :], (128, 1)))
    out["identf"] = np.eye(128, dtype=np.float32)
    out["identb"] = np.eye(128, dtype=ml_dtypes.bfloat16)
    tt = np.arange(128)
    m0 = np.where(tt[None, :] < tt[:, None], 0.0, NEGC).astype(np.float32)
    m1 = np.where(tt[None, :] > tt[:, None], 0.0, NEGC).astype(np.float32)
    m2 = np.where(tt[None, :] >= tt[:, None], 0.0, NEGC).astype(np.float32)
    out["m0T"] = np.ascontiguousarray(m0.T)
    out["m1T"] = np.ascontiguousarray(m1.T)
    out["m2T"] = np.ascontiguousarray(m2.T)
    out["negup"] = np.ascontiguousarray(
        np.where(tt[None, :] >= tt[:, None], -1.0, 0.0).astype(np.float32))
    ep = np.zeros((128, 8), np.float32)
    ep[:, 0] = 1e-5; ep[:, 1] = 1e-6; ep[:, 2] = 1.0 / 128.0
    ep[:, 3] = 1.0 / D; ep[:, 4] = 1.0 / IF; ep[:, 5] = LNQ
    out["epsc"] = ep
    return out


def kernel(**inputs):
    shared = _host_layout(inputs)
    X = np.ascontiguousarray(np.asarray(inputs["L0_attn_hidden_states"], dtype=np.float32))
    nc = build_program(dev=DEV)
    in_maps = []
    ncore = 8
    for b in range(ncore):
        m = dict(shared)
        m["x"] = np.ascontiguousarray(X[b])
        in_maps.append(m)
    res = run_bass_kernel_spmd(nc, in_maps, core_ids=list(range(ncore)))
    if DEV:
        kernel.dev_results = res.results
    return np.stack([r["out"] for r in res.results], axis=0)
